# version 18
# baseline (speedup 1.0000x reference)
import numpy as np
import ml_dtypes
from contextlib import ExitStack
import concourse.bass as bass
import concourse.mybir as mybir
from concourse.bass_utils import run_bass_kernel_spmd

F32 = mybir.dt.float32
BF16 = mybir.dt.bfloat16
AF = mybir.ActivationFunctionType
ALU = mybir.AluOpType
AX = mybir.AxisListType
BFNP = ml_dtypes.bfloat16

D = 1024
NCH = 8
HA, DHA, FTA = 16, 64, 256
HB, DHB, WINB = 8, 64, 512
HID = 2730
HIDP = 2816
W1N = 2 * HIDP
NK2 = HIDP // 128
QK_SCALE = 8.0
TT = 512



def host_consts(T):
    c = {}
    c["ident"] = np.eye(128, dtype=np.float32)
    c["ones_col"] = np.ones((128, 1), np.float32)
    c["ones_row"] = np.ones((1, 128), np.float32)
    sel = np.zeros((128, 4, 8), np.float32)
    for cc in range(4):
        sel[0:64, cc, 2 * cc] = 1.0
        sel[64:128, cc, 2 * cc + 1] = 1.0
    c["selq"] = sel
    sb = np.zeros((8, 4, 128), np.float32)
    for cc in range(4):
        sb[2 * cc, cc, 0:64] = 1.0
        sb[2 * cc + 1, cc, 64:128] = 1.0
    c["selbc"] = sb
    P = np.zeros((128, 128), np.float32)
    for m in range(128):
        blk = (m // 64) * 64
        ml = m % 64
        P[blk + (ml + 32) % 64, m] = -1.0 if ml < 32 else 1.0
    c["perm"] = P
    half = np.arange(0, DHB, 2, dtype=np.float64)
    inv_freq = 1.0 / 10000.0 ** (half / DHB)
    t = np.arange(T, dtype=np.float64)
    freqs = t[:, None] * inv_freq[None, :]
    freqs = np.concatenate([freqs, freqs], -1)
    base = (half + 0.4 * DHB) / (1.4 * DHB)
    power = (t - T // 2) / float(WINB // 2)
    scale = base[None, :] ** power[:, None]
    scale = np.concatenate([scale, scale], -1)
    cos, sin = np.cos(freqs), np.sin(freqs)

    def two(v):
        return np.ascontiguousarray(np.tile(v.T.astype(np.float32), (2, 1)))

    c["cq"] = two(cos * scale * QK_SCALE)
    c["sq"] = two(sin * scale * QK_SCALE)
    c["ck"] = two(cos / scale)
    c["sk"] = two(sin / scale)
    jj = np.arange(640)[None, :]
    il = np.arange(128)[:, None]
    gen = ((jj >= il) & (jj <= il + WINB)).astype(np.float32)
    masks = np.zeros((5, 128, 640), np.float32)
    masks[4] = gen
    for qc in range(4):
        masks[qc] = gen * (jj >= WINB - qc * 128)
    c["mask"] = np.ascontiguousarray(masks.transpose(1, 0, 2))
    return c


def col128(b):
    return np.ascontiguousarray(np.asarray(b, np.float32).reshape(-1, 128).T)


def host_weights(inp):
    w = {}
    w["Wq"] = inp["L0_attn_Wq"]
    w["Wk"] = inp["L0_attn_Wk"]
    w["Wv"] = inp["L0_attn_Wv"]
    w["Wo"] = inp["L0_attn_Wo"]
    w["bq"] = col128(inp["L0_attn_bq"])
    w["bk"] = col128(inp["L0_attn_bk"])
    w["bv"] = np.ascontiguousarray(np.asarray(inp["L0_attn_bv"], np.float32).reshape(1, D))
    w["bo"] = col128(inp["L0_attn_bo"])
    w["rf"] = inp["L0_attn_random_features"]
    for l in range(2):
        w[f"wqkv{l}"] = np.ascontiguousarray(inp["L0_ffn_wqkv"][l])
        w[f"wo{l}"] = np.ascontiguousarray(inp["L0_ffn_wo"][l])
        qs = np.asarray(inp["L0_ffn_q_scale"][l], np.float32)
        ks = np.asarray(inp["L0_ffn_k_scale"][l], np.float32)
        w[f"qs{l}"] = np.ascontiguousarray(np.tile(qs, 2)[:, None])
        w[f"qsr{l}"] = np.ascontiguousarray(np.tile(np.roll(qs, -32), 2)[:, None])
        w[f"ks{l}"] = np.ascontiguousarray(np.tile(ks, 2)[:, None])
        w[f"ksr{l}"] = np.ascontiguousarray(np.tile(np.roll(ks, -32), 2)[:, None])
        w[f"lg{l}"] = np.ascontiguousarray(np.asarray(inp["L0_ffn_ln_g"][l], np.float32).reshape(1, D))
        w[f"lb{l}"] = col128(inp["L0_ffn_ln_b"][l])
        w1 = np.asarray(inp["L0_ffn_w1"][l], np.float32)
        w1p = np.zeros((D, W1N), np.float32)
        w1p[:, :HID] = w1[:, :HID]
        w1p[:, HIDP:HIDP + HID] = w1[:, HID:]
        w[f"w1{l}"] = w1p
        b1 = np.asarray(inp["L0_ffn_b1"][l], np.float32)
        b1p = np.zeros((W1N,), np.float32)
        b1p[:HID] = b1[:HID]
        b1p[HIDP:HIDP + HID] = b1[HID:]
        w[f"b1{l}"] = col128(b1p)
        w2 = np.asarray(inp["L0_ffn_w2"][l], np.float32)
        w2p = np.zeros((HIDP, D), np.float32)
        w2p[:HID] = w2
        w[f"w2{l}"] = w2p
        w[f"b2{l}"] = col128(inp["L0_ffn_b2"][l])
    return w



class Ctr:
    __slots__ = ("h", "n")

    def __init__(self, h):
        self.h = h
        self.n = 0

    def inc(self, instr, k=1):
        instr.then_inc(self.h, k)
        self.n += k
        return self.n

    def dma(self, instr):
        return self.inc(instr, 16)

    def w(self, eng, n=None):
        v = self.n if n is None else n
        assert v <= self.n, "wait on not-yet-emitted count"
        if v > 0:
            eng.wait_ge(self.h, v)


class Res:
    __slots__ = ("ve", "sc", "pe", "gp")

    def __init__(self):
        self.ve = self.sc = self.pe = self.gp = 0

    def used(self, ve=None, sc=None, pe=None, gp=None):
        if ve is not None:
            self.ve = max(self.ve, ve)
        if sc is not None:
            self.sc = max(self.sc, sc)
        if pe is not None:
            self.pe = max(self.pe, pe)
        if gp is not None:
            self.gp = max(self.gp, gp)

    def wait(self, b, eng):
        if self.ve:
            b.VE.w(eng, self.ve)
        if self.sc:
            b.SC.w(eng, self.sc)
        if self.pe:
            b.PE.w(eng, self.pe)
        if self.gp:
            b.GP.w(eng, self.gp)


class B:
    pass


class PEProxy:

    def __init__(self, eng):
        self._e = eng
        self.last = None

    def matmul(self, *a, **k):
        self.last = self._e.matmul(*a, **k)
        return self.last

    def transpose(self, *a, **k):
        self.last = self._e.transpose(*a, **k)
        return self.last

    def wait_ge(self, *a, **k):
        return self._e.wait_ge(*a, **k)

    def nop(self, *a, **k):
        return self._e.nop(*a, **k)


def build_nc(T=4096, dbg=(), parts=("A0", "A1", "A4", "B0a", "B0f", "B1a", "B1f"),
             a1_sub="abcde"):
    b = B()
    b.T = T
    b.NT = T // TT
    b.NW = T // WINB
    nc = bass.Bass("TRN2", target_bir_lowering=False,
                   detect_race_conditions=False)
    b.nc = nc
    st = ExitStack()
    b.st = st

    def dram(name, shape, dt, kind="ExternalInput"):
        return nc.dram_tensor(name, list(shape), dt, kind=kind).ap()

    def scratch(name, shape, dt):
        kind = "ExternalOutput" if name in dbg else "Internal"
        return dram(name, shape, dt, kind=kind)

    b.dram = dram
    b.x_h = dram("x", (D, T), F32)
    b.Wq_h = dram("Wq", (D, D), F32)
    b.Wk_h = dram("Wk", (D, D), F32)
    b.Wv_h = dram("Wv", (D, D), F32)
    b.Wo_h = dram("Wo", (D, D), F32)
    b.bq_h = dram("bq", (128, 8), F32)
    b.bk_h = dram("bk", (128, 8), F32)
    b.bv_h = dram("bv", (1, D), F32)
    b.bo_h = dram("bo", (128, 8), F32)
    b.rf_h = dram("rf", (DHA, FTA), F32)
    b.wqkv_h = [dram(f"wqkv{l}", (D, 1536), F32) for l in range(2)]
    b.wol_h = [dram(f"wo{l}", (512, D), F32) for l in range(2)]
    b.qs_h = [dram(f"qs{l}", (128, 1), F32) for l in range(2)]
    b.qsr_h = [dram(f"qsr{l}", (128, 1), F32) for l in range(2)]
    b.ks_h = [dram(f"ks{l}", (128, 1), F32) for l in range(2)]
    b.ksr_h = [dram(f"ksr{l}", (128, 1), F32) for l in range(2)]
    b.lg_h = [dram(f"lg{l}", (1, D), F32) for l in range(2)]
    b.lb_h = [dram(f"lb{l}", (128, 8), F32) for l in range(2)]
    b.w1_h = [dram(f"w1{l}", (D, W1N), F32) for l in range(2)]
    b.b1_h = [dram(f"b1{l}", (128, 44), F32) for l in range(2)]
    b.w2_h = [dram(f"w2{l}", (HIDP, D), F32) for l in range(2)]
    b.b2_h = [dram(f"b2{l}", (128, 8), F32) for l in range(2)]
    b.ident_h = dram("ident", (128, 128), F32)
    b.onesc_h = dram("ones_col", (128, 1), F32)
    b.onesr_h = dram("ones_row", (1, 128), F32)
    b.selq_h = dram("selq", (128, 4, 8), F32)
    b.selbc_h = dram("selbc", (8, 4, 128), F32)
    b.perm_h = dram("perm", (128, 128), F32)
    b.cq_h = dram("cq", (128, T), F32)
    b.sq_h = dram("sq", (128, T), F32)
    b.ck_h = dram("ck", (128, T), F32)
    b.sk_h = dram("sk", (128, T), F32)
    b.mask_h = dram("mask", (128, 5, 640), F32)
    b.xln_d = scratch("xln", (D, T), BF16)
    b.xA_d = scratch("xA", (D, T), F32)
    b.xffn_d = scratch("xffn", (D, T), F32)
    b.kvt_dbg = scratch("kvt", (128, 32 * DHA), F32) if "kvt" in dbg else None
    b.out_h = dram("out", (D, T), F32, kind="ExternalOutput")
    b.x3 = b.x_h.rearrange("(c p) t -> p c t", p=128)
    b.xln3 = b.xln_d.rearrange("(c p) t -> p c t", p=128)
    b.xA3 = b.xA_d.rearrange("(c p) t -> p c t", p=128)
    b.xffn3 = b.xffn_d.rearrange("(c p) t -> p c t", p=128)
    b.out3 = b.out_h.rearrange("(c p) t -> p c t", p=128)

    b.LD = Ctr(st.enter_context(nc.semaphore("ld")))
    b.LS = Ctr(st.enter_context(nc.semaphore("ls")))
    b.PE = Ctr(st.enter_context(nc.semaphore("pe")))
    b.VE = Ctr(st.enter_context(nc.semaphore("ve")))
    b.SC = Ctr(st.enter_context(nc.semaphore("sc")))
    b.GP = Ctr(st.enter_context(nc.semaphore("gp")))
    b.CTRS = [b.LD, b.LS, b.PE, b.VE, b.SC, b.GP]

    def mksem(name):
        c = Ctr(st.enter_context(nc.semaphore(name)))
        b.CTRS.append(c)
        return c

    b.mksem = mksem

    b.g = nc.gpsimd
    b.sy = nc.sync
    b.pe = PEProxy(nc.tensor)
    b.ve = nc.vector
    b.sc = nc.scalar

    def phase_end():
        for c in b.CTRS:
            if c.n:
                nc.sync.wait_ge(c.h, c.n)
        nc.all_engine_barrier()

    b.phase_end = phase_end

    b.ident_b = st.enter_context(nc.sbuf_tensor("ident_b", [128, 128], BF16))
    b.onesc_b = st.enter_context(nc.sbuf_tensor("onesc_b", [128, 1], BF16))
    b.onesr_f = st.enter_context(nc.sbuf_tensor("onesr_f", [1, 128], F32))
    b.selq_b = st.enter_context(nc.sbuf_tensor("selq_b", [128, 4, 8], BF16))
    b.selbc_b = st.enter_context(nc.sbuf_tensor("selbc_b", [8, 4, 128], BF16))
    b.perm_b = st.enter_context(nc.sbuf_tensor("perm_b", [128, 128], BF16))
    b.r1_d = scratch("r1row", (1, T), F32)
    b.m1_d = scratch("m1row", (1, T), F32)
    b.r2_d = scratch("r2row", (1, T), F32)
    b.m2_d = scratch("m2row", (1, T), F32)
    b.rg_d = [scratch(f"rgrow{l}", (1, T), F32) for l in range(2)]
    b.nm_d = [scratch(f"nmrow{l}", (1, T), F32) for l in range(2)]
    b.rows4_d = [scratch(f"rows4_{l}", (1, 4, T), F32) for l in range(2)]


    LD, LS = b.LD, b.LS
    g, sy = b.g, b.sy
    LD.dma(g.dma_start(out=b.ident_b[:, :], in_=b.ident_h[:, :]))
    LD.dma(g.dma_start(out=b.onesc_b[:, :], in_=b.onesc_h[:, :]))
    LD.dma(g.dma_start(out=b.selq_b[:, :, :], in_=b.selq_h[:, :, :]))
    LD.dma(g.dma_start(out=b.perm_b[:, :], in_=b.perm_h[:, :]))
    LS.dma(sy.dma_start(out=b.onesr_f[:, :], in_=b.onesr_h[:, :]))
    LD.dma(g.dma_start(out=b.selbc_b[:, :, :], in_=b.selbc_h[:, :, :]))
    phase_end()

    b.a1_sub = a1_sub
    with ExitStack() as a_scope:
        b.kvt_b = a_scope.enter_context(
            nc.sbuf_tensor("kvt_b", [128, 32, DHA], BF16))
        b.wq_sb = a_scope.enter_context(nc.sbuf_tensor("a4_wq", [128, NCH, D], BF16))
        b.wo_sb = a_scope.enter_context(nc.sbuf_tensor("a4_wo", [128, NCH, D], BF16))
        if "A0" in parts:
            phase_a0(b)
        if "A1" in parts:
            phase_a1(b)
        if "A4" in parts:
            phase_a4(b)
    if "A4" in parts:
        phase_ln2s(b)
    for l in range(2):
        if f"B{l}a" in parts:
            phase_battn(b, l)
        if f"B{l}f" in parts:
            phase_bffn(b, l)
    st.close()
    return nc


def ln_soup(b, stp, s_sb, q2_sb, r_dram, mr_dram, tag, neg=False):
    nc, ve, sc, sy, VE, SC, LS = b.nc, b.ve, b.sc, b.sy, b.VE, b.SC, b.LS
    T = b.T
    t_ = stp.enter_context(nc.sbuf_tensor(f"t_{tag}", [1, T], F32))
    ve.tensor_scalar_mul(s_sb[:, :], s_sb[:, :], 1.0 / D)
    ve.tensor_mul(t_[:, :], s_sb[:, :], s_sb[:, :])
    ve.scalar_tensor_tensor(t_[:, :], q2_sb[:, :], 1.0 / D, t_[:, :],
                            ALU.mult, ALU.subtract)
    vv = VE.inc(ve.tensor_scalar_add(t_[:, :], t_[:, :], 1e-5))
    b.VE.w(sc, vv)
    sc.activation(t_[:, :], t_[:, :], AF.Ln, bias=0.0, scale=1.0)
    s2 = SC.inc(sc.activation(q2_sb[:, :], t_[:, :], AF.Exp, bias=0.0, scale=-0.5))
    b.SC.w(ve, s2)
    if neg:
        vr = VE.inc(ve.scalar_tensor_tensor(s_sb[:, :], s_sb[:, :], -1.0,
                                            q2_sb[:, :], ALU.mult, ALU.mult))
    else:
        vr = VE.inc(ve.tensor_mul(s_sb[:, :], s_sb[:, :], q2_sb[:, :]))
    VE.w(sy, vr)
    LS.dma(sy.dma_start(out=r_dram[:, :], in_=q2_sb[:, :]))
    LS.dma(sy.dma_start(out=mr_dram[:, :], in_=s_sb[:, :]))


def elu1(b, psrc, bcol, m, e_scr, rl_scr, dst, pe_done):
    sc, ve, SC, VE, PE = b.sc, b.ve, b.SC, b.VE, b.PE
    VE.w(sc, VE.n)
    PE.w(sc, pe_done)
    s1 = SC.inc(sc.activation(e_scr[:, :], psrc, AF.Exp,
                              bias=bcol[:, m:m + 1], scale=1.0))
    PE.w(ve, pe_done)
    SC.w(ve, s1)
    ve.tensor_scalar(rl_scr[:, :], psrc, bcol[:, m:m + 1], 0.0, ALU.add, ALU.max)
    v = VE.inc(ve.scalar_tensor_tensor(dst, e_scr[:, :], 1.0, rl_scr[:, :],
                                       ALU.min, ALU.add))
    return s1, v


def phase_a0(b):
    nc, g, sy, pe, ve = b.nc, b.g, b.sy, b.pe, b.ve
    LD, LS, PE, VE = b.LD, b.LS, b.PE, b.VE
    NT, T = b.NT, b.T
    with ExitStack() as stp:
        xb = stp.enter_context(nc.sbuf_tensor("a0_xb", [128, 2, NCH, TT], BF16))
        xsq = stp.enter_context(nc.sbuf_tensor("a0_xsq", [128, NCH, TT], BF16))
        s_sb = stp.enter_context(nc.sbuf_tensor("a0_s", [1, T], F32))
        q2_sb = stp.enter_context(nc.sbuf_tensor("a0_q2", [1, T], F32))
        pr = stp.enter_context(nc.psum_tensor("a0_pr", [1, 1024], F32))
        XSL = [b.mksem("a0x0"), b.mksem("a0x1")]
        pe_rec = [0] * NT
        ve_rec = [0] * NT
        for tt in range(NT + 1):
            if tt < NT:
                if tt >= 2:
                    b.PE.w(g, pe_rec[tt - 2])
                XSL[tt % 2].dma(g.dma_start(
                    out=xb[:, tt % 2, :, :], in_=b.x3[:, :, tt * TT:(tt + 1) * TT]))
            if tt == 0:
                continue
            i = tt - 1
            sl = i % 2
            XSL[sl].w(ve)
            if i >= 1:
                PE.w(ve, pe_rec[i - 1])
            vsq = VE.inc(ve.tensor_mul(xsq[:, :, :], xb[:, sl, :, :], xb[:, sl, :, :]))
            XSL[sl].w(pe)
            VE.w(pe, vsq)
            if i >= 1:
                VE.w(pe, ve_rec[i - 1])
            for c in range(NCH):
                pe.matmul(pr[:, 0:TT], b.onesc_b[:, :], xb[:, sl, c, :],
                          start=(c == 0), stop=(c == NCH - 1))
            for c in range(NCH):
                pe.matmul(pr[:, TT:2 * TT], b.onesc_b[:, :], xsq[:, c, :],
                          start=(c == 0), stop=(c == NCH - 1))
            pe_rec[i] = PE.inc(pe.last)
            PE.w(ve, pe_rec[i])
            ve.tensor_copy(s_sb[:, i * TT:(i + 1) * TT], pr[:, 0:TT])
            ve_rec[i] = VE.inc(ve.tensor_copy(q2_sb[:, i * TT:(i + 1) * TT],
                                              pr[:, TT:2 * TT]))
        ln_soup(b, stp, s_sb, q2_sb, b.r1_d, b.m1_d, "ln1")
        b.phase_end()


def phase_a1(b):
    nc, g, sy, pe, ve, sc = b.nc, b.g, b.sy, b.pe, b.ve, b.sc
    LD, LS, PE, VE, SC, GP = b.LD, b.LS, b.PE, b.VE, b.SC, b.GP
    NT, T = b.NT, b.T
    with ExitStack() as stp:
        wk_sb = stp.enter_context(nc.sbuf_tensor("a1_wk", [128, NCH, D], BF16))
        wv_sb = stp.enter_context(nc.sbuf_tensor("a1_wv", [128, NCH, D], BF16))
        rf_sb = stp.enter_context(nc.sbuf_tensor("a1_rf", [128, FTA], BF16))
        bkc = stp.enter_context(nc.sbuf_tensor("a1_bkc", [128, 8], F32))
        bvbc = stp.enter_context(nc.sbuf_tensor("a1_bvbc", [128, D], F32))
        xb = stp.enter_context(nc.sbuf_tensor("a1_xb", [128, 2, NCH, TT], BF16))
        xln = stp.enter_context(nc.sbuf_tensor("a1_xln", [128, 2, NCH, TT], BF16))
        kel = stp.enter_context(nc.sbuf_tensor("a1_kel", [128, NCH, TT], BF16))
        e_scr = stp.enter_context(nc.sbuf_tensor("a1_e", [128, TT], F32))
        rl_scr = stp.enter_context(nc.sbuf_tensor("a1_rl", [128, TT], F32))
        v_tile = stp.enter_context(nc.sbuf_tensor("a1_v", [128, 4, D], BF16))
        kp_sb = stp.enter_context(nc.sbuf_tensor("a1_kp", [128, 2, 2, 256], BF16))
        kvacc = stp.enter_context(nc.sbuf_tensor("a1_kva", [128, 4, TT], F32))
        rowb = stp.enter_context(nc.sbuf_tensor("a1_rowb", [1, 2, 2, TT], F32))
        pA = stp.enter_context(nc.psum_tensor("a1_pA", [128, 1024], F32))
        pkp = stp.enter_context(nc.psum_tensor("a1_pkp", [128, 1024], F32))
        pkv = stp.enter_context(nc.psum_tensor("a1_pkv", [128, 2048], F32))

        LD.dma(g.dma_start(out=wk_sb[:, :, :],
                           in_=b.Wk_h.rearrange("(c p) m -> p c m", p=128)))
        LD.dma(g.dma_start(out=wv_sb[:, :, :],
                           in_=b.Wv_h.rearrange("(c p) m -> p c m", p=128)))
        LD.dma(g.dma_start(out=rf_sb[0:64, :], in_=b.rf_h[:, :]))
        LD.dma(g.dma_start(out=rf_sb[64:128, :], in_=b.rf_h[:, :]))
        LS.dma(sy.dma_start(out=bkc[:, :], in_=b.bk_h[:, :]))
        LS.dma(sy.dma_start(out=bvbc[:, :],
                            in_=b.bv_h.partition_broadcast(128)[:, 0, :]))
        GP.inc(g.memset(kvacc[:, :, :], 0.0))
        LD.dma(g.dma_start(out=b.wq_sb[:, :, :],
                           in_=b.Wq_h.rearrange("(c p) m -> p c m", p=128)))
        LD.dma(g.dma_start(out=b.wo_sb[:, :, :],
                           in_=b.Wo_h.rearrange("(c p) m -> p c m", p=128)))
        wk_ld = LD.n
        c_ld = LS.n
        gz = GP.n

        XSL = [b.mksem("a1x0"), b.mksem("a1x1")]
        RSL = [b.mksem("a1r0"), b.mksem("a1r1")]
        SSL = [b.mksem("a1s0"), b.mksem("a1s1")]
        pa = [Res(), Res()]
        pkp_r = Res()
        kpr = [Res(), Res()]
        pkv_r = Res()
        kelr = Res()
        vtr = Res()
        xbr = [Res(), Res()]
        rowr = [Res(), Res()]
        xlnr = [Res(), Res()]
        for tt in range(NT + 1):
            if tt < NT:
                s2 = tt % 2
                xbr[s2].wait(b, g)
                XSL[s2].dma(g.dma_start(
                    out=xb[:, s2, :, :], in_=b.x3[:, :, tt * TT:(tt + 1) * TT]))
                rowr[s2].wait(b, sy)
                RSL[s2].dma(sy.dma_start(out=rowb[:, s2, 0, :],
                                         in_=b.r1_d[:, tt * TT:(tt + 1) * TT]))
                RSL[s2].dma(sy.dma_start(out=rowb[:, s2, 1, :],
                                         in_=b.m1_d[:, tt * TT:(tt + 1) * TT]))
            if tt == 0:
                continue
            i = tt - 1
            sl = i % 2
            pa[0].wait(b, pe)
            pa[1].wait(b, pe)
            RSL[sl].w(pe)
            LS.w(pe)
            pe.matmul(pA[:, 0:TT], b.onesr_f[:, :], rowb[:, sl, 0, :],
                      start=True, stop=True)
            mm_bc = PE.inc(pe.matmul(pA[:, TT:2 * TT], b.onesr_f[:, :],
                                     rowb[:, sl, 1, :], start=True, stop=True))
            rowr[sl].used(pe=mm_bc)
            PE.w(ve, mm_bc)
            XSL[sl].w(ve)
            SSL[sl].w(ve)
            xlnr[sl].wait(b, ve)
            ve.tensor_tensor(xln[:, sl, :, :], xb[:, sl, :, :],
                             pA[:, 0:TT].unsqueeze(1).broadcast_to([128, NCH, TT]),
                             ALU.mult)
            v_xln = VE.inc(ve.tensor_tensor(
                xln[:, sl, :, :], xln[:, sl, :, :],
                pA[:, TT:2 * TT].unsqueeze(1).broadcast_to([128, NCH, TT]),
                ALU.subtract))
            xbr[sl].used(ve=v_xln)
            pa[0].used(ve=v_xln)
            pa[1].used(ve=v_xln)
            VE.w(sy, v_xln)
            SSL[sl].dma(sy.dma_start(out=b.xln3[:, :, i * TT:(i + 1) * TT],
                                     in_=xln[:, sl, :, :]))
            LD.w(pe)
            PE.w(ve, mm_bc)
            kelr.wait(b, ve)
            kelr.wait(b, sc)
            ve_kel = [0] * NCH
            for m in range(NCH):
                bk_ = m % 2
                pb = bk_ * TT
                pa[bk_].wait(b, pe)
                VE.w(pe, v_xln)
                for k in range(NCH):
                    pe.matmul(pA[:, pb:pb + TT], wk_sb[:, k, m * 128:(m + 1) * 128],
                              xln[:, sl, k, :], start=(k == 0), stop=(k == NCH - 1))
                pd = PE.inc(pe.last)
                s1, vk = elu1(b, pA[:, pb:pb + TT], bkc, m, e_scr, rl_scr,
                              kel[:, m, :], pd)
                ve_kel[m] = vk
                pa[bk_].used(ve=vk, sc=s1)
            vtr.wait(b, ve)
            for tc in range(4):
                for mh in range(2):
                    j = tc * 2 + mh
                    bk_ = j % 2
                    pb = bk_ * 512
                    pa[bk_].wait(b, pe)
                    for k in range(NCH):
                        pe.matmul(pA[:, pb:pb + 512],
                                  xln[:, sl, k, tc * 128:(tc + 1) * 128],
                                  wv_sb[:, k, mh * 512:(mh + 1) * 512],
                                  start=(k == 0), stop=(k == NCH - 1))
                    pev = PE.inc(pe.last)
                    PE.w(ve, pev)
                    v_v = VE.inc(ve.tensor_tensor(
                        v_tile[:, tc, mh * 512:(mh + 1) * 512], pA[:, pb:pb + 512],
                        bvbc[:, mh * 512:(mh + 1) * 512], ALU.add))
                    pa[bk_].used(ve=v_v)
            xlnr[sl].used(pe=PE.n)
            for tc in range(4):
                pend = None
                for hp in range(8):
                    slot = hp % 2
                    if slot == 0:
                        tgt, rl = pkp, [pkp_r]
                    else:
                        tgt, rl = pA, [pa[0], pa[1]]
                    for r in rl:
                        r.wait(b, pe)
                    VE.w(pe, ve_kel[NCH - 1])
                    kpr[slot].wait(b, ve)
                    for par in range(2):
                        h = hp * 2 + par
                        r0 = par * 64
                        pe.matmul(tgt[:, par * 512:par * 512 + 256],
                                  kel[r0:r0 + 64, h // 2, tc * 128:(tc + 1) * 128],
                                  rf_sb[r0:r0 + 64, :], start=True, stop=True)
                    pekp = PE.inc(pe.last)
                    PE.w(ve, pekp)
                    v_kp = VE.inc(ve.tensor_copy(
                        kp_sb[:, slot, :, :],
                        tgt.rearrange("p (a n) -> p a n", n=512)[:, :, 0:256]))
                    for r in rl:
                        r.used(ve=v_kp)
                    if pend is not None:
                        php, pvkp = pend
                        VE.w(pe, pvkp)
                        pkv_r.wait(b, pe)
                        for par in range(2):
                            h = php * 2 + par
                            for fc in range(2):
                                s = h * 2 + fc
                                pe.matmul(pkv[:, s * 64:(s + 1) * 64],
                                          kp_sb[:, php % 2, par,
                                                fc * 128:(fc + 1) * 128],
                                          v_tile[:, tc, h * 64:(h + 1) * 64],
                                          start=True, stop=True)
                        kpr[php % 2].used(pe=PE.inc(pe.last))
                    pend = (hp, v_kp)
                php, pvkp = pend
                VE.w(pe, pvkp)
                pkv_r.wait(b, pe)
                for par in range(2):
                    h = php * 2 + par
                    for fc in range(2):
                        s = h * 2 + fc
                        pe.matmul(pkv[:, s * 64:(s + 1) * 64],
                                  kp_sb[:, php % 2, par, fc * 128:(fc + 1) * 128],
                                  v_tile[:, tc, h * 64:(h + 1) * 64],
                                  start=True, stop=True)
                pkvd = PE.inc(pe.last)
                kpr[php % 2].used(pe=pkvd)
                vtr.used(pe=pkvd)
                kelr.used(pe=pkvd)
                PE.w(ve, pkvd)
                GP.w(ve, gz)
                v_kva = VE.inc(ve.tensor_tensor(
                    kvacc[:, :, :].rearrange("p a n -> p (a n)"), pkv[:, :],
                    kvacc[:, :, :].rearrange("p a n -> p (a n)"), ALU.add))
                pkv_r.used(ve=v_kva)
        vkf = VE.inc(ve.tensor_copy(b.kvt_b[:, :, :].rearrange("p a n -> p (a n)"),
                                    kvacc[:, :, :].rearrange("p a n -> p (a n)")))
        if b.kvt_dbg is not None:
            VE.w(sy, vkf)
            LS.dma(sy.dma_start(out=b.kvt_dbg[:, :],
                                in_=kvacc[:, :, :].rearrange("p a n -> p (a n)")))
        b.phase_end()


def phase_a4(b):
    nc, g, sy, pe, ve, sc = b.nc, b.g, b.sy, b.pe, b.ve, b.sc
    LD, LS, PE, VE, SC = b.LD, b.LS, b.PE, b.VE, b.SC
    NT, T = b.NT, b.T
    with ExitStack() as stp:
        wq_sb = b.wq_sb
        wo_sb = b.wo_sb
        rf_sb = stp.enter_context(nc.sbuf_tensor("a4_rf", [128, FTA], BF16))
        bqc = stp.enter_context(nc.sbuf_tensor("a4_bqc", [128, 8], F32))
        boc = stp.enter_context(nc.sbuf_tensor("a4_boc", [128, 8], F32))
        xlt = stp.enter_context(nc.sbuf_tensor("a4_xlt", [128, 2, NCH, TT], BF16))
        xt = stp.enter_context(nc.sbuf_tensor("a4_xt", [128, NCH, TT], F32))
        qel = stp.enter_context(nc.sbuf_tensor("a4_qel", [128, NCH, TT], BF16))
        e_scr = stp.enter_context(nc.sbuf_tensor("a4_e", [128, TT], F32))
        rl_scr = stp.enter_context(nc.sbuf_tensor("a4_rl", [128, TT], F32))
        qp_sb = stp.enter_context(nc.sbuf_tensor("a4_qp", [128, 2, 2, TT], BF16))
        afm = stp.enter_context(nc.sbuf_tensor("a4_afm", [128, NCH, TT], BF16))
        xA_t = stp.enter_context(nc.sbuf_tensor("a4_xA", [128, NCH, TT], F32))
        xAb = stp.enter_context(nc.sbuf_tensor("a4_xAb", [128, NCH, TT], BF16))
        xAsq = stp.enter_context(nc.sbuf_tensor("a4_xAsq", [128, NCH, TT], BF16))
        s_sb = stp.enter_context(nc.sbuf_tensor("a4_s", [1, T], F32))
        q2_sb = stp.enter_context(nc.sbuf_tensor("a4_q2", [1, T], F32))
        pg = stp.enter_context(nc.psum_tensor("a4_pg", [128, 1024], F32))
        pqp = stp.enter_context(nc.psum_tensor("a4_pqp", [128, 1024], F32))
        pout = stp.enter_context(nc.psum_tensor("a4_pout", [128, 1024], F32))
        prow = stp.enter_context(nc.psum_tensor("a4_prow", [1, 1024], F32))

        LD.dma(g.dma_start(out=rf_sb[0:64, :], in_=b.rf_h[:, :]))
        LD.dma(g.dma_start(out=rf_sb[64:128, :], in_=b.rf_h[:, :]))
        LS.dma(sy.dma_start(out=bqc[:, :], in_=b.bq_h[:, :]))
        LS.dma(sy.dma_start(out=boc[:, :], in_=b.bo_h[:, :]))
        w_ld, c_ld = LD.n, LS.n

        XSL = [b.mksem("a4x0"), b.mksem("a4x1")]
        XTS = b.mksem("a4xt")
        SAS = b.mksem("a4sa")
        b.v_afm_last = 0
        pg_b = [Res(), Res()]
        pqp_r = Res()
        pout_b = [Res(), Res()]
        prow_r = Res()
        qpr = [Res(), Res()]
        qelr = Res()
        afmr = Res()
        xltr = [Res(), Res()]
        xtr = Res()
        for tt in range(NT + 1):
            if tt < NT:
                s2 = tt % 2
                xltr[s2].wait(b, sy)
                XSL[s2].dma(sy.dma_start(
                    out=xlt[:, s2, :, :],
                    in_=b.xln3[:, :, tt * TT:(tt + 1) * TT]))
            if tt == 0:
                continue
            i = tt - 1
            sl = i % 2
            xtr.wait(b, sy)
            SAS.w(sy)
            XTS.dma(sy.dma_start(out=xt[:, :, :],
                                 in_=b.x3[:, :, i * TT:(i + 1) * TT]))
            LD.w(pe)
            XSL[sl].w(pe)
            qelr.wait(b, ve)
            qelr.wait(b, sc)
            ve_qel = [0] * NCH
            for m in range(NCH):
                bk_ = m % 2
                pb = bk_ * TT
                pg_b[bk_].wait(b, pe)
                for k in range(NCH):
                    pe.matmul(pg[:, pb:pb + TT], wq_sb[:, k, m * 128:(m + 1) * 128],
                              xlt[:, sl, k, :], start=(k == 0), stop=(k == NCH - 1))
                pd = PE.inc(pe.last)
                s1, vq = elu1(b, pg[:, pb:pb + TT], bqc, m, e_scr, rl_scr,
                              qel[:, m, :], pd)
                ve_qel[m] = vq
                pg_b[bk_].used(ve=vq, sc=s1)
            xltr[sl].used(pe=PE.n)
            def emit_out(pd_, last=False):
                h_, v_qp_ = pd_
                hp_, par_ = h_ // 2, h_ % 2
                r0_ = par_ * 64
                ob_ = (hp_ % 2) * TT
                VE.w(pe, v_qp_)
                if par_ == 0:
                    pout_b[hp_ % 2].wait(b, pe)
                for fc_ in range(2):
                    s_ = h_ * 2 + fc_
                    pe.matmul(pout[r0_:r0_ + 64, ob_:ob_ + TT], b.kvt_b[:, s_, :],
                              qp_sb[:, h_ % 2, fc_, :], start=(fc_ == 0),
                              stop=(fc_ == 1), tile_position=(0, r0_))
                cnt_ = PE.inc(pe.last)
                qpr[h_ % 2].used(pe=cnt_)
                if par_ == 1:
                    PE.w(ve, cnt_)
                    v_ = VE.inc(ve.tensor_copy(afm[:, hp_, :], pout[:, ob_:ob_ + TT]))
                    pout_b[hp_ % 2].used(ve=v_)
                    b.v_afm_last = v_

            afmr.wait(b, ve)
            pend = None
            for h in range(HA):
                hp = h // 2
                par = h % 2
                r0 = par * 64
                slot = h % 2
                if slot == 0:
                    tgt, rl = pqp, [pqp_r]
                else:
                    tgt, rl = pg, [pg_b[0], pg_b[1]]
                for r in rl:
                    r.wait(b, pe)
                VE.w(pe, ve_qel[NCH - 1])
                qpr[slot].wait(b, ve)
                for fc in range(2):
                    pe.matmul(tgt[:, fc * TT:(fc + 1) * TT],
                              rf_sb[r0:r0 + 64, fc * 128:(fc + 1) * 128],
                              qel[r0:r0 + 64, h // 2, :], start=True, stop=True)
                pq = PE.inc(pe.last)
                PE.w(ve, pq)
                v_qp = VE.inc(ve.tensor_copy(
                    qp_sb[:, slot, :, :],
                    tgt.rearrange("p (a n) -> p a n", n=TT)))
                for r in rl:
                    r.used(ve=v_qp)
                if pend is not None:
                    emit_out(pend)
                pend = (h, v_qp)
            emit_out(pend, last=True)
            v_xa = 0
            for m in range(NCH):
                bk_ = m % 2
                pb = bk_ * TT
                pg_b[bk_].wait(b, pe)
                VE.w(pe, b.v_afm_last)
                for k in range(NCH):
                    pe.matmul(pg[:, pb:pb + TT], wo_sb[:, k, m * 128:(m + 1) * 128],
                              afm[:, k, :], start=(k == 0), stop=(k == NCH - 1))
                pwo = PE.inc(pe.last)
                PE.w(ve, pwo)
                XTS.w(ve)
                LS.w(ve)
                v_xa = VE.inc(ve.scalar_tensor_tensor(
                    xA_t[:, m, :], pg[:, pb:pb + TT], boc[:, m:m + 1],
                    xt[:, m, :], ALU.add, ALU.add))
                pg_b[bk_].used(ve=v_xa)
            afmr.used(pe=PE.n)
            xtr.used(ve=v_xa)
            vc = VE.inc(ve.tensor_copy(xAb[:, :, :], xA_t[:, :, :]))
            vsq = VE.inc(ve.tensor_mul(xAsq[:, :, :], xAb[:, :, :], xAb[:, :, :]))
            VE.w(sy, v_xa)
            SAS.dma(sy.dma_start(out=b.xA3[:, :, i * TT:(i + 1) * TT],
                                 in_=xA_t[:, :, :]))
            SAS.dma(sy.dma_start(out=b.out3[:, :, i * TT:(i + 1) * TT],
                                 in_=xA_t[:, :, :]))
            VE.w(pe, vsq)
            prow_r.wait(b, pe)
            for c in range(NCH):
                pe.matmul(prow[:, 0:TT], b.onesc_b[:, :], xAb[:, c, :],
                          start=(c == 0), stop=(c == NCH - 1))
            for c in range(NCH):
                pe.matmul(prow[:, TT:2 * TT], b.onesc_b[:, :], xAsq[:, c, :],
                          start=(c == 0), stop=(c == NCH - 1))
            pr = PE.inc(pe.last)
            PE.w(ve, pr)
            ve.tensor_copy(s_sb[:, i * TT:(i + 1) * TT], prow[:, 0:TT])
            vrow = VE.inc(ve.tensor_copy(q2_sb[:, i * TT:(i + 1) * TT],
                                         prow[:, TT:2 * TT]))
            prow_r.used(ve=vrow)
        ln_soup(b, stp, s_sb, q2_sb, b.r2_d, b.m2_d, "ln2")
        b.phase_end()


def phase_ln2s(b):
    nc, g, sy, pe, ve, sc = b.nc, b.g, b.sy, b.pe, b.ve, b.sc
    LD, LS, PE, VE, SC, GP = b.LD, b.LS, b.PE, b.VE, b.SC, b.GP
    NT, T = b.NT, b.T
    with ExitStack() as stp:
        xt = stp.enter_context(nc.sbuf_tensor("l2_xt", [128, 2, NCH, TT], F32))
        xo = stp.enter_context(nc.sbuf_tensor("l2_xo", [128, 2, NCH, TT], F32))
        rowb = stp.enter_context(nc.sbuf_tensor("l2_row", [1, 2, 2, TT], F32))
        pA = stp.enter_context(nc.psum_tensor("l2_pA", [128, 1024], F32))
        XSL = [b.mksem("l2x0"), b.mksem("l2x1")]
        RSL = [b.mksem("l2r0"), b.mksem("l2r1")]
        SSL = [b.mksem("l2s0"), b.mksem("l2s1")]
        xtr = [Res(), Res()]
        xor_ = [Res(), Res()]
        rowr = [Res(), Res()]
        pa = [Res(), Res()]
        for tt in range(NT + 1):
            if tt < NT:
                s2 = tt % 2
                xtr[s2].wait(b, sy)
                XSL[s2].dma(sy.dma_start(
                    out=xt[:, s2, :, :], in_=b.xA3[:, :, tt * TT:(tt + 1) * TT]))
                rowr[s2].wait(b, sy)
                RSL[s2].dma(sy.dma_start(out=rowb[:, s2, 0, :],
                                         in_=b.r2_d[:, tt * TT:(tt + 1) * TT]))
                RSL[s2].dma(sy.dma_start(out=rowb[:, s2, 1, :],
                                         in_=b.m2_d[:, tt * TT:(tt + 1) * TT]))
            if tt == 0:
                continue
            i = tt - 1
            sl = i % 2
            pa[0].wait(b, pe)
            pa[1].wait(b, pe)
            RSL[sl].w(pe)
            pe.matmul(pA[:, 0:TT], b.onesr_f[:, :], rowb[:, sl, 0, :],
                      start=True, stop=True)
            mm_bc = PE.inc(pe.matmul(pA[:, TT:2 * TT], b.onesr_f[:, :],
                                     rowb[:, sl, 1, :], start=True, stop=True))
            rowr[sl].used(pe=mm_bc)
            PE.w(ve, mm_bc)
            XSL[sl].w(ve)
            SSL[sl].w(ve)
            xor_[sl].wait(b, ve)
            ve.tensor_tensor(xo[:, sl, :, :], xt[:, sl, :, :],
                             pA[:, 0:TT].unsqueeze(1).broadcast_to([128, NCH, TT]),
                             ALU.mult)
            v_x = VE.inc(ve.tensor_tensor(
                xo[:, sl, :, :], xo[:, sl, :, :],
                pA[:, TT:2 * TT].unsqueeze(1).broadcast_to([128, NCH, TT]),
                ALU.subtract))
            xtr[sl].used(ve=v_x)
            pa[0].used(ve=v_x)
            pa[1].used(ve=v_x)
            VE.w(sy, v_x)
            SSL[sl].dma(sy.dma_start(out=b.xffn3[:, :, i * TT:(i + 1) * TT],
                                     in_=xo[:, sl, :, :]))
        b.phase_end()


def phase_battn(b, l):
    nc, g, sy, pe, ve, sc = b.nc, b.g, b.sy, b.pe, b.ve, b.sc
    LD, LS, PE, VE, SC, GP = b.LD, b.LS, b.PE, b.VE, b.SC, b.GP
    NW, T = b.NW, b.T
    with ExitStack() as stp:
        wqkv = stp.enter_context(nc.sbuf_tensor(f"b{l}_wqkv", [128, NCH, 1536], BF16))
        wo = stp.enter_context(nc.sbuf_tensor(f"b{l}_wo", [128, 4, D], BF16))
        cq = stp.enter_context(nc.sbuf_tensor(f"b{l}_cq", [128, T], BF16))
        sq = stp.enter_context(nc.sbuf_tensor(f"b{l}_sq", [128, T], BF16))
        ck = stp.enter_context(nc.sbuf_tensor(f"b{l}_ck", [128, T], BF16))
        sk = stp.enter_context(nc.sbuf_tensor(f"b{l}_sk", [128, T], BF16))
        qsc = stp.enter_context(nc.sbuf_tensor(f"b{l}_qsc", [128, 1], F32))
        qsrc = stp.enter_context(nc.sbuf_tensor(f"b{l}_qsrc", [128, 1], F32))
        ksc = stp.enter_context(nc.sbuf_tensor(f"b{l}_ksc", [128, 1], F32))
        ksrc = stp.enter_context(nc.sbuf_tensor(f"b{l}_ksrc", [128, 1], F32))
        xft = stp.enter_context(nc.sbuf_tensor(f"b{l}_xft",
                                               [128, 2, NCH, WINB + 1], F32))
        z = stp.enter_context(nc.sbuf_tensor(f"b{l}_z", [128, 2, NCH, WINB], BF16))
        qraw = stp.enter_context(nc.sbuf_tensor(f"b{l}_qraw", [128, 4, WINB], BF16))
        kraw = stp.enter_context(nc.sbuf_tensor(f"b{l}_kraw", [128, 4, WINB], BF16))
        qrot = stp.enter_context(nc.sbuf_tensor(f"b{l}_qrot", [128, 4, WINB], BF16))
        sqs = qrot
        krr = stp.enter_context(nc.sbuf_tensor(f"b{l}_krr", [128, 2, 4, WINB], BF16))
        vrr = stp.enter_context(nc.sbuf_tensor(f"b{l}_vrr", [128, 3, 4, WINB], BF16))
        u1 = stp.enter_context(nc.sbuf_tensor(f"b{l}_u1", [128, 2, WINB], BF16))
        rsq = stp.enter_context(nc.sbuf_tensor(f"b{l}_rsq", [8, WINB], F32))
        rsk = stp.enter_context(nc.sbuf_tensor(f"b{l}_rsk", [8, WINB], F32))
        rsqb = stp.enter_context(nc.sbuf_tensor(f"b{l}_rsqb", [8, WINB], BF16))
        rskb = stp.enter_context(nc.sbuf_tensor(f"b{l}_rskb", [8, WINB], BF16))
        row4 = stp.enter_context(nc.sbuf_tensor(f"b{l}_row4", [1, 4, WINB], F32))
        mask_sb = stp.enter_context(nc.sbuf_tensor(f"b{l}_mask", [128, 1, 640], BF16))
        ae = stp.enter_context(nc.sbuf_tensor(f"b{l}_ae", [128, 4, 640], BF16))
        am = stp.enter_context(nc.sbuf_tensor(f"b{l}_am", [128, 4, 640], BF16))
        am2 = stp.enter_context(nc.sbuf_tensor(f"b{l}_am2", [128, 4, 640], BF16))
        nmx = stp.enter_context(nc.sbuf_tensor(f"b{l}_nmx", [128, 4], F32))
        zsum = stp.enter_context(nc.sbuf_tensor(f"b{l}_zs", [128, 4], F32))
        rz = stp.enter_context(nc.sbuf_tensor(f"b{l}_rz", [128, 4], F32))
        atT = stp.enter_context(nc.sbuf_tensor(f"b{l}_atT", [128, NCH, WINB], BF16))
        ofm = stp.enter_context(nc.sbuf_tensor(f"b{l}_ofm", [128, 4, WINB], BF16))
        xout = stp.enter_context(nc.sbuf_tensor(f"b{l}_xout", [128, NCH, WINB], F32))
        PA = stp.enter_context(nc.psum_tensor(f"b{l}_PA", [128, 1024], F32))
        PB = stp.enter_context(nc.psum_tensor(f"b{l}_PB", [128, 1024], F32))
        PC = stp.enter_context(nc.psum_tensor(f"b{l}_PC", [128, 1024], F32))
        PT = stp.enter_context(nc.psum_tensor(f"b{l}_PT", [128, 1024], BF16))
        PT2 = stp.enter_context(nc.psum_tensor(f"b{l}_PT2", [128, 1024], BF16))

        LD.dma(g.dma_start(out=wqkv[:, :, :],
                           in_=b.wqkv_h[l].rearrange("(c p) m -> p c m", p=128)))
        LD.dma(g.dma_start(out=wo[:, :, :],
                           in_=b.wol_h[l].rearrange("(c p) m -> p c m", p=128)))
        LD.dma(g.dma_start(out=mask_sb[:, 0, :], in_=b.mask_h[:, 4, :]))
        LD.dma(g.dma_start(out=cq[:, :], in_=b.cq_h[:, :]))
        LD.dma(g.dma_start(out=sq[:, :], in_=b.sq_h[:, :]))
        LD.dma(g.dma_start(out=ck[:, :], in_=b.ck_h[:, :]))
        LD.dma(g.dma_start(out=sk[:, :], in_=b.sk_h[:, :]))
        LS.dma(sy.dma_start(out=qsc[:, :], in_=b.qs_h[l][:, :]))
        LS.dma(sy.dma_start(out=qsrc[:, :], in_=b.qsr_h[l][:, :]))
        LS.dma(sy.dma_start(out=ksc[:, :], in_=b.ks_h[l][:, :]))
        LS.dma(sy.dma_start(out=ksrc[:, :], in_=b.ksr_h[l][:, :]))
        GP.inc(g.memset(krr[:, 1, :, :], 0.0))
        GP.inc(g.memset(vrr[:, 2, :, :], 0.0))
        GP.inc(g.memset(xft[:, :, :, 0:1], 0.0))
        GP.inc(g.memset(z[:, 0, 4:8, 0:1], 0.0))
        w_ld, c_ld, g_ms = LD.n, LS.n, GP.n
        LD.w(pe)
        LS.w(pe)
        GP.w(pe, g_ms)
        GP.w(ve, g_ms)
        GP.w(sy, g_ms)
        paB = [Res(), Res()]
        pbB = [Res(), Res()]
        pcB = [Res(), Res()]
        ptR = [Res(), Res()]
        atTr = Res()
        zR = [Res(), Res()]
        vrrR = [Res(), Res(), Res()]
        qkr = Res()
        xftR = [Res(), Res()]
        aer = [Res(), Res(), Res(), Res()]
        XS2 = [b.mksem(f"b{l}x0"), b.mksem(f"b{l}x1")]
        ZS2 = [b.mksem(f"b{l}z0"), b.mksem(f"b{l}z1")]
        wst = {}

        def head_tasks(w2):
            if w2 >= NW:
                return []
            t02 = w2 * WINB
            sl2 = w2 % 2
            st2 = wst.setdefault(w2, {})

            def t_load():
                xftR[sl2].wait(b, sy)
                if w2 == 0:
                    XS2[sl2].dma(sy.dma_start(out=xft[:, sl2, :, 1:WINB + 1],
                                              in_=b.xffn3[:, :, 0:WINB]))
                else:
                    XS2[sl2].dma(sy.dma_start(
                        out=xft[:, sl2, :, :],
                        in_=b.xffn3[:, :, t02 - 1:t02 + WINB]))

            def t_z():
                zR[sl2].wait(b, g)
                if w2 == 0:
                    ZS2[sl2].dma(g.dma_start(out=z[:, sl2, 0:4, :],
                                             in_=b.xffn3[:, 0:4, 0:WINB]))
                    ZS2[sl2].dma(g.dma_start(out=z[:, sl2, 4:8, 1:WINB],
                                             in_=b.xffn3[:, 4:8, 0:WINB - 1]))
                else:
                    ZS2[sl2].dma(g.dma_start(out=z[:, sl2, 0:4, :],
                                             in_=b.xffn3[:, 0:4, t02:t02 + WINB]))
                    ZS2[sl2].dma(g.dma_start(
                        out=z[:, sl2, 4:8, :],
                        in_=b.xffn3[:, 4:8, t02 - 1:t02 - 1 + WINB]))

            def mk_qkv(m):
                def t_qkv():
                    bk_ = m % 2
                    pb = bk_ * 512
                    pbB[bk_].wait(b, pe)
                    ZS2[sl2].w(pe)
                    for k in range(NCH):
                        pe.matmul(PB[:, pb:pb + 512],
                                  wqkv[:, k, m * 128:(m + 1) * 128],
                                  z[:, sl2, k, :], start=(k == 0),
                                  stop=(k == NCH - 1))
                    pq = PE.inc(pe.last)
                    dst = qraw[:, m, :] if m < 4 else kraw[:, m - 4, :]
                    PE.w(sc, pq)
                    qkr.wait(b, sc)
                    pbB[bk_].used(sc=SC.inc(sc.copy(dst, PB[:, pb:pb + 512])))
                return t_qkv

            def mk_v(tc):
                def t_v():
                    bk_ = tc % 2
                    pb = bk_ * 512
                    pbB[bk_].wait(b, pe)
                    vrrR[w2 % 3].wait(b, pe)
                    ZS2[sl2].w(pe)
                    for k in range(NCH):
                        pe.matmul(PB[:, pb:pb + 512],
                                  z[:, sl2, k, tc * 128:(tc + 1) * 128],
                                  wqkv[:, k, 1024:1536], start=(k == 0),
                                  stop=(k == NCH - 1))
                    pv = PE.inc(pe.last)
                    PE.w(sc, pv)
                    pbB[bk_].used(sc=SC.inc(sc.copy(
                        vrr[:, w2 % 3, tc, :], PB[:, pb:pb + 512])))
                return t_v

            def t_done():
                st2["evd_sc"] = SC.n
                st2["evd_ve"] = VE.n
                zR[sl2].used(pe=PE.n)

            return ([t_load, t_z] + [mk_qkv(m) for m in range(8)] +
                    [mk_v(tc) for tc in range(4)] + [t_done])

        for t in head_tasks(0):
            t()

        for w in range(NW):
            t0 = w * WINB
            wsl = w % 2
            psl = 1 - wsl
            vsl = w % 3
            pvsl = (w - 1) % 3 if w > 0 else 2
            st = wst[w]
            res_t = xft
            SC.w(ve, st["evd_sc"])
            for raw, rr, rrb in ((qraw, rsq, rsqb), (kraw, rsk, rskb)):
                vsq = VE.inc(ve.tensor_mul(sqs[:, :, :], raw[:, :, :], raw[:, :, :]))
                VE.w(pe, vsq)
                pcB[0].wait(b, pe)
                for c in range(4):
                    pe.matmul(PC[0:8, 0:512], b.selq_b[:, c, :], sqs[:, c, :],
                              start=(c == 0), stop=(c == 3))
                pn = PE.inc(pe.last)
                PE.w(ve, pn)
                veps = VE.inc(ve.tensor_scalar_add(rr[:, :], PC[0:8, 0:512], 1e-30))
                pcB[0].used(ve=veps)
                VE.w(sc, veps)
                sc.activation(rr[:, :], rr[:, :], AF.Ln, bias=0.0, scale=1.0)
                s2 = SC.inc(sc.activation(rr[:, :], rr[:, :], AF.Exp, bias=0.0,
                                          scale=-0.5))
                SC.w(ve, s2)
                VE.inc(ve.tensor_scalar_min(rrb[:, :], rr[:, :], 1e12))
            qk_pe = 0
            for ti, (raw, rot_q, C_t, S_t, scol, srcol, rrb) in enumerate((
                    (qraw, True, cq, sq, qsc, qsrc, rsqb),
                    (kraw, False, ck, sk, ksc, ksrc, rskb))):
                for c in range(4):
                    bk_ = (ti * 4 + c) % 2
                    pbB[bk_].wait(b, pe)
                    pcB[bk_].wait(b, pe)
                    pe.matmul(PB[:, bk_ * 512:bk_ * 512 + 512], b.perm_b[:, :],
                              raw[:, c, :], start=True, stop=True)
                    pe.matmul(PC[:, bk_ * 512:bk_ * 512 + 512], b.selbc_b[:, c, :],
                              rrb[:, :], start=True, stop=True)
                    pr2 = PE.inc(pe.last)
                    qk_pe = pr2
                    PE.w(ve, pr2)
                    ve.scalar_tensor_tensor(u1[:, 0, :], raw[:, c, :], scol[:, :],
                                            C_t[:, t0:t0 + WINB], ALU.mult, ALU.mult)
                    ve.scalar_tensor_tensor(u1[:, 1, :],
                                            PB[:, bk_ * 512:bk_ * 512 + 512],
                                            srcol[:, :],
                                            S_t[:, t0:t0 + WINB], ALU.mult, ALU.mult)
                    ve.tensor_add(u1[:, 0, :], u1[:, 0, :], u1[:, 1, :])
                    dst = qrot[:, c, :] if rot_q else krr[:, wsl, c, :]
                    v4 = VE.inc(ve.tensor_tensor(dst, u1[:, 0, :],
                                                 PC[:, bk_ * 512:bk_ * 512 + 512],
                                                 ALU.mult))
                    pbB[bk_].used(ve=v4)
                    pcB[bk_].used(ve=v4)
            qkr.used(pe=qk_pe, ve=VE.n)
            vrot = VE.n
            s_n_rec = {}
            v_ev_rec = {}
            s_sim_rec = {}
            s_vm_rec = {}
            tasks = head_tasks(w + 1)

            def sim_env(j):
                return (PC, [pcB[0], pcB[1]]) if j % 2 == 0 else \
                       (PA, [paB[0], paB[1]])

            def emit_sim(j, h, qc):
                r0 = (h % 2) * 64
                hc = h // 2
                simT, simR = sim_env(j)
                for r in simR:
                    r.wait(b, pe)
                VE.w(pe, vrot)
                lo = qc * 128
                hi = qc * 128 + 640
                a = lo
                while a < hi:
                    e = min((a // 512 + 1) * 512, hi)
                    if a < lo + 512 < e:
                        e = lo + 512
                    slot, cs = (psl, a) if a < 512 else (wsl, a - 512)
                    pe.matmul(simT[:, a - lo:e - lo],
                              qrot[r0:r0 + 64, hc, lo:lo + 128],
                              krr[r0:r0 + 64, slot, hc, cs:cs + (e - a)],
                              start=True, stop=True)
                    a = e
                s_sim_rec[j] = PE.inc(pe.last)

            def emit_max(j, h, qc):
                s2i = j % 4
                simT, simR = sim_env(j)
                PE.w(ve, s_sim_rec[j])
                aer[s2i].wait(b, ve)
                vm = VE.inc(ve.reduce_max(nmx[:, s2i:s2i + 1], simT[:, 0:640],
                                          AX.X, negate=True))
                s_vm_rec[j] = vm
                for r in simR:
                    r.used(ve=vm)

            def emit_post(j, h, qc):
                s2i = j % 4
                simT, simR = sim_env(j)
                VE.w(sc, s_vm_rec[j])
                se = SC.inc(sc.activation(ae[:, s2i, :], simT[:, 0:640], AF.Exp,
                                          bias=nmx[:, s2i:s2i + 1], scale=1.0))
                for r in simR:
                    r.used(sc=se)
                SC.w(ve, se)
                if w == 0 and qc < 4:
                    z0 = (4 - qc) * 128
                    if z0 > 0:
                        ve.memset(am[:, s2i, 0:z0], 0.0)
                    ve.scalar_tensor_tensor(am[:, s2i, z0:640], ae[:, s2i, z0:640],
                                            1.0, mask_sb[:, 0, z0:640],
                                            ALU.mult, ALU.mult,
                                            accum_out=zsum[:, s2i:s2i + 1])
                else:
                    ve.scalar_tensor_tensor(am[:, s2i, :], ae[:, s2i, :], 1.0,
                                            mask_sb[:, 0, :], ALU.mult, ALU.mult,
                                            accum_out=zsum[:, s2i:s2i + 1])
                ve.tensor_scalar_max(zsum[:, s2i:s2i + 1], zsum[:, s2i:s2i + 1],
                                     1e-30)
                v_z = VE.inc(ve.reciprocal(rz[:, s2i:s2i + 1],
                                           zsum[:, s2i:s2i + 1]))
                if j % 2 == 0:
                    s_n_rec[j] = ("ve", VE.inc(ve.tensor_tensor(
                        am2[:, s2i, :], am[:, s2i, :],
                        rz[:, s2i:s2i + 1].broadcast_to([128, 640]), ALU.mult)))
                else:
                    VE.w(sc, v_z)
                    s_n_rec[j] = ("sc", SC.inc(sc.activation(
                        am2[:, s2i, :], am[:, s2i, :], AF.Identity, bias=0.0,
                        scale=rz[:, s2i:s2i + 1])))

            def emit_transp_evac(j, h, qc):
                s2i = j % 4
                pt_ = PT if j % 2 == 0 else PT2
                ptRj = ptR[j % 2]
                ptRj.wait(b, pe)
                atTr.wait(b, pe)
                kind_n, cnt_n = s_n_rec[j]
                if kind_n == "ve":
                    VE.w(pe, cnt_n)
                else:
                    SC.w(pe, cnt_n)
                for jj in range(5):
                    pe.transpose(pt_[:, jj * 128:(jj + 1) * 128],
                                 am2[:, s2i, jj * 128:(jj + 1) * 128],
                                 b.ident_b[:, :])
                ptd = PE.inc(pe.last)
                aer[s2i].used(pe=ptd)
                PE.w(sc, ptd)
                s_ev = SC.inc(sc.copy(
                    atT[:, qc:qc + 5, qc * 128:(qc + 1) * 128],
                    pt_.rearrange("p (j n) -> p j n", n=128)[:, 0:5, :]))
                ptRj.used(sc=s_ev)
                v_ev_rec[h] = ("sc", s_ev)

            def emit_av(h):
                nonlocal v_ofm
                r0 = (h % 2) * 64
                hc = h // 2
                ob = (hc % 2) * 512
                for hh in (h, h - 1 if h % 2 else h):
                    kind, cnt = v_ev_rec[hh]
                    if kind == "ve":
                        VE.w(pe, cnt)
                    else:
                        SC.w(pe, cnt)
                pbB[hc % 2].wait(b, pe)
                first = True
                for kcl in (4, 0, 1, 2, 3, 5, 6, 7):
                    qlo = max(kcl - 4, 0) * 128
                    qhi = min(kcl + 1, 4) * 128
                    if qhi <= qlo:
                        continue
                    slot = pvsl if kcl < 4 else vsl
                    tc = kcl % 4
                    pe.matmul(PB[r0:r0 + 64, ob + qlo:ob + qhi],
                              vrr[:, slot, tc, h * 64:(h + 1) * 64],
                              atT[:, kcl, qlo:qhi],
                              start=first, stop=(kcl == 7), tile_position=(0, r0))
                    first = False
                pav = PE.inc(pe.last)
                atTr.used(pe=pav)
                vrrR[vsl].used(pe=pav)
                vrrR[pvsl].used(pe=pav)
                if h % 2 == 1:
                    PE.w(ve, pav)
                    v_ofm = VE.inc(ve.tensor_copy(ofm[:, hc, :], PB[:, ob:ob + 512]))
                    pbB[hc % 2].used(ve=v_ofm)

            items = []
            for hp_ in range(HB // 2):
                for qc_ in range(4):
                    items.append((2 * hp_, qc_))
                    items.append((2 * hp_ + 1, qc_))
            LAG = 2
            v_ofm = 0
            ti_ = 0
            emit_sim(0, *items[0])
            emit_max(0, *items[0])
            for j in range(len(items) + LAG):
                if j + 1 < len(items):
                    emit_sim(j + 1, *items[j + 1])
                    emit_max(j + 1, *items[j + 1])
                if j < len(items):
                    emit_post(j, *items[j])
                while ti_ < len(tasks) and ti_ <= (j * len(tasks)) // max(
                        len(items) - 4, 1):
                    tasks[ti_]()
                    ti_ += 1
                if j >= LAG:
                    h2, qc2 = items[j - LAG]
                    emit_transp_evac(j - LAG, h2, qc2)
                    if qc2 == 3:
                        emit_av(h2)
            while ti_ < len(tasks):
                tasks[ti_]()
                ti_ += 1
            v_xo = 0
            for m in range(NCH):
                bk_ = m % 2
                pb = bk_ * 512
                paB[bk_].wait(b, pe)
                VE.w(pe, v_ofm)
                for k in range(4):
                    pe.matmul(PA[:, pb:pb + 512], wo[:, k, m * 128:(m + 1) * 128],
                              ofm[:, k, :], start=(k == 0), stop=(k == 3))
                pwo = PE.inc(pe.last)
                PE.w(ve, pwo)
                XS2[wsl].w(ve)
                v_xo = VE.inc(ve.tensor_tensor(xout[:, m, :], PA[:, pb:pb + 512],
                                               res_t[:, wsl, m, 1:WINB + 1],
                                               ALU.add))
                paB[bk_].used(ve=v_xo)
            xftR[wsl].used(ve=v_xo)
            VE.w(sy, v_xo)
            ZS2[psl].w(sy)
            LS.dma(sy.dma_start(out=b.xffn3[:, :, t0:t0 + WINB], in_=xout[:, :, :]))
            xob = z
            zqt = atT
            VE.w(sc, v_xo)
            zR[wsl].wait(b, sc)
            vxb = SC.inc(sc.copy(xob[:, wsl, :, :], xout[:, :, :]))
            SC.w(ve, vxb)
            vq2 = VE.inc(ve.tensor_mul(zqt[:, :, :], xob[:, wsl, :, :],
                                       xob[:, wsl, :, :]))
            VE.w(pe, vq2)
            SC.w(pe, vxb)
            for r in (paB[0], paB[1], pcB[0], pcB[1]):
                r.wait(b, pe)
            for c in range(4):
                pe.matmul(PA[0:1, 0:512], b.onesc_b[:, :], xob[:, wsl, c, :],
                          start=(c == 0), stop=(c == 3))
            for c in range(4, 8):
                pe.matmul(PA[0:1, 512:1024], b.onesc_b[:, :], xob[:, wsl, c, :],
                          start=(c == 4), stop=(c == 7))
            for c in range(4):
                pe.matmul(PC[0:1, 0:512], b.onesc_b[:, :], zqt[:, c, :],
                          start=(c == 0), stop=(c == 3))
            for c in range(4, 8):
                pe.matmul(PC[0:1, 512:1024], b.onesc_b[:, :], zqt[:, c, :],
                          start=(c == 4), stop=(c == 7))
            prw = PE.inc(pe.last)
            atTr.used(pe=prw)
            zR[wsl].used(pe=prw)
            PE.w(sc, prw)
            sc.copy(row4[:, 0:2, :].rearrange("p a n -> p (a n)"),
                    PA[0:1, 0:1024])
            s_r4 = SC.inc(sc.copy(
                row4[:, 2:4, :].rearrange("p a n -> p (a n)"), PC[0:1, 0:1024]))
            for r in (paB[0], paB[1], pcB[0], pcB[1]):
                r.used(sc=s_r4)
            SC.w(sy, s_r4)
            LS.dma(sy.dma_start(out=b.rows4_d[l][:, :, t0:t0 + WINB],
                                in_=row4[:, :, :]))
        b.phase_end()


def phase_bffn(b, l):
    nc, g, sy, pe, ve, sc = b.nc, b.g, b.sy, b.pe, b.ve, b.sc
    LD, LS, PE, VE, SC, GP = b.LD, b.LS, b.PE, b.VE, b.SC, b.GP
    NT, T = b.NT, b.T
    with ExitStack() as stp:
        lgc = stp.enter_context(nc.sbuf_tensor(f"f{l}_lgc", [128, 8], F32))
        lbc = stp.enter_context(nc.sbuf_tensor(f"f{l}_lbc", [128, 8], F32))
        b1c = stp.enter_context(nc.sbuf_tensor(f"f{l}_b1c", [128, 44], F32))
        b2c = stp.enter_context(nc.sbuf_tensor(f"f{l}_b2c", [128, 8], F32))
        LS.dma(sy.dma_start(out=lgc[:, :], in_=b.lg_h[l][:, :]))
        LS.dma(sy.dma_start(out=lbc[:, :], in_=b.lb_h[l][:, :]))
        LS.dma(sy.dma_start(out=b1c[:, :], in_=b.b1_h[l][:, :]))
        LS.dma(sy.dma_start(out=b2c[:, :], in_=b.b2_h[l][:, :]))

        w1 = stp.enter_context(nc.sbuf_tensor(f"f{l}_w1", [128, NCH, W1N], BF16))
        LD.dma(g.dma_start(out=w1[:, :, :],
                           in_=b.w1_h[l].rearrange("(c p) m -> p c m", p=128)))

        with ExitStack() as s1:
            t1 = s1.enter_context(nc.sbuf_tensor(f"f{l}_t1", [1, T], F32))
            t2 = s1.enter_context(nc.sbuf_tensor(f"f{l}_t2", [1, T], F32))
            t3 = s1.enter_context(nc.sbuf_tensor(f"f{l}_t3", [1, T], F32))
            r4 = b.rows4_d[l]
            LS.dma(sy.dma_start(out=t1[:, :], in_=r4[:, 0, :]))
            LS.dma(sy.dma_start(out=t2[:, :], in_=r4[:, 1, :]))
            LS.dma(sy.dma_start(out=t3[:, :], in_=r4[:, 2, :]))
            LS.w(ve)
            VE.inc(ve.tensor_add(t1[:, 1:T], t1[:, 1:T], t2[:, 0:T - 1]))
            VE.w(sy, VE.n)
            LS.dma(sy.dma_start(out=t2[:, :], in_=r4[:, 3, :]))
            LS.w(ve)
            VE.inc(ve.tensor_add(t3[:, 1:T], t3[:, 1:T], t2[:, 0:T - 1]))
            ve.tensor_copy(t2[:, :], t3[:, :])
            ln_soup(b, s1, t1, t2, b.rg_d[l], b.nm_d[l], f"f{l}", neg=True)
            b.phase_end()

        with ExitStack() as s2p:
            w2 = s2p.enter_context(nc.sbuf_tensor(f"f{l}_w2", [128, NK2, D], BF16))
            LD.dma(g.dma_start(out=w2[:, :, :],
                               in_=b.w2_h[l].rearrange("(c p) m -> p c m", p=128)))
            xft = s2p.enter_context(nc.sbuf_tensor(f"f{l}_xft2",
                                                   [128, 2, NCH, TT + 1], F32))
            xln2 = s2p.enter_context(nc.sbuf_tensor(f"f{l}_xl2", [128, NCH, TT], BF16))
            uscr = s2p.enter_context(nc.sbuf_tensor(f"f{l}_us", [128, 1, TT], F32))
            glu = s2p.enter_context(nc.sbuf_tensor(f"f{l}_glu", [128, NK2, TT], BF16))
            gg = s2p.enter_context(nc.sbuf_tensor(f"f{l}_gg", [128, 1, TT], BF16))
            rowb = s2p.enter_context(nc.sbuf_tensor(f"f{l}_rowb", [1, 2, 2, TT], F32))
            pbc = s2p.enter_context(nc.psum_tensor(f"f{l}_pbc", [128, 1024], F32))
            pag = s2p.enter_context(nc.psum_tensor(f"f{l}_pag", [128, 2048], F32))
            pw2 = s2p.enter_context(nc.psum_tensor(f"f{l}_pw2", [128, 1024], F32))
            GP.inc(g.memset(xft[:, :, :, 0:1], 0.0))
            XFS = b.mksem(f"f{l}s2x")
            RS2 = b.mksem(f"f{l}s2r")
            STS = b.mksem(f"f{l}s2s")
            xftr = [Res(), Res()]
            rowr = [Res(), Res()]
            pbcB = [Res(), Res()]
            agB = [Res(), Res()]
            w2B = [Res(), Res()]
            z2r = Res()
            glur = Res()
            ggr = [Res(), Res()]
            usr = [Res(), Res()]
            LD.w(pe)
            GP.w(ve, GP.n)
            GP.w(sy, GP.n)
            XF2 = [XFS, b.mksem(f"f{l}s2y")]
            RSL2 = [RS2, b.mksem(f"f{l}s2r2")]
            vxl_rec = [0] * (NT + 1)

            def emit_ln_apply(i2):
                sl2 = i2 % 2
                pbcB[0].wait(b, pe)
                pbcB[1].wait(b, pe)
                RSL2[sl2].w(pe)
                pe.matmul(pbc[:, 0:TT], b.onesr_f[:, :], rowb[:, sl2, 0, :],
                          start=True, stop=True)
                pb2 = PE.inc(pe.matmul(pbc[:, TT:2 * TT], b.onesr_f[:, :],
                                       rowb[:, sl2, 1, :], start=True, stop=True))
                rowr[sl2].used(pe=pb2)
                for c in range(NCH):
                    us = 0
                    off = 1 if c < 4 else 0
                    PE.w(ve, pb2)
                    XF2[sl2].w(ve)
                    usr[us].wait(b, ve)
                    z2r.wait(b, ve)
                    ve.tensor_tensor(uscr[:, us, :], xft[:, sl2, c, off:off + TT],
                                     pbc[:, 0:TT], ALU.mult)
                    vu = VE.inc(ve.tensor_tensor(uscr[:, us, :], uscr[:, us, :],
                                                 pbc[:, TT:2 * TT], ALU.add))
                    xftr[sl2].used(ve=vu)
                    pbcB[0].used(ve=vu)
                    pbcB[1].used(ve=vu)
                    VE.w(sc, vu)
                    LS.w(sc)
                    s_x = SC.inc(sc.activation(xln2[:, c, :], uscr[:, us, :],
                                               AF.Identity, bias=lbc[:, c:c + 1],
                                               scale=lgc[:, c:c + 1]))
                    usr[us].used(sc=s_x)
                    z2r.used(sc=s_x)
                vxl_rec[i2] = SC.n

            for i in range(NT):
                a = i * TT
                sl = i % 2
                if i == 0:
                    for pf in (0, 1):
                        if pf >= NT:
                            break
                        ap_ = pf * TT
                        if pf == 0:
                            XF2[0].dma(sy.dma_start(out=xft[:, 0, :, 1:TT + 1],
                                                    in_=b.xffn3[:, :, 0:TT]))
                        else:
                            XF2[1].dma(sy.dma_start(
                                out=xft[:, 1, :, :],
                                in_=b.xffn3[:, :, ap_ - 1:ap_ + TT]))
                        RSL2[pf].dma(sy.dma_start(out=rowb[:, pf, 0, :],
                                                  in_=b.rg_d[l][:, ap_:ap_ + TT]))
                        RSL2[pf].dma(sy.dma_start(out=rowb[:, pf, 1, :],
                                                  in_=b.nm_d[l][:, ap_:ap_ + TT]))
                elif i + 1 < NT:
                    ap_ = (i + 1) * TT
                    s3 = (i + 1) % 2
                    xftr[s3].wait(b, sy)
                    STS.w(sy)
                    XF2[s3].dma(sy.dma_start(out=xft[:, s3, :, :],
                                             in_=b.xffn3[:, :, ap_ - 1:ap_ + TT]))
                    rowr[s3].wait(b, sy)
                    RSL2[s3].dma(sy.dma_start(out=rowb[:, s3, 0, :],
                                              in_=b.rg_d[l][:, ap_:ap_ + TT]))
                    RSL2[s3].dma(sy.dma_start(out=rowb[:, s3, 1, :],
                                              in_=b.nm_d[l][:, ap_:ap_ + TT]))
                if i == 0:
                    emit_ln_apply(0)
                SC.w(pe, vxl_rec[i])
                for j in range(NK2):
                    bk_ = j % 2
                    pb = bk_ * 512
                    agB[bk_].wait(b, pe)
                    for k in range(NCH):
                        pe.matmul(pag[:, pb:pb + 512],
                                  w1[:, k, j * 128:(j + 1) * 128],
                                  xln2[:, k, :], start=(k == 0), stop=(k == NCH - 1))
                    for k in range(NCH):
                        pe.matmul(pag[:, 1024 + pb:1024 + pb + 512],
                                  w1[:, k, (NK2 + j) * 128:(NK2 + j + 1) * 128],
                                  xln2[:, k, :], start=(k == 0), stop=(k == NCH - 1))
                    pj = PE.inc(pe.last)
                    PE.w(sc, pj)
                    ggr[0].wait(b, sc)
                    sg = SC.inc(sc.activation(gg[:, 0, :],
                                              pag[:, 1024 + pb:1024 + pb + 512],
                                              AF.Gelu,
                                              bias=b1c[:, NK2 + j:NK2 + j + 1],
                                              scale=1.0))
                    SC.w(ve, sg)
                    vg = VE.inc(ve.scalar_tensor_tensor(
                        glu[:, j, :], pag[:, pb:pb + 512], b1c[:, j:j + 1],
                        gg[:, 0, :], ALU.add, ALU.mult))
                    agB[bk_].used(sc=sg, ve=vg)
                    ggr[0].used(ve=vg)
                z2r.used(pe=PE.n)
                vglu = VE.n
                if i + 1 < NT:
                    emit_ln_apply(i + 1)
                VE.w(pe, vglu)
                v_xo = 0
                for m in range(NCH):
                    bk_ = m % 2
                    pb = bk_ * 512
                    w2B[bk_].wait(b, pe)
                    for k in range(NK2):
                        pe.matmul(pw2[:, pb:pb + 512], w2[:, k, m * 128:(m + 1) * 128],
                                  glu[:, k, :], start=(k == 0), stop=(k == NK2 - 1))
                    pm = PE.inc(pe.last)
                    PE.w(ve, pm)
                    v_xo = VE.inc(ve.scalar_tensor_tensor(
                        xft[:, sl, m, 1:TT + 1], pw2[:, pb:pb + 512],
                        b2c[:, m:m + 1],
                        xft[:, sl, m, 1:TT + 1], ALU.add, ALU.add))
                    w2B[bk_].used(ve=v_xo)
                glur.used(pe=PE.n)
                xftr[sl].used(ve=v_xo)
                if l == 0:
                    VE.w(sy, v_xo)
                    STS.dma(sy.dma_start(out=b.xffn3[:, :, a:a + TT],
                                         in_=xft[:, sl, :, 1:TT + 1]))
                else:
                    VE.w(g, v_xo)
                    STS.dma(g.dma_start(out=b.out3[:, :, a:a + TT],
                                        in_=xft[:, sl, :, 1:TT + 1],
                                        accum_op=ALU.add))
            b.phase_end()



def kernel(**inputs):
    T = 4096
    consts = host_consts(T)
    w = host_weights(inputs)
    nc = build_nc(T)
    x = np.asarray(inputs["L0_attn_x"], np.float32)
    in_maps = []
    for core in range(8):
        m = {"x": np.ascontiguousarray(x[core].T)}
        for k, v in w.items():
            m[k] = np.ascontiguousarray(np.asarray(v, np.float32))
        for k, v in consts.items():
            m[k] = np.ascontiguousarray(np.asarray(v, np.float32))
        in_maps.append(m)
    res = run_bass_kernel_spmd(nc, in_maps, core_ids=list(range(8)))
    out = np.stack([np.asarray(res.results[i]["out"]).T for i in range(8)])
    return np.ascontiguousarray(out.astype(np.float32))


# version 19
# speedup vs baseline: 1.0212x; 1.0212x over previous
import numpy as np
import ml_dtypes
from contextlib import ExitStack
import concourse.bass as bass
import concourse.mybir as mybir
from concourse.bass_utils import run_bass_kernel_spmd

F32 = mybir.dt.float32
BF16 = mybir.dt.bfloat16
AF = mybir.ActivationFunctionType
ALU = mybir.AluOpType
AX = mybir.AxisListType
BFNP = ml_dtypes.bfloat16

D = 1024
NCH = 8
HA, DHA, FTA = 16, 64, 256
HB, DHB, WINB = 8, 64, 512
HID = 2730
HIDP = 2816
W1N = 2 * HIDP
NK2 = HIDP // 128
QK_SCALE = 8.0
TT = 512



def host_consts(T):
    c = {}
    c["ident"] = np.eye(128, dtype=np.float32)
    c["ones_col"] = np.ones((128, 1), np.float32)
    c["ones_row"] = np.ones((1, 128), np.float32)
    sel = np.zeros((128, 4, 8), np.float32)
    for cc in range(4):
        sel[0:64, cc, 2 * cc] = 1.0
        sel[64:128, cc, 2 * cc + 1] = 1.0
    c["selq"] = sel
    sb = np.zeros((8, 4, 128), np.float32)
    for cc in range(4):
        sb[2 * cc, cc, 0:64] = 1.0
        sb[2 * cc + 1, cc, 64:128] = 1.0
    c["selbc"] = sb
    P = np.zeros((128, 128), np.float32)
    for m in range(128):
        blk = (m // 64) * 64
        ml = m % 64
        P[blk + (ml + 32) % 64, m] = -1.0 if ml < 32 else 1.0
    c["perm"] = P
    half = np.arange(0, DHB, 2, dtype=np.float64)
    inv_freq = 1.0 / 10000.0 ** (half / DHB)
    t = np.arange(T, dtype=np.float64)
    freqs = t[:, None] * inv_freq[None, :]
    freqs = np.concatenate([freqs, freqs], -1)
    base = (half + 0.4 * DHB) / (1.4 * DHB)
    power = (t - T // 2) / float(WINB // 2)
    scale = base[None, :] ** power[:, None]
    scale = np.concatenate([scale, scale], -1)
    cos, sin = np.cos(freqs), np.sin(freqs)

    def two(v):
        return np.ascontiguousarray(np.tile(v.T.astype(np.float32), (2, 1)))

    c["cq"] = two(cos * scale * QK_SCALE)
    c["sq"] = two(sin * scale * QK_SCALE)
    c["ck"] = two(cos / scale)
    c["sk"] = two(sin / scale)
    jj = np.arange(640)[None, :]
    il = np.arange(128)[:, None]
    gen = ((jj >= il) & (jj <= il + WINB)).astype(np.float32)
    masks = np.zeros((5, 128, 640), np.float32)
    masks[4] = gen
    for qc in range(4):
        masks[qc] = gen * (jj >= WINB - qc * 128)
    c["mask"] = np.ascontiguousarray(masks.transpose(1, 0, 2))
    return c


def col128(b):
    return np.ascontiguousarray(np.asarray(b, np.float32).reshape(-1, 128).T)


def host_weights(inp):
    w = {}
    w["Wq"] = inp["L0_attn_Wq"]
    w["Wk"] = inp["L0_attn_Wk"]
    w["Wv"] = inp["L0_attn_Wv"]
    w["Wo"] = inp["L0_attn_Wo"]
    w["bq"] = col128(inp["L0_attn_bq"])
    w["bk"] = col128(inp["L0_attn_bk"])
    w["bv"] = np.ascontiguousarray(np.asarray(inp["L0_attn_bv"], np.float32).reshape(1, D))
    w["bo"] = col128(inp["L0_attn_bo"])
    w["rf"] = inp["L0_attn_random_features"]
    for l in range(2):
        w[f"wqkv{l}"] = np.ascontiguousarray(inp["L0_ffn_wqkv"][l])
        w[f"wo{l}"] = np.ascontiguousarray(inp["L0_ffn_wo"][l])
        qs = np.asarray(inp["L0_ffn_q_scale"][l], np.float32)
        ks = np.asarray(inp["L0_ffn_k_scale"][l], np.float32)
        w[f"qs{l}"] = np.ascontiguousarray(np.tile(qs, 2)[:, None])
        w[f"qsr{l}"] = np.ascontiguousarray(np.tile(np.roll(qs, -32), 2)[:, None])
        w[f"ks{l}"] = np.ascontiguousarray(np.tile(ks, 2)[:, None])
        w[f"ksr{l}"] = np.ascontiguousarray(np.tile(np.roll(ks, -32), 2)[:, None])
        w[f"lg{l}"] = np.ascontiguousarray(np.asarray(inp["L0_ffn_ln_g"][l], np.float32).reshape(1, D))
        w[f"lb{l}"] = col128(inp["L0_ffn_ln_b"][l])
        w1 = np.asarray(inp["L0_ffn_w1"][l], np.float32)
        w1p = np.zeros((D, W1N), np.float32)
        w1p[:, :HID] = w1[:, :HID]
        w1p[:, HIDP:HIDP + HID] = w1[:, HID:]
        w[f"w1{l}"] = w1p
        b1 = np.asarray(inp["L0_ffn_b1"][l], np.float32)
        b1p = np.zeros((W1N,), np.float32)
        b1p[:HID] = b1[:HID]
        b1p[HIDP:HIDP + HID] = b1[HID:]
        w[f"b1{l}"] = col128(b1p)
        w2 = np.asarray(inp["L0_ffn_w2"][l], np.float32)
        w2p = np.zeros((HIDP, D), np.float32)
        w2p[:HID] = w2
        w[f"w2{l}"] = w2p
        w[f"b2{l}"] = col128(inp["L0_ffn_b2"][l])
    return w



class Ctr:
    __slots__ = ("h", "n")

    def __init__(self, h):
        self.h = h
        self.n = 0

    def inc(self, instr, k=1):
        instr.then_inc(self.h, k)
        self.n += k
        return self.n

    def dma(self, instr):
        return self.inc(instr, 16)

    def w(self, eng, n=None):
        v = self.n if n is None else n
        assert v <= self.n, "wait on not-yet-emitted count"
        if v > 0:
            eng.wait_ge(self.h, v)


class Res:
    __slots__ = ("ve", "sc", "pe", "gp")

    def __init__(self):
        self.ve = self.sc = self.pe = self.gp = 0

    def used(self, ve=None, sc=None, pe=None, gp=None):
        if ve is not None:
            self.ve = max(self.ve, ve)
        if sc is not None:
            self.sc = max(self.sc, sc)
        if pe is not None:
            self.pe = max(self.pe, pe)
        if gp is not None:
            self.gp = max(self.gp, gp)

    def wait(self, b, eng):
        if self.ve:
            b.VE.w(eng, self.ve)
        if self.sc:
            b.SC.w(eng, self.sc)
        if self.pe:
            b.PE.w(eng, self.pe)
        if self.gp:
            b.GP.w(eng, self.gp)


class B:
    pass


class PEProxy:

    def __init__(self, eng):
        self._e = eng
        self.last = None

    def matmul(self, *a, **k):
        self.last = self._e.matmul(*a, **k)
        return self.last

    def transpose(self, *a, **k):
        self.last = self._e.transpose(*a, **k)
        return self.last

    def wait_ge(self, *a, **k):
        return self._e.wait_ge(*a, **k)

    def nop(self, *a, **k):
        return self._e.nop(*a, **k)


def build_nc(T=4096, dbg=(), parts=("A0", "A1", "A4", "B0a", "B0f", "B1a", "B1f"),
             a1_sub="abcde"):
    b = B()
    b.T = T
    b.NT = T // TT
    b.NW = T // WINB
    nc = bass.Bass("TRN2", target_bir_lowering=False,
                   detect_race_conditions=False)
    b.nc = nc
    st = ExitStack()
    b.st = st

    def dram(name, shape, dt, kind="ExternalInput"):
        return nc.dram_tensor(name, list(shape), dt, kind=kind).ap()

    def scratch(name, shape, dt):
        kind = "ExternalOutput" if name in dbg else "Internal"
        return dram(name, shape, dt, kind=kind)

    b.dram = dram
    b.x_h = dram("x", (D, T), F32)
    b.Wq_h = dram("Wq", (D, D), F32)
    b.Wk_h = dram("Wk", (D, D), F32)
    b.Wv_h = dram("Wv", (D, D), F32)
    b.Wo_h = dram("Wo", (D, D), F32)
    b.bq_h = dram("bq", (128, 8), F32)
    b.bk_h = dram("bk", (128, 8), F32)
    b.bv_h = dram("bv", (1, D), F32)
    b.bo_h = dram("bo", (128, 8), F32)
    b.rf_h = dram("rf", (DHA, FTA), F32)
    b.wqkv_h = [dram(f"wqkv{l}", (D, 1536), F32) for l in range(2)]
    b.wol_h = [dram(f"wo{l}", (512, D), F32) for l in range(2)]
    b.qs_h = [dram(f"qs{l}", (128, 1), F32) for l in range(2)]
    b.qsr_h = [dram(f"qsr{l}", (128, 1), F32) for l in range(2)]
    b.ks_h = [dram(f"ks{l}", (128, 1), F32) for l in range(2)]
    b.ksr_h = [dram(f"ksr{l}", (128, 1), F32) for l in range(2)]
    b.lg_h = [dram(f"lg{l}", (1, D), F32) for l in range(2)]
    b.lb_h = [dram(f"lb{l}", (128, 8), F32) for l in range(2)]
    b.w1_h = [dram(f"w1{l}", (D, W1N), F32) for l in range(2)]
    b.b1_h = [dram(f"b1{l}", (128, 44), F32) for l in range(2)]
    b.w2_h = [dram(f"w2{l}", (HIDP, D), F32) for l in range(2)]
    b.b2_h = [dram(f"b2{l}", (128, 8), F32) for l in range(2)]
    b.ident_h = dram("ident", (128, 128), F32)
    b.onesc_h = dram("ones_col", (128, 1), F32)
    b.onesr_h = dram("ones_row", (1, 128), F32)
    b.selq_h = dram("selq", (128, 4, 8), F32)
    b.selbc_h = dram("selbc", (8, 4, 128), F32)
    b.perm_h = dram("perm", (128, 128), F32)
    b.cq_h = dram("cq", (128, T), F32)
    b.sq_h = dram("sq", (128, T), F32)
    b.ck_h = dram("ck", (128, T), F32)
    b.sk_h = dram("sk", (128, T), F32)
    b.mask_h = dram("mask", (128, 5, 640), F32)
    b.xln_d = scratch("xln", (D, T), BF16)
    b.xA_d = scratch("xA", (D, T), F32)
    b.xffn_d = scratch("xffn", (D, T), F32)
    b.kvt_dbg = scratch("kvt", (128, 32 * DHA), F32) if "kvt" in dbg else None
    b.out_h = dram("out", (D, T), F32, kind="ExternalOutput")
    b.x3 = b.x_h.rearrange("(c p) t -> p c t", p=128)
    b.xln3 = b.xln_d.rearrange("(c p) t -> p c t", p=128)
    b.xA3 = b.xA_d.rearrange("(c p) t -> p c t", p=128)
    b.xffn3 = b.xffn_d.rearrange("(c p) t -> p c t", p=128)
    b.out3 = b.out_h.rearrange("(c p) t -> p c t", p=128)

    b.LD = Ctr(st.enter_context(nc.semaphore("ld")))
    b.LS = Ctr(st.enter_context(nc.semaphore("ls")))
    b.PE = Ctr(st.enter_context(nc.semaphore("pe")))
    b.VE = Ctr(st.enter_context(nc.semaphore("ve")))
    b.SC = Ctr(st.enter_context(nc.semaphore("sc")))
    b.GP = Ctr(st.enter_context(nc.semaphore("gp")))
    b.CTRS = [b.LD, b.LS, b.PE, b.VE, b.SC, b.GP]

    def mksem(name):
        c = Ctr(st.enter_context(nc.semaphore(name)))
        b.CTRS.append(c)
        return c

    b.mksem = mksem

    b.g = nc.gpsimd
    b.sy = nc.sync
    b.pe = PEProxy(nc.tensor)
    b.ve = nc.vector
    b.sc = nc.scalar

    def phase_end():
        for c in b.CTRS:
            if c.n:
                nc.sync.wait_ge(c.h, c.n)
        nc.all_engine_barrier()

    b.phase_end = phase_end

    b.ident_b = st.enter_context(nc.sbuf_tensor("ident_b", [128, 128], BF16))
    b.onesc_b = st.enter_context(nc.sbuf_tensor("onesc_b", [128, 1], BF16))
    b.onesr_f = st.enter_context(nc.sbuf_tensor("onesr_f", [1, 128], F32))
    b.selq_b = st.enter_context(nc.sbuf_tensor("selq_b", [128, 4, 8], BF16))
    b.selbc_b = st.enter_context(nc.sbuf_tensor("selbc_b", [8, 4, 128], BF16))
    b.perm_b = st.enter_context(nc.sbuf_tensor("perm_b", [128, 128], BF16))
    b.r1_d = scratch("r1row", (1, T), F32)
    b.m1_d = scratch("m1row", (1, T), F32)
    b.r2_d = scratch("r2row", (1, T), F32)
    b.m2_d = scratch("m2row", (1, T), F32)
    b.rg_d = [scratch(f"rgrow{l}", (1, T), F32) for l in range(2)]
    b.nm_d = [scratch(f"nmrow{l}", (1, T), F32) for l in range(2)]
    b.rows4_d = [scratch(f"rows4_{l}", (1, 4, T), F32) for l in range(2)]


    LD, LS = b.LD, b.LS
    g, sy = b.g, b.sy
    LD.dma(g.dma_start(out=b.ident_b[:, :], in_=b.ident_h[:, :]))
    LD.dma(g.dma_start(out=b.onesc_b[:, :], in_=b.onesc_h[:, :]))
    LD.dma(g.dma_start(out=b.selq_b[:, :, :], in_=b.selq_h[:, :, :]))
    LD.dma(g.dma_start(out=b.perm_b[:, :], in_=b.perm_h[:, :]))
    LS.dma(sy.dma_start(out=b.onesr_f[:, :], in_=b.onesr_h[:, :]))
    LD.dma(g.dma_start(out=b.selbc_b[:, :, :], in_=b.selbc_h[:, :, :]))
    phase_end()

    b.a1_sub = a1_sub
    with ExitStack() as a_scope:
        b.kvt_b = a_scope.enter_context(
            nc.sbuf_tensor("kvt_b", [128, 32, DHA], BF16))
        b.wq_sb = a_scope.enter_context(nc.sbuf_tensor("a4_wq", [128, NCH, D], BF16))
        b.wo_sb = a_scope.enter_context(nc.sbuf_tensor("a4_wo", [128, NCH, D], BF16))
        if "A0" in parts:
            phase_a0(b)
        if "A1" in parts:
            phase_a1(b)
        if "A4" in parts:
            phase_a4(b)
    if "A4" in parts:
        phase_ln2s(b)
    for l in range(2):
        if f"B{l}a" in parts:
            phase_battn(b, l)
        if f"B{l}f" in parts:
            phase_bffn(b, l)
    st.close()
    return nc


def ln_soup(b, stp, s_sb, q2_sb, r_dram, mr_dram, tag, neg=False):
    nc, ve, sc, sy, VE, SC, LS = b.nc, b.ve, b.sc, b.sy, b.VE, b.SC, b.LS
    T = b.T
    t_ = stp.enter_context(nc.sbuf_tensor(f"t_{tag}", [1, T], F32))
    ve.tensor_scalar_mul(s_sb[:, :], s_sb[:, :], 1.0 / D)
    ve.tensor_mul(t_[:, :], s_sb[:, :], s_sb[:, :])
    ve.scalar_tensor_tensor(t_[:, :], q2_sb[:, :], 1.0 / D, t_[:, :],
                            ALU.mult, ALU.subtract)
    vv = VE.inc(ve.tensor_scalar_add(t_[:, :], t_[:, :], 1e-5))
    b.VE.w(sc, vv)
    sc.activation(t_[:, :], t_[:, :], AF.Ln, bias=0.0, scale=1.0)
    s2 = SC.inc(sc.activation(q2_sb[:, :], t_[:, :], AF.Exp, bias=0.0, scale=-0.5))
    b.SC.w(ve, s2)
    if neg:
        vr = VE.inc(ve.scalar_tensor_tensor(s_sb[:, :], s_sb[:, :], -1.0,
                                            q2_sb[:, :], ALU.mult, ALU.mult))
    else:
        vr = VE.inc(ve.tensor_mul(s_sb[:, :], s_sb[:, :], q2_sb[:, :]))
    VE.w(sy, vr)
    LS.dma(sy.dma_start(out=r_dram[:, :], in_=q2_sb[:, :]))
    LS.dma(sy.dma_start(out=mr_dram[:, :], in_=s_sb[:, :]))


def elu1(b, psrc, bcol, m, e_scr, rl_scr, dst, pe_done):
    sc, ve, SC, VE, PE = b.sc, b.ve, b.SC, b.VE, b.PE
    VE.w(sc, VE.n)
    PE.w(sc, pe_done)
    s1 = SC.inc(sc.activation(e_scr[:, :], psrc, AF.Exp,
                              bias=bcol[:, m:m + 1], scale=1.0))
    PE.w(ve, pe_done)
    SC.w(ve, s1)
    ve.tensor_scalar(rl_scr[:, :], psrc, bcol[:, m:m + 1], 0.0, ALU.add, ALU.max)
    v = VE.inc(ve.scalar_tensor_tensor(dst, e_scr[:, :], 1.0, rl_scr[:, :],
                                       ALU.min, ALU.add))
    return s1, v


def phase_a0(b):
    nc, g, sy, pe, ve = b.nc, b.g, b.sy, b.pe, b.ve
    LD, LS, PE, VE = b.LD, b.LS, b.PE, b.VE
    NT, T = b.NT, b.T
    with ExitStack() as stp:
        xb = stp.enter_context(nc.sbuf_tensor("a0_xb", [128, 2, NCH, TT], BF16))
        xsq = stp.enter_context(nc.sbuf_tensor("a0_xsq", [128, NCH, TT], BF16))
        s_sb = stp.enter_context(nc.sbuf_tensor("a0_s", [1, T], F32))
        q2_sb = stp.enter_context(nc.sbuf_tensor("a0_q2", [1, T], F32))
        pr = stp.enter_context(nc.psum_tensor("a0_pr", [1, 1024], F32))
        XSL = [b.mksem("a0x0"), b.mksem("a0x1")]
        pe_rec = [0] * NT
        ve_rec = [0] * NT
        for tt in range(NT + 1):
            if tt < NT:
                if tt >= 2:
                    b.PE.w(g, pe_rec[tt - 2])
                XSL[tt % 2].dma(g.dma_start(
                    out=xb[:, tt % 2, :, :], in_=b.x3[:, :, tt * TT:(tt + 1) * TT]))
            if tt == 0:
                continue
            i = tt - 1
            sl = i % 2
            XSL[sl].w(ve)
            if i >= 1:
                PE.w(ve, pe_rec[i - 1])
            vsq = VE.inc(ve.tensor_mul(xsq[:, :, :], xb[:, sl, :, :], xb[:, sl, :, :]))
            XSL[sl].w(pe)
            VE.w(pe, vsq)
            if i >= 1:
                VE.w(pe, ve_rec[i - 1])
            for c in range(NCH):
                pe.matmul(pr[:, 0:TT], b.onesc_b[:, :], xb[:, sl, c, :],
                          start=(c == 0), stop=(c == NCH - 1))
            for c in range(NCH):
                pe.matmul(pr[:, TT:2 * TT], b.onesc_b[:, :], xsq[:, c, :],
                          start=(c == 0), stop=(c == NCH - 1))
            pe_rec[i] = PE.inc(pe.last)
            PE.w(ve, pe_rec[i])
            ve.tensor_copy(s_sb[:, i * TT:(i + 1) * TT], pr[:, 0:TT])
            ve_rec[i] = VE.inc(ve.tensor_copy(q2_sb[:, i * TT:(i + 1) * TT],
                                              pr[:, TT:2 * TT]))
        ln_soup(b, stp, s_sb, q2_sb, b.r1_d, b.m1_d, "ln1")
        b.phase_end()


def phase_a1(b):
    nc, g, sy, pe, ve, sc = b.nc, b.g, b.sy, b.pe, b.ve, b.sc
    LD, LS, PE, VE, SC, GP = b.LD, b.LS, b.PE, b.VE, b.SC, b.GP
    NT, T = b.NT, b.T
    with ExitStack() as stp:
        wk_sb = stp.enter_context(nc.sbuf_tensor("a1_wk", [128, NCH, D], BF16))
        wv_sb = stp.enter_context(nc.sbuf_tensor("a1_wv", [128, NCH, D], BF16))
        rf_sb = stp.enter_context(nc.sbuf_tensor("a1_rf", [128, FTA], BF16))
        bkc = stp.enter_context(nc.sbuf_tensor("a1_bkc", [128, 8], F32))
        bvbc = stp.enter_context(nc.sbuf_tensor("a1_bvbc", [128, D], F32))
        xb = stp.enter_context(nc.sbuf_tensor("a1_xb", [128, 2, NCH, TT], BF16))
        xln = stp.enter_context(nc.sbuf_tensor("a1_xln", [128, 2, NCH, TT], BF16))
        kel = stp.enter_context(nc.sbuf_tensor("a1_kel", [128, NCH, TT], BF16))
        e_scr = stp.enter_context(nc.sbuf_tensor("a1_e", [128, TT], F32))
        rl_scr = stp.enter_context(nc.sbuf_tensor("a1_rl", [128, TT], F32))
        v_tile = stp.enter_context(nc.sbuf_tensor("a1_v", [128, 4, D], BF16))
        kp_sb = stp.enter_context(nc.sbuf_tensor("a1_kp", [128, 2, 2, 256], BF16))
        kvacc = stp.enter_context(nc.sbuf_tensor("a1_kva", [128, 4, TT], F32))
        rowb = stp.enter_context(nc.sbuf_tensor("a1_rowb", [1, 2, 2, TT], F32))
        pA = stp.enter_context(nc.psum_tensor("a1_pA", [128, 1024], F32))
        pkp = stp.enter_context(nc.psum_tensor("a1_pkp", [128, 1024], F32))
        pkv = stp.enter_context(nc.psum_tensor("a1_pkv", [128, 2048], F32))

        LD.dma(g.dma_start(out=wk_sb[:, :, :],
                           in_=b.Wk_h.rearrange("(c p) m -> p c m", p=128)))
        LD.dma(g.dma_start(out=wv_sb[:, :, :],
                           in_=b.Wv_h.rearrange("(c p) m -> p c m", p=128)))
        LD.dma(g.dma_start(out=rf_sb[0:64, :], in_=b.rf_h[:, :]))
        LD.dma(g.dma_start(out=rf_sb[64:128, :], in_=b.rf_h[:, :]))
        LS.dma(sy.dma_start(out=bkc[:, :], in_=b.bk_h[:, :]))
        LS.dma(sy.dma_start(out=bvbc[:, :],
                            in_=b.bv_h.partition_broadcast(128)[:, 0, :]))
        GP.inc(g.memset(kvacc[:, :, :], 0.0))
        LD.dma(g.dma_start(out=b.wq_sb[:, :, :],
                           in_=b.Wq_h.rearrange("(c p) m -> p c m", p=128)))
        LD.dma(g.dma_start(out=b.wo_sb[:, :, :],
                           in_=b.Wo_h.rearrange("(c p) m -> p c m", p=128)))
        wk_ld = LD.n
        c_ld = LS.n
        gz = GP.n

        XSL = [b.mksem("a1x0"), b.mksem("a1x1")]
        RSL = [b.mksem("a1r0"), b.mksem("a1r1")]
        SSL = [b.mksem("a1s0"), b.mksem("a1s1")]
        pa = [Res(), Res()]
        pkp_r = Res()
        kpr = [Res(), Res()]
        pkv_r = Res()
        kelr = Res()
        vtr = Res()
        xbr = [Res(), Res()]
        rowr = [Res(), Res()]
        xlnr = [Res(), Res()]
        for tt in range(NT + 1):
            if tt < NT:
                s2 = tt % 2
                xbr[s2].wait(b, g)
                XSL[s2].dma(g.dma_start(
                    out=xb[:, s2, :, :], in_=b.x3[:, :, tt * TT:(tt + 1) * TT]))
                rowr[s2].wait(b, sy)
                RSL[s2].dma(sy.dma_start(out=rowb[:, s2, 0, :],
                                         in_=b.r1_d[:, tt * TT:(tt + 1) * TT]))
                RSL[s2].dma(sy.dma_start(out=rowb[:, s2, 1, :],
                                         in_=b.m1_d[:, tt * TT:(tt + 1) * TT]))
            if tt == 0:
                continue
            i = tt - 1
            sl = i % 2
            pa[0].wait(b, pe)
            pa[1].wait(b, pe)
            RSL[sl].w(pe)
            LS.w(pe)
            pe.matmul(pA[:, 0:TT], b.onesr_f[:, :], rowb[:, sl, 0, :],
                      start=True, stop=True)
            mm_bc = PE.inc(pe.matmul(pA[:, TT:2 * TT], b.onesr_f[:, :],
                                     rowb[:, sl, 1, :], start=True, stop=True))
            rowr[sl].used(pe=mm_bc)
            PE.w(ve, mm_bc)
            XSL[sl].w(ve)
            SSL[sl].w(ve)
            xlnr[sl].wait(b, ve)
            ve.tensor_tensor(xln[:, sl, :, :], xb[:, sl, :, :],
                             pA[:, 0:TT].unsqueeze(1).broadcast_to([128, NCH, TT]),
                             ALU.mult)
            v_xln = VE.inc(ve.tensor_tensor(
                xln[:, sl, :, :], xln[:, sl, :, :],
                pA[:, TT:2 * TT].unsqueeze(1).broadcast_to([128, NCH, TT]),
                ALU.subtract))
            xbr[sl].used(ve=v_xln)
            pa[0].used(ve=v_xln)
            pa[1].used(ve=v_xln)
            VE.w(sy, v_xln)
            SSL[sl].dma(sy.dma_start(out=b.xln3[:, :, i * TT:(i + 1) * TT],
                                     in_=xln[:, sl, :, :]))
            LD.w(pe)
            PE.w(ve, mm_bc)
            kelr.wait(b, ve)
            kelr.wait(b, sc)
            ve_kel = [0] * NCH
            for m in range(NCH):
                bk_ = m % 2
                pb = bk_ * TT
                pa[bk_].wait(b, pe)
                VE.w(pe, v_xln)
                for k in range(NCH):
                    pe.matmul(pA[:, pb:pb + TT], wk_sb[:, k, m * 128:(m + 1) * 128],
                              xln[:, sl, k, :], start=(k == 0), stop=(k == NCH - 1))
                pd = PE.inc(pe.last)
                s1, vk = elu1(b, pA[:, pb:pb + TT], bkc, m, e_scr, rl_scr,
                              kel[:, m, :], pd)
                ve_kel[m] = vk
                pa[bk_].used(ve=vk, sc=s1)
            vtr.wait(b, ve)
            for tc in range(4):
                for mh in range(2):
                    j = tc * 2 + mh
                    bk_ = j % 2
                    pb = bk_ * 512
                    pa[bk_].wait(b, pe)
                    for k in range(NCH):
                        pe.matmul(pA[:, pb:pb + 512],
                                  xln[:, sl, k, tc * 128:(tc + 1) * 128],
                                  wv_sb[:, k, mh * 512:(mh + 1) * 512],
                                  start=(k == 0), stop=(k == NCH - 1))
                    pev = PE.inc(pe.last)
                    PE.w(ve, pev)
                    v_v = VE.inc(ve.tensor_tensor(
                        v_tile[:, tc, mh * 512:(mh + 1) * 512], pA[:, pb:pb + 512],
                        bvbc[:, mh * 512:(mh + 1) * 512], ALU.add))
                    pa[bk_].used(ve=v_v)
            xlnr[sl].used(pe=PE.n)
            for tc in range(4):
                pend = None
                for hp in range(8):
                    slot = hp % 2
                    if slot == 0:
                        tgt, rl = pkp, [pkp_r]
                    else:
                        tgt, rl = pA, [pa[0], pa[1]]
                    for r in rl:
                        r.wait(b, pe)
                    VE.w(pe, ve_kel[NCH - 1])
                    kpr[slot].wait(b, ve)
                    for par in range(2):
                        h = hp * 2 + par
                        r0 = par * 64
                        pe.matmul(tgt[:, par * 512:par * 512 + 256],
                                  kel[r0:r0 + 64, h // 2, tc * 128:(tc + 1) * 128],
                                  rf_sb[r0:r0 + 64, :], start=True, stop=True)
                    pekp = PE.inc(pe.last)
                    PE.w(ve, pekp)
                    v_kp = VE.inc(ve.tensor_copy(
                        kp_sb[:, slot, :, :],
                        tgt.rearrange("p (a n) -> p a n", n=512)[:, :, 0:256]))
                    for r in rl:
                        r.used(ve=v_kp)
                    if pend is not None:
                        php, pvkp = pend
                        VE.w(pe, pvkp)
                        pkv_r.wait(b, pe)
                        for par in range(2):
                            h = php * 2 + par
                            for fc in range(2):
                                s = h * 2 + fc
                                pe.matmul(pkv[:, s * 64:(s + 1) * 64],
                                          kp_sb[:, php % 2, par,
                                                fc * 128:(fc + 1) * 128],
                                          v_tile[:, tc, h * 64:(h + 1) * 64],
                                          start=True, stop=True)
                        kpr[php % 2].used(pe=PE.inc(pe.last))
                    pend = (hp, v_kp)
                php, pvkp = pend
                VE.w(pe, pvkp)
                pkv_r.wait(b, pe)
                for par in range(2):
                    h = php * 2 + par
                    for fc in range(2):
                        s = h * 2 + fc
                        pe.matmul(pkv[:, s * 64:(s + 1) * 64],
                                  kp_sb[:, php % 2, par, fc * 128:(fc + 1) * 128],
                                  v_tile[:, tc, h * 64:(h + 1) * 64],
                                  start=True, stop=True)
                pkvd = PE.inc(pe.last)
                kpr[php % 2].used(pe=pkvd)
                vtr.used(pe=pkvd)
                kelr.used(pe=pkvd)
                PE.w(ve, pkvd)
                GP.w(ve, gz)
                v_kva = VE.inc(ve.tensor_tensor(
                    kvacc[:, :, :].rearrange("p a n -> p (a n)"), pkv[:, :],
                    kvacc[:, :, :].rearrange("p a n -> p (a n)"), ALU.add))
                pkv_r.used(ve=v_kva)
        vkf = VE.inc(ve.tensor_copy(b.kvt_b[:, :, :].rearrange("p a n -> p (a n)"),
                                    kvacc[:, :, :].rearrange("p a n -> p (a n)")))
        if b.kvt_dbg is not None:
            VE.w(sy, vkf)
            LS.dma(sy.dma_start(out=b.kvt_dbg[:, :],
                                in_=kvacc[:, :, :].rearrange("p a n -> p (a n)")))
        b.phase_end()


def phase_a4(b):
    nc, g, sy, pe, ve, sc = b.nc, b.g, b.sy, b.pe, b.ve, b.sc
    LD, LS, PE, VE, SC = b.LD, b.LS, b.PE, b.VE, b.SC
    NT, T = b.NT, b.T
    with ExitStack() as stp:
        wq_sb = b.wq_sb
        wo_sb = b.wo_sb
        rf_sb = stp.enter_context(nc.sbuf_tensor("a4_rf", [128, FTA], BF16))
        bqc = stp.enter_context(nc.sbuf_tensor("a4_bqc", [128, 8], F32))
        boc = stp.enter_context(nc.sbuf_tensor("a4_boc", [128, 8], F32))
        xlt = stp.enter_context(nc.sbuf_tensor("a4_xlt", [128, 2, NCH, TT], BF16))
        xt = stp.enter_context(nc.sbuf_tensor("a4_xt", [128, NCH, TT], F32))
        qel = stp.enter_context(nc.sbuf_tensor("a4_qel", [128, NCH, TT], BF16))
        e_scr = stp.enter_context(nc.sbuf_tensor("a4_e", [128, TT], F32))
        rl_scr = stp.enter_context(nc.sbuf_tensor("a4_rl", [128, TT], F32))
        qp_sb = stp.enter_context(nc.sbuf_tensor("a4_qp", [128, 2, 2, TT], BF16))
        afm = stp.enter_context(nc.sbuf_tensor("a4_afm", [128, NCH, TT], BF16))
        xA_t = stp.enter_context(nc.sbuf_tensor("a4_xA", [128, NCH, TT], F32))
        xAb = stp.enter_context(nc.sbuf_tensor("a4_xAb", [128, NCH, TT], BF16))
        xAsq = stp.enter_context(nc.sbuf_tensor("a4_xAsq", [128, NCH, TT], BF16))
        s_sb = stp.enter_context(nc.sbuf_tensor("a4_s", [1, T], F32))
        q2_sb = stp.enter_context(nc.sbuf_tensor("a4_q2", [1, T], F32))
        pg = stp.enter_context(nc.psum_tensor("a4_pg", [128, 1024], F32))
        pqp = stp.enter_context(nc.psum_tensor("a4_pqp", [128, 1024], F32))
        pout = stp.enter_context(nc.psum_tensor("a4_pout", [128, 1024], F32))
        prow = stp.enter_context(nc.psum_tensor("a4_prow", [1, 1024], F32))

        LD.dma(g.dma_start(out=rf_sb[0:64, :], in_=b.rf_h[:, :]))
        LD.dma(g.dma_start(out=rf_sb[64:128, :], in_=b.rf_h[:, :]))
        LS.dma(sy.dma_start(out=bqc[:, :], in_=b.bq_h[:, :]))
        LS.dma(sy.dma_start(out=boc[:, :], in_=b.bo_h[:, :]))
        w_ld, c_ld = LD.n, LS.n

        XSL = [b.mksem("a4x0"), b.mksem("a4x1")]
        XTS = b.mksem("a4xt")
        SAS = b.mksem("a4sa")
        b.v_afm_last = 0
        pg_b = [Res(), Res()]
        pqp_r = Res()
        pout_b = [Res(), Res()]
        prow_r = Res()
        qpr = [Res(), Res()]
        qelr = Res()
        afmr = Res()
        xltr = [Res(), Res()]
        xtr = Res()
        for tt in range(NT + 1):
            if tt < NT:
                s2 = tt % 2
                xltr[s2].wait(b, sy)
                XSL[s2].dma(sy.dma_start(
                    out=xlt[:, s2, :, :],
                    in_=b.xln3[:, :, tt * TT:(tt + 1) * TT]))
            if tt == 0:
                continue
            i = tt - 1
            sl = i % 2
            xtr.wait(b, sy)
            SAS.w(sy)
            XTS.dma(sy.dma_start(out=xt[:, :, :],
                                 in_=b.x3[:, :, i * TT:(i + 1) * TT]))
            LD.w(pe)
            XSL[sl].w(pe)
            qelr.wait(b, ve)
            qelr.wait(b, sc)
            ve_qel = [0] * NCH
            for m in range(NCH):
                bk_ = m % 2
                pb = bk_ * TT
                pg_b[bk_].wait(b, pe)
                for k in range(NCH):
                    pe.matmul(pg[:, pb:pb + TT], wq_sb[:, k, m * 128:(m + 1) * 128],
                              xlt[:, sl, k, :], start=(k == 0), stop=(k == NCH - 1))
                pd = PE.inc(pe.last)
                s1, vq = elu1(b, pg[:, pb:pb + TT], bqc, m, e_scr, rl_scr,
                              qel[:, m, :], pd)
                ve_qel[m] = vq
                pg_b[bk_].used(ve=vq, sc=s1)
            xltr[sl].used(pe=PE.n)
            def emit_out(pd_, last=False):
                h_, v_qp_ = pd_
                hp_, par_ = h_ // 2, h_ % 2
                r0_ = par_ * 64
                ob_ = (hp_ % 2) * TT
                VE.w(pe, v_qp_)
                if par_ == 0:
                    pout_b[hp_ % 2].wait(b, pe)
                for fc_ in range(2):
                    s_ = h_ * 2 + fc_
                    pe.matmul(pout[r0_:r0_ + 64, ob_:ob_ + TT], b.kvt_b[:, s_, :],
                              qp_sb[:, h_ % 2, fc_, :], start=(fc_ == 0),
                              stop=(fc_ == 1), tile_position=(0, r0_))
                cnt_ = PE.inc(pe.last)
                qpr[h_ % 2].used(pe=cnt_)
                if par_ == 1:
                    PE.w(ve, cnt_)
                    v_ = VE.inc(ve.tensor_copy(afm[:, hp_, :], pout[:, ob_:ob_ + TT]))
                    pout_b[hp_ % 2].used(ve=v_)
                    b.v_afm_last = v_

            afmr.wait(b, ve)
            pend = None
            for h in range(HA):
                hp = h // 2
                par = h % 2
                r0 = par * 64
                slot = h % 2
                if slot == 0:
                    tgt, rl = pqp, [pqp_r]
                else:
                    tgt, rl = pg, [pg_b[0], pg_b[1]]
                for r in rl:
                    r.wait(b, pe)
                VE.w(pe, ve_qel[NCH - 1])
                qpr[slot].wait(b, ve)
                for fc in range(2):
                    pe.matmul(tgt[:, fc * TT:(fc + 1) * TT],
                              rf_sb[r0:r0 + 64, fc * 128:(fc + 1) * 128],
                              qel[r0:r0 + 64, h // 2, :], start=True, stop=True)
                pq = PE.inc(pe.last)
                PE.w(ve, pq)
                v_qp = VE.inc(ve.tensor_copy(
                    qp_sb[:, slot, :, :],
                    tgt.rearrange("p (a n) -> p a n", n=TT)))
                for r in rl:
                    r.used(ve=v_qp)
                if pend is not None:
                    emit_out(pend)
                pend = (h, v_qp)
            emit_out(pend, last=True)
            v_xa = 0
            for m in range(NCH):
                bk_ = m % 2
                pb = bk_ * TT
                pg_b[bk_].wait(b, pe)
                VE.w(pe, b.v_afm_last)
                for k in range(NCH):
                    pe.matmul(pg[:, pb:pb + TT], wo_sb[:, k, m * 128:(m + 1) * 128],
                              afm[:, k, :], start=(k == 0), stop=(k == NCH - 1))
                pwo = PE.inc(pe.last)
                PE.w(ve, pwo)
                XTS.w(ve)
                LS.w(ve)
                v_xa = VE.inc(ve.scalar_tensor_tensor(
                    xA_t[:, m, :], pg[:, pb:pb + TT], boc[:, m:m + 1],
                    xt[:, m, :], ALU.add, ALU.add))
                pg_b[bk_].used(ve=v_xa)
            afmr.used(pe=PE.n)
            xtr.used(ve=v_xa)
            vc = VE.inc(ve.tensor_copy(xAb[:, :, :], xA_t[:, :, :]))
            vsq = VE.inc(ve.tensor_mul(xAsq[:, :, :], xAb[:, :, :], xAb[:, :, :]))
            VE.w(sy, v_xa)
            SAS.dma(sy.dma_start(out=b.xA3[:, :, i * TT:(i + 1) * TT],
                                 in_=xA_t[:, :, :]))
            SAS.dma(sy.dma_start(out=b.out3[:, :, i * TT:(i + 1) * TT],
                                 in_=xA_t[:, :, :]))
            VE.w(pe, vsq)
            prow_r.wait(b, pe)
            for c in range(NCH):
                pe.matmul(prow[:, 0:TT], b.onesc_b[:, :], xAb[:, c, :],
                          start=(c == 0), stop=(c == NCH - 1))
            for c in range(NCH):
                pe.matmul(prow[:, TT:2 * TT], b.onesc_b[:, :], xAsq[:, c, :],
                          start=(c == 0), stop=(c == NCH - 1))
            pr = PE.inc(pe.last)
            PE.w(ve, pr)
            ve.tensor_copy(s_sb[:, i * TT:(i + 1) * TT], prow[:, 0:TT])
            vrow = VE.inc(ve.tensor_copy(q2_sb[:, i * TT:(i + 1) * TT],
                                         prow[:, TT:2 * TT]))
            prow_r.used(ve=vrow)
        ln_soup(b, stp, s_sb, q2_sb, b.r2_d, b.m2_d, "ln2")
        b.phase_end()


def phase_ln2s(b):
    nc, g, sy, pe, ve, sc = b.nc, b.g, b.sy, b.pe, b.ve, b.sc
    LD, LS, PE, VE, SC, GP = b.LD, b.LS, b.PE, b.VE, b.SC, b.GP
    NT, T = b.NT, b.T
    with ExitStack() as stp:
        xt = stp.enter_context(nc.sbuf_tensor("l2_xt", [128, 2, NCH, TT], F32))
        xo = stp.enter_context(nc.sbuf_tensor("l2_xo", [128, 2, NCH, TT], F32))
        rowb = stp.enter_context(nc.sbuf_tensor("l2_row", [1, 2, 2, TT], F32))
        pA = stp.enter_context(nc.psum_tensor("l2_pA", [128, 1024], F32))
        XSL = [b.mksem("l2x0"), b.mksem("l2x1")]
        RSL = [b.mksem("l2r0"), b.mksem("l2r1")]
        SSL = [b.mksem("l2s0"), b.mksem("l2s1")]
        xtr = [Res(), Res()]
        xor_ = [Res(), Res()]
        rowr = [Res(), Res()]
        pa = [Res(), Res()]
        for tt in range(NT + 1):
            if tt < NT:
                s2 = tt % 2
                xtr[s2].wait(b, sy)
                XSL[s2].dma(sy.dma_start(
                    out=xt[:, s2, :, :], in_=b.xA3[:, :, tt * TT:(tt + 1) * TT]))
                rowr[s2].wait(b, sy)
                RSL[s2].dma(sy.dma_start(out=rowb[:, s2, 0, :],
                                         in_=b.r2_d[:, tt * TT:(tt + 1) * TT]))
                RSL[s2].dma(sy.dma_start(out=rowb[:, s2, 1, :],
                                         in_=b.m2_d[:, tt * TT:(tt + 1) * TT]))
            if tt == 0:
                continue
            i = tt - 1
            sl = i % 2
            pa[0].wait(b, pe)
            pa[1].wait(b, pe)
            RSL[sl].w(pe)
            pe.matmul(pA[:, 0:TT], b.onesr_f[:, :], rowb[:, sl, 0, :],
                      start=True, stop=True)
            mm_bc = PE.inc(pe.matmul(pA[:, TT:2 * TT], b.onesr_f[:, :],
                                     rowb[:, sl, 1, :], start=True, stop=True))
            rowr[sl].used(pe=mm_bc)
            PE.w(ve, mm_bc)
            XSL[sl].w(ve)
            SSL[sl].w(ve)
            xor_[sl].wait(b, ve)
            ve.tensor_tensor(xo[:, sl, :, :], xt[:, sl, :, :],
                             pA[:, 0:TT].unsqueeze(1).broadcast_to([128, NCH, TT]),
                             ALU.mult)
            v_x = VE.inc(ve.tensor_tensor(
                xo[:, sl, :, :], xo[:, sl, :, :],
                pA[:, TT:2 * TT].unsqueeze(1).broadcast_to([128, NCH, TT]),
                ALU.subtract))
            xtr[sl].used(ve=v_x)
            pa[0].used(ve=v_x)
            pa[1].used(ve=v_x)
            VE.w(sy, v_x)
            SSL[sl].dma(sy.dma_start(out=b.xffn3[:, :, i * TT:(i + 1) * TT],
                                     in_=xo[:, sl, :, :]))
        b.phase_end()


def phase_battn(b, l):
    nc, g, sy, pe, ve, sc = b.nc, b.g, b.sy, b.pe, b.ve, b.sc
    LD, LS, PE, VE, SC, GP = b.LD, b.LS, b.PE, b.VE, b.SC, b.GP
    NW, T = b.NW, b.T
    with ExitStack() as stp:
        wqkv = stp.enter_context(nc.sbuf_tensor(f"b{l}_wqkv", [128, NCH, 1536], BF16))
        wo = stp.enter_context(nc.sbuf_tensor(f"b{l}_wo", [128, 4, D], BF16))
        cq = stp.enter_context(nc.sbuf_tensor(f"b{l}_cq", [128, T], BF16))
        sq = stp.enter_context(nc.sbuf_tensor(f"b{l}_sq", [128, T], BF16))
        ck = stp.enter_context(nc.sbuf_tensor(f"b{l}_ck", [128, T], BF16))
        sk = stp.enter_context(nc.sbuf_tensor(f"b{l}_sk", [128, T], BF16))
        qsc = stp.enter_context(nc.sbuf_tensor(f"b{l}_qsc", [128, 1], F32))
        qsrc = stp.enter_context(nc.sbuf_tensor(f"b{l}_qsrc", [128, 1], F32))
        ksc = stp.enter_context(nc.sbuf_tensor(f"b{l}_ksc", [128, 1], F32))
        ksrc = stp.enter_context(nc.sbuf_tensor(f"b{l}_ksrc", [128, 1], F32))
        xft = stp.enter_context(nc.sbuf_tensor(f"b{l}_xft",
                                               [128, 2, NCH, WINB + 1], F32))
        z = stp.enter_context(nc.sbuf_tensor(f"b{l}_z", [128, 2, NCH, WINB], BF16))
        qraw = stp.enter_context(nc.sbuf_tensor(f"b{l}_qraw", [128, 4, WINB], BF16))
        kraw = stp.enter_context(nc.sbuf_tensor(f"b{l}_kraw", [128, 4, WINB], BF16))
        qrot = stp.enter_context(nc.sbuf_tensor(f"b{l}_qrot", [128, 4, WINB], BF16))
        sqs = qrot
        krr = stp.enter_context(nc.sbuf_tensor(f"b{l}_krr", [128, 2, 4, WINB], BF16))
        vrr = stp.enter_context(nc.sbuf_tensor(f"b{l}_vrr", [128, 3, 4, WINB], BF16))
        u1 = stp.enter_context(nc.sbuf_tensor(f"b{l}_u1", [128, 2, WINB], BF16))
        rsq = stp.enter_context(nc.sbuf_tensor(f"b{l}_rsq", [8, WINB], F32))
        rsk = stp.enter_context(nc.sbuf_tensor(f"b{l}_rsk", [8, WINB], F32))
        rsqb = stp.enter_context(nc.sbuf_tensor(f"b{l}_rsqb", [8, WINB], BF16))
        rskb = stp.enter_context(nc.sbuf_tensor(f"b{l}_rskb", [8, WINB], BF16))
        row4 = stp.enter_context(nc.sbuf_tensor(f"b{l}_row4", [1, 4, WINB], F32))
        mask_sb = stp.enter_context(nc.sbuf_tensor(f"b{l}_mask", [128, 1, 640], BF16))
        ae = stp.enter_context(nc.sbuf_tensor(f"b{l}_ae", [128, 4, 640], BF16))
        am = stp.enter_context(nc.sbuf_tensor(f"b{l}_am", [128, 4, 640], BF16))
        am2 = stp.enter_context(nc.sbuf_tensor(f"b{l}_am2", [128, 4, 640], BF16))
        nmx = stp.enter_context(nc.sbuf_tensor(f"b{l}_nmx", [128, 4], F32))
        zsum = stp.enter_context(nc.sbuf_tensor(f"b{l}_zs", [128, 4], F32))
        rz = stp.enter_context(nc.sbuf_tensor(f"b{l}_rz", [128, 4], F32))
        atT = stp.enter_context(nc.sbuf_tensor(f"b{l}_atT", [128, NCH, WINB], BF16))
        ofm = stp.enter_context(nc.sbuf_tensor(f"b{l}_ofm", [128, 4, WINB], BF16))
        xout = stp.enter_context(nc.sbuf_tensor(f"b{l}_xout", [128, NCH, WINB], F32))
        PA = stp.enter_context(nc.psum_tensor(f"b{l}_PA", [128, 1024], F32))
        PB = stp.enter_context(nc.psum_tensor(f"b{l}_PB", [128, 1024], F32))
        PC = stp.enter_context(nc.psum_tensor(f"b{l}_PC", [128, 1024], F32))
        PT = stp.enter_context(nc.psum_tensor(f"b{l}_PT", [128, 1024], BF16))
        PT2 = stp.enter_context(nc.psum_tensor(f"b{l}_PT2", [128, 1024], BF16))

        LD.dma(g.dma_start(out=wqkv[:, :, :],
                           in_=b.wqkv_h[l].rearrange("(c p) m -> p c m", p=128)))
        LD.dma(g.dma_start(out=wo[:, :, :],
                           in_=b.wol_h[l].rearrange("(c p) m -> p c m", p=128)))
        LD.dma(g.dma_start(out=mask_sb[:, 0, :], in_=b.mask_h[:, 4, :]))
        LD.dma(g.dma_start(out=cq[:, :], in_=b.cq_h[:, :]))
        LD.dma(g.dma_start(out=sq[:, :], in_=b.sq_h[:, :]))
        LD.dma(g.dma_start(out=ck[:, :], in_=b.ck_h[:, :]))
        LD.dma(g.dma_start(out=sk[:, :], in_=b.sk_h[:, :]))
        LS.dma(sy.dma_start(out=qsc[:, :], in_=b.qs_h[l][:, :]))
        LS.dma(sy.dma_start(out=qsrc[:, :], in_=b.qsr_h[l][:, :]))
        LS.dma(sy.dma_start(out=ksc[:, :], in_=b.ks_h[l][:, :]))
        LS.dma(sy.dma_start(out=ksrc[:, :], in_=b.ksr_h[l][:, :]))
        GP.inc(g.memset(krr[:, 1, :, :], 0.0))
        GP.inc(g.memset(vrr[:, 2, :, :], 0.0))
        GP.inc(g.memset(xft[:, :, :, 0:1], 0.0))
        GP.inc(g.memset(z[:, 0, 4:8, 0:1], 0.0))
        w_ld, c_ld, g_ms = LD.n, LS.n, GP.n
        LD.w(pe)
        LS.w(pe)
        GP.w(pe, g_ms)
        GP.w(ve, g_ms)
        GP.w(sy, g_ms)
        paB = [Res(), Res()]
        pbB = [Res(), Res()]
        pcB = [Res(), Res()]
        ptR = [Res(), Res()]
        atTr = Res()
        zR = [Res(), Res()]
        vrrR = [Res(), Res(), Res()]
        qkr = Res()
        xftR = [Res(), Res()]
        aer = [Res(), Res(), Res(), Res()]
        XS2 = [b.mksem(f"b{l}x0"), b.mksem(f"b{l}x1")]
        ZS2 = [b.mksem(f"b{l}z0"), b.mksem(f"b{l}z1")]
        wst = {}

        def head_tasks(w2):
            if w2 >= NW:
                return []
            t02 = w2 * WINB
            sl2 = w2 % 2
            st2 = wst.setdefault(w2, {})

            def t_load():
                xftR[sl2].wait(b, sy)
                if w2 == 0:
                    XS2[sl2].dma(sy.dma_start(out=xft[:, sl2, :, 1:WINB + 1],
                                              in_=b.xffn3[:, :, 0:WINB]))
                else:
                    XS2[sl2].dma(sy.dma_start(
                        out=xft[:, sl2, :, :],
                        in_=b.xffn3[:, :, t02 - 1:t02 + WINB]))

            def t_z():
                zR[sl2].wait(b, g)
                if w2 == 0:
                    ZS2[sl2].dma(g.dma_start(out=z[:, sl2, 0:4, :],
                                             in_=b.xffn3[:, 0:4, 0:WINB]))
                    ZS2[sl2].dma(g.dma_start(out=z[:, sl2, 4:8, 1:WINB],
                                             in_=b.xffn3[:, 4:8, 0:WINB - 1]))
                else:
                    ZS2[sl2].dma(g.dma_start(out=z[:, sl2, 0:4, :],
                                             in_=b.xffn3[:, 0:4, t02:t02 + WINB]))
                    ZS2[sl2].dma(g.dma_start(
                        out=z[:, sl2, 4:8, :],
                        in_=b.xffn3[:, 4:8, t02 - 1:t02 - 1 + WINB]))

            def mk_qkv(m):
                def t_qkv():
                    bk_ = m % 2
                    pb = bk_ * 512
                    pbB[bk_].wait(b, pe)
                    ZS2[sl2].w(pe)
                    for k in range(NCH):
                        pe.matmul(PB[:, pb:pb + 512],
                                  wqkv[:, k, m * 128:(m + 1) * 128],
                                  z[:, sl2, k, :], start=(k == 0),
                                  stop=(k == NCH - 1))
                    pq = PE.inc(pe.last)
                    dst = qraw[:, m, :] if m < 4 else kraw[:, m - 4, :]
                    if m % 2 == 0:
                        PE.w(sc, pq)
                        qkr.wait(b, sc)
                        pbB[bk_].used(sc=SC.inc(sc.copy(dst, PB[:, pb:pb + 512])))
                    else:
                        PE.w(ve, pq)
                        qkr.wait(b, ve)
                        pbB[bk_].used(ve=VE.inc(ve.tensor_copy(
                            dst, PB[:, pb:pb + 512])))
                return t_qkv

            def mk_v(tc):
                def t_v():
                    bk_ = tc % 2
                    pb = bk_ * 512
                    pbB[bk_].wait(b, pe)
                    vrrR[w2 % 3].wait(b, pe)
                    ZS2[sl2].w(pe)
                    for k in range(NCH):
                        pe.matmul(PB[:, pb:pb + 512],
                                  z[:, sl2, k, tc * 128:(tc + 1) * 128],
                                  wqkv[:, k, 1024:1536], start=(k == 0),
                                  stop=(k == NCH - 1))
                    pv = PE.inc(pe.last)
                    if tc % 2 == 0:
                        PE.w(sc, pv)
                        pbB[bk_].used(sc=SC.inc(sc.copy(
                            vrr[:, w2 % 3, tc, :], PB[:, pb:pb + 512])))
                    else:
                        PE.w(ve, pv)
                        pbB[bk_].used(ve=VE.inc(ve.tensor_copy(
                            vrr[:, w2 % 3, tc, :], PB[:, pb:pb + 512])))
                return t_v

            def t_done():
                st2["evd_sc"] = SC.n
                st2["evd_ve"] = VE.n
                zR[sl2].used(pe=PE.n)

            return ([t_load, t_z] + [mk_qkv(m) for m in range(8)] +
                    [mk_v(tc) for tc in range(4)] + [t_done])

        for t in head_tasks(0):
            t()

        for w in range(NW):
            t0 = w * WINB
            wsl = w % 2
            psl = 1 - wsl
            vsl = w % 3
            pvsl = (w - 1) % 3 if w > 0 else 2
            st = wst[w]
            res_t = xft
            SC.w(ve, st["evd_sc"])
            for raw, rr, rrb in ((qraw, rsq, rsqb), (kraw, rsk, rskb)):
                vsq = VE.inc(ve.tensor_mul(sqs[:, :, :], raw[:, :, :], raw[:, :, :]))
                VE.w(pe, vsq)
                pcB[0].wait(b, pe)
                for c in range(4):
                    pe.matmul(PC[0:8, 0:512], b.selq_b[:, c, :], sqs[:, c, :],
                              start=(c == 0), stop=(c == 3))
                pn = PE.inc(pe.last)
                PE.w(ve, pn)
                veps = VE.inc(ve.tensor_scalar_add(rr[:, :], PC[0:8, 0:512], 1e-30))
                pcB[0].used(ve=veps)
                VE.w(sc, veps)
                sc.activation(rr[:, :], rr[:, :], AF.Ln, bias=0.0, scale=1.0)
                s2 = SC.inc(sc.activation(rr[:, :], rr[:, :], AF.Exp, bias=0.0,
                                          scale=-0.5))
                SC.w(ve, s2)
                VE.inc(ve.tensor_scalar_min(rrb[:, :], rr[:, :], 1e12))
            qk_pe = 0
            for ti, (raw, rot_q, C_t, S_t, scol, srcol, rrb) in enumerate((
                    (qraw, True, cq, sq, qsc, qsrc, rsqb),
                    (kraw, False, ck, sk, ksc, ksrc, rskb))):
                for c in range(4):
                    bk_ = (ti * 4 + c) % 2
                    pbB[bk_].wait(b, pe)
                    pcB[bk_].wait(b, pe)
                    pe.matmul(PB[:, bk_ * 512:bk_ * 512 + 512], b.perm_b[:, :],
                              raw[:, c, :], start=True, stop=True)
                    pe.matmul(PC[:, bk_ * 512:bk_ * 512 + 512], b.selbc_b[:, c, :],
                              rrb[:, :], start=True, stop=True)
                    pr2 = PE.inc(pe.last)
                    qk_pe = pr2
                    PE.w(ve, pr2)
                    ve.scalar_tensor_tensor(u1[:, 0, :], raw[:, c, :], scol[:, :],
                                            C_t[:, t0:t0 + WINB], ALU.mult, ALU.mult)
                    ve.scalar_tensor_tensor(u1[:, 1, :],
                                            PB[:, bk_ * 512:bk_ * 512 + 512],
                                            srcol[:, :],
                                            S_t[:, t0:t0 + WINB], ALU.mult, ALU.mult)
                    ve.tensor_add(u1[:, 0, :], u1[:, 0, :], u1[:, 1, :])
                    dst = qrot[:, c, :] if rot_q else krr[:, wsl, c, :]
                    v4 = VE.inc(ve.tensor_tensor(dst, u1[:, 0, :],
                                                 PC[:, bk_ * 512:bk_ * 512 + 512],
                                                 ALU.mult))
                    pbB[bk_].used(ve=v4)
                    pcB[bk_].used(ve=v4)
            qkr.used(pe=qk_pe, ve=VE.n)
            vrot = VE.n
            s_n_rec = {}
            v_ev_rec = {}
            s_sim_rec = {}
            s_vm_rec = {}
            tasks = head_tasks(w + 1)

            def sim_env(j):
                return (PC, [pcB[0], pcB[1]]) if j % 2 == 0 else \
                       (PA, [paB[0], paB[1]])

            def sim_mms(j, h, qc):
                r0 = (h % 2) * 64
                hc = h // 2
                simT, _ = sim_env(j)
                lo = qc * 128
                hi = qc * 128 + 640
                a = lo
                out = []
                while a < hi:
                    e = min((a // 512 + 1) * 512, hi)
                    if a < lo + 512 < e:
                        e = lo + 512
                    slot, cs = (psl, a) if a < 512 else (wsl, a - 512)
                    out.append((simT[:, a - lo:e - lo],
                                qrot[r0:r0 + 64, hc, lo:lo + 128],
                                krr[r0:r0 + 64, slot, hc, cs:cs + (e - a)]))
                    a = e
                return out

            def emit_sim_pair(js):
                for j in js:
                    for r in sim_env(j)[1]:
                        r.wait(b, pe)
                VE.w(pe, vrot)
                mms = [sim_mms(j, *items[j]) for j in js]
                for parts in zip(*mms):
                    for o_, s_, m_ in parts:
                        pe.matmul(o_, s_, m_, start=True, stop=True)
                cnt = PE.inc(pe.last)
                for j in js:
                    s_sim_rec[j] = cnt

            def emit_max(j, h, qc):
                s2i = j % 4
                simT, simR = sim_env(j)
                PE.w(ve, s_sim_rec[j])
                aer[s2i].wait(b, ve)
                vm = VE.inc(ve.reduce_max(nmx[:, s2i:s2i + 1], simT[:, 0:640],
                                          AX.X, negate=True))
                s_vm_rec[j] = vm
                for r in simR:
                    r.used(ve=vm)

            def emit_post(j, h, qc):
                s2i = j % 4
                simT, simR = sim_env(j)
                VE.w(sc, s_vm_rec[j])
                se = SC.inc(sc.activation(ae[:, s2i, :], simT[:, 0:640], AF.Exp,
                                          bias=nmx[:, s2i:s2i + 1], scale=1.0))
                for r in simR:
                    r.used(sc=se)
                SC.w(ve, se)
                if w == 0 and qc < 4:
                    z0 = (4 - qc) * 128
                    if z0 > 0:
                        ve.memset(am[:, s2i, 0:z0], 0.0)
                    ve.scalar_tensor_tensor(am[:, s2i, z0:640], ae[:, s2i, z0:640],
                                            1.0, mask_sb[:, 0, z0:640],
                                            ALU.mult, ALU.mult,
                                            accum_out=zsum[:, s2i:s2i + 1])
                else:
                    ve.scalar_tensor_tensor(am[:, s2i, :], ae[:, s2i, :], 1.0,
                                            mask_sb[:, 0, :], ALU.mult, ALU.mult,
                                            accum_out=zsum[:, s2i:s2i + 1])
                ve.tensor_scalar_max(zsum[:, s2i:s2i + 1], zsum[:, s2i:s2i + 1],
                                     1e-30)
                v_z = VE.inc(ve.reciprocal(rz[:, s2i:s2i + 1],
                                           zsum[:, s2i:s2i + 1]))
                if j % 2 == 0:
                    s_n_rec[j] = ("ve", VE.inc(ve.tensor_tensor(
                        am2[:, s2i, :], am[:, s2i, :],
                        rz[:, s2i:s2i + 1].broadcast_to([128, 640]), ALU.mult)))
                else:
                    VE.w(sc, v_z)
                    s_n_rec[j] = ("sc", SC.inc(sc.activation(
                        am2[:, s2i, :], am[:, s2i, :], AF.Identity, bias=0.0,
                        scale=rz[:, s2i:s2i + 1])))

            def emit_transp_evac(j, h, qc):
                s2i = j % 4
                pt_ = PT if j % 2 == 0 else PT2
                ptRj = ptR[j % 2]
                ptRj.wait(b, pe)
                atTr.wait(b, pe)
                kind_n, cnt_n = s_n_rec[j]
                if kind_n == "ve":
                    VE.w(pe, cnt_n)
                else:
                    SC.w(pe, cnt_n)
                for jj in range(5):
                    pe.transpose(pt_[:, jj * 128:(jj + 1) * 128],
                                 am2[:, s2i, jj * 128:(jj + 1) * 128],
                                 b.ident_b[:, :])
                ptd = PE.inc(pe.last)
                aer[s2i].used(pe=ptd)
                PE.w(sc, ptd)
                s_ev = SC.inc(sc.copy(
                    atT[:, qc:qc + 5, qc * 128:(qc + 1) * 128],
                    pt_.rearrange("p (j n) -> p j n", n=128)[:, 0:5, :]))
                ptRj.used(sc=s_ev)
                v_ev_rec[h] = ("sc", s_ev)

            def emit_av(h):
                nonlocal v_ofm
                r0 = (h % 2) * 64
                hc = h // 2
                ob = (hc % 2) * 512
                for hh in (h, h - 1 if h % 2 else h):
                    kind, cnt = v_ev_rec[hh]
                    if kind == "ve":
                        VE.w(pe, cnt)
                    else:
                        SC.w(pe, cnt)
                pbB[hc % 2].wait(b, pe)
                first = True
                for kcl in (4, 0, 1, 2, 3, 5, 6, 7):
                    qlo = max(kcl - 4, 0) * 128
                    qhi = min(kcl + 1, 4) * 128
                    if qhi <= qlo:
                        continue
                    slot = pvsl if kcl < 4 else vsl
                    tc = kcl % 4
                    pe.matmul(PB[r0:r0 + 64, ob + qlo:ob + qhi],
                              vrr[:, slot, tc, h * 64:(h + 1) * 64],
                              atT[:, kcl, qlo:qhi],
                              start=first, stop=(kcl == 7), tile_position=(0, r0))
                    first = False
                pav = PE.inc(pe.last)
                atTr.used(pe=pav)
                vrrR[vsl].used(pe=pav)
                vrrR[pvsl].used(pe=pav)
                if h % 2 == 1:
                    PE.w(ve, pav)
                    v_ofm = VE.inc(ve.tensor_copy(ofm[:, hc, :], PB[:, ob:ob + 512]))
                    pbB[hc % 2].used(ve=v_ofm)

            items = []
            for hp_ in range(HB // 2):
                for qc_ in range(4):
                    items.append((2 * hp_, qc_))
                    items.append((2 * hp_ + 1, qc_))
            LAG = 2
            v_ofm = 0
            ti_ = 0
            emit_sim_pair([0, 1])
            emit_max(0, *items[0])
            emit_max(1, *items[1])
            for j in range(len(items) + LAG):
                if j < len(items):
                    emit_post(j, *items[j])
                if j % 2 == 1 and j + 1 < len(items):
                    js = [j + 1] + ([j + 2] if j + 2 < len(items) else [])
                    emit_sim_pair(js)
                    for j2 in js:
                        emit_max(j2, *items[j2])
                while ti_ < len(tasks) and ti_ <= (j * len(tasks)) // max(
                        len(items) - 4, 1):
                    tasks[ti_]()
                    ti_ += 1
                if j >= LAG:
                    h2, qc2 = items[j - LAG]
                    emit_transp_evac(j - LAG, h2, qc2)
                    if qc2 == 3:
                        emit_av(h2)
            while ti_ < len(tasks):
                tasks[ti_]()
                ti_ += 1
            v_xo = 0
            for m in range(NCH):
                bk_ = m % 2
                pb = bk_ * 512
                paB[bk_].wait(b, pe)
                VE.w(pe, v_ofm)
                for k in range(4):
                    pe.matmul(PA[:, pb:pb + 512], wo[:, k, m * 128:(m + 1) * 128],
                              ofm[:, k, :], start=(k == 0), stop=(k == 3))
                pwo = PE.inc(pe.last)
                PE.w(ve, pwo)
                XS2[wsl].w(ve)
                v_xo = VE.inc(ve.tensor_tensor(xout[:, m, :], PA[:, pb:pb + 512],
                                               res_t[:, wsl, m, 1:WINB + 1],
                                               ALU.add))
                paB[bk_].used(ve=v_xo)
            xftR[wsl].used(ve=v_xo)
            VE.w(sy, v_xo)
            ZS2[psl].w(sy)
            LS.dma(sy.dma_start(out=b.xffn3[:, :, t0:t0 + WINB], in_=xout[:, :, :]))
            xob = z
            zqt = atT
            VE.w(sc, v_xo)
            zR[wsl].wait(b, sc)
            vxb = SC.inc(sc.copy(xob[:, wsl, :, :], xout[:, :, :]))
            SC.w(ve, vxb)
            vq2 = VE.inc(ve.tensor_mul(zqt[:, :, :], xob[:, wsl, :, :],
                                       xob[:, wsl, :, :]))
            VE.w(pe, vq2)
            SC.w(pe, vxb)
            for r in (paB[0], paB[1], pcB[0], pcB[1]):
                r.wait(b, pe)
            for c in range(4):
                pe.matmul(PA[0:1, 0:512], b.onesc_b[:, :], xob[:, wsl, c, :],
                          start=(c == 0), stop=(c == 3))
            for c in range(4, 8):
                pe.matmul(PA[0:1, 512:1024], b.onesc_b[:, :], xob[:, wsl, c, :],
                          start=(c == 4), stop=(c == 7))
            for c in range(4):
                pe.matmul(PC[0:1, 0:512], b.onesc_b[:, :], zqt[:, c, :],
                          start=(c == 0), stop=(c == 3))
            for c in range(4, 8):
                pe.matmul(PC[0:1, 512:1024], b.onesc_b[:, :], zqt[:, c, :],
                          start=(c == 4), stop=(c == 7))
            prw = PE.inc(pe.last)
            atTr.used(pe=prw)
            zR[wsl].used(pe=prw)
            PE.w(sc, prw)
            sc.copy(row4[:, 0:2, :].rearrange("p a n -> p (a n)"),
                    PA[0:1, 0:1024])
            s_r4 = SC.inc(sc.copy(
                row4[:, 2:4, :].rearrange("p a n -> p (a n)"), PC[0:1, 0:1024]))
            for r in (paB[0], paB[1], pcB[0], pcB[1]):
                r.used(sc=s_r4)
            SC.w(sy, s_r4)
            LS.dma(sy.dma_start(out=b.rows4_d[l][:, :, t0:t0 + WINB],
                                in_=row4[:, :, :]))
        b.phase_end()


def phase_bffn(b, l):
    nc, g, sy, pe, ve, sc = b.nc, b.g, b.sy, b.pe, b.ve, b.sc
    LD, LS, PE, VE, SC, GP = b.LD, b.LS, b.PE, b.VE, b.SC, b.GP
    NT, T = b.NT, b.T
    with ExitStack() as stp:
        lgc = stp.enter_context(nc.sbuf_tensor(f"f{l}_lgc", [128, 8], F32))
        lbc = stp.enter_context(nc.sbuf_tensor(f"f{l}_lbc", [128, 8], F32))
        b1c = stp.enter_context(nc.sbuf_tensor(f"f{l}_b1c", [128, 44], F32))
        b2c = stp.enter_context(nc.sbuf_tensor(f"f{l}_b2c", [128, 8], F32))
        LS.dma(sy.dma_start(out=lgc[:, :], in_=b.lg_h[l][:, :]))
        LS.dma(sy.dma_start(out=lbc[:, :], in_=b.lb_h[l][:, :]))
        LS.dma(sy.dma_start(out=b1c[:, :], in_=b.b1_h[l][:, :]))
        LS.dma(sy.dma_start(out=b2c[:, :], in_=b.b2_h[l][:, :]))

        w1 = stp.enter_context(nc.sbuf_tensor(f"f{l}_w1", [128, NCH, W1N], BF16))
        LD.dma(g.dma_start(out=w1[:, :, :],
                           in_=b.w1_h[l].rearrange("(c p) m -> p c m", p=128)))

        with ExitStack() as s1:
            t1 = s1.enter_context(nc.sbuf_tensor(f"f{l}_t1", [1, T], F32))
            t2 = s1.enter_context(nc.sbuf_tensor(f"f{l}_t2", [1, T], F32))
            t3 = s1.enter_context(nc.sbuf_tensor(f"f{l}_t3", [1, T], F32))
            r4 = b.rows4_d[l]
            LS.dma(sy.dma_start(out=t1[:, :], in_=r4[:, 0, :]))
            LS.dma(sy.dma_start(out=t2[:, :], in_=r4[:, 1, :]))
            LS.dma(sy.dma_start(out=t3[:, :], in_=r4[:, 2, :]))
            LS.w(ve)
            VE.inc(ve.tensor_add(t1[:, 1:T], t1[:, 1:T], t2[:, 0:T - 1]))
            VE.w(sy, VE.n)
            LS.dma(sy.dma_start(out=t2[:, :], in_=r4[:, 3, :]))
            LS.w(ve)
            VE.inc(ve.tensor_add(t3[:, 1:T], t3[:, 1:T], t2[:, 0:T - 1]))
            ve.tensor_copy(t2[:, :], t3[:, :])
            ln_soup(b, s1, t1, t2, b.rg_d[l], b.nm_d[l], f"f{l}", neg=True)
            b.phase_end()

        with ExitStack() as s2p:
            w2 = s2p.enter_context(nc.sbuf_tensor(f"f{l}_w2", [128, NK2, D], BF16))
            LD.dma(g.dma_start(out=w2[:, :, :],
                               in_=b.w2_h[l].rearrange("(c p) m -> p c m", p=128)))
            xft = s2p.enter_context(nc.sbuf_tensor(f"f{l}_xft2",
                                                   [128, 2, NCH, TT + 1], F32))
            xln2 = s2p.enter_context(nc.sbuf_tensor(f"f{l}_xl2", [128, NCH, TT], BF16))
            uscr = s2p.enter_context(nc.sbuf_tensor(f"f{l}_us", [128, 1, TT], F32))
            glu = s2p.enter_context(nc.sbuf_tensor(f"f{l}_glu", [128, NK2, TT], BF16))
            gg = s2p.enter_context(nc.sbuf_tensor(f"f{l}_gg", [128, 1, TT], BF16))
            rowb = s2p.enter_context(nc.sbuf_tensor(f"f{l}_rowb", [1, 2, 2, TT], F32))
            pbc = s2p.enter_context(nc.psum_tensor(f"f{l}_pbc", [128, 1024], F32))
            pag = s2p.enter_context(nc.psum_tensor(f"f{l}_pag", [128, 2048], F32))
            pw2 = s2p.enter_context(nc.psum_tensor(f"f{l}_pw2", [128, 1024], F32))
            GP.inc(g.memset(xft[:, :, :, 0:1], 0.0))
            XFS = b.mksem(f"f{l}s2x")
            RS2 = b.mksem(f"f{l}s2r")
            STS = b.mksem(f"f{l}s2s")
            xftr = [Res(), Res()]
            rowr = [Res(), Res()]
            pbcB = [Res(), Res()]
            agB = [Res(), Res()]
            w2B = [Res(), Res()]
            z2r = Res()
            glur = Res()
            ggr = [Res(), Res()]
            usr = [Res(), Res()]
            LD.w(pe)
            GP.w(ve, GP.n)
            GP.w(sy, GP.n)
            XF2 = [XFS, b.mksem(f"f{l}s2y")]
            RSL2 = [RS2, b.mksem(f"f{l}s2r2")]
            vxl_rec = [0] * (NT + 1)

            def emit_ln_apply(i2):
                sl2 = i2 % 2
                pbcB[0].wait(b, pe)
                pbcB[1].wait(b, pe)
                RSL2[sl2].w(pe)
                pe.matmul(pbc[:, 0:TT], b.onesr_f[:, :], rowb[:, sl2, 0, :],
                          start=True, stop=True)
                pb2 = PE.inc(pe.matmul(pbc[:, TT:2 * TT], b.onesr_f[:, :],
                                       rowb[:, sl2, 1, :], start=True, stop=True))
                rowr[sl2].used(pe=pb2)
                for c in range(NCH):
                    us = 0
                    off = 1 if c < 4 else 0
                    PE.w(ve, pb2)
                    XF2[sl2].w(ve)
                    usr[us].wait(b, ve)
                    z2r.wait(b, ve)
                    ve.tensor_tensor(uscr[:, us, :], xft[:, sl2, c, off:off + TT],
                                     pbc[:, 0:TT], ALU.mult)
                    vu = VE.inc(ve.tensor_tensor(uscr[:, us, :], uscr[:, us, :],
                                                 pbc[:, TT:2 * TT], ALU.add))
                    xftr[sl2].used(ve=vu)
                    pbcB[0].used(ve=vu)
                    pbcB[1].used(ve=vu)
                    VE.w(sc, vu)
                    LS.w(sc)
                    s_x = SC.inc(sc.activation(xln2[:, c, :], uscr[:, us, :],
                                               AF.Identity, bias=lbc[:, c:c + 1],
                                               scale=lgc[:, c:c + 1]))
                    usr[us].used(sc=s_x)
                    z2r.used(sc=s_x)
                vxl_rec[i2] = SC.n

            for i in range(NT):
                a = i * TT
                sl = i % 2
                if i == 0:
                    for pf in (0, 1):
                        if pf >= NT:
                            break
                        ap_ = pf * TT
                        if pf == 0:
                            XF2[0].dma(sy.dma_start(out=xft[:, 0, :, 1:TT + 1],
                                                    in_=b.xffn3[:, :, 0:TT]))
                        else:
                            XF2[1].dma(sy.dma_start(
                                out=xft[:, 1, :, :],
                                in_=b.xffn3[:, :, ap_ - 1:ap_ + TT]))
                        RSL2[pf].dma(sy.dma_start(out=rowb[:, pf, 0, :],
                                                  in_=b.rg_d[l][:, ap_:ap_ + TT]))
                        RSL2[pf].dma(sy.dma_start(out=rowb[:, pf, 1, :],
                                                  in_=b.nm_d[l][:, ap_:ap_ + TT]))
                elif i + 1 < NT:
                    ap_ = (i + 1) * TT
                    s3 = (i + 1) % 2
                    xftr[s3].wait(b, sy)
                    STS.w(sy)
                    XF2[s3].dma(sy.dma_start(out=xft[:, s3, :, :],
                                             in_=b.xffn3[:, :, ap_ - 1:ap_ + TT]))
                    rowr[s3].wait(b, sy)
                    RSL2[s3].dma(sy.dma_start(out=rowb[:, s3, 0, :],
                                              in_=b.rg_d[l][:, ap_:ap_ + TT]))
                    RSL2[s3].dma(sy.dma_start(out=rowb[:, s3, 1, :],
                                              in_=b.nm_d[l][:, ap_:ap_ + TT]))
                if i == 0:
                    emit_ln_apply(0)
                SC.w(pe, vxl_rec[i])
                for j in range(NK2):
                    bk_ = j % 2
                    pb = bk_ * 512
                    agB[bk_].wait(b, pe)
                    for k in range(NCH):
                        pe.matmul(pag[:, pb:pb + 512],
                                  w1[:, k, j * 128:(j + 1) * 128],
                                  xln2[:, k, :], start=(k == 0), stop=(k == NCH - 1))
                    for k in range(NCH):
                        pe.matmul(pag[:, 1024 + pb:1024 + pb + 512],
                                  w1[:, k, (NK2 + j) * 128:(NK2 + j + 1) * 128],
                                  xln2[:, k, :], start=(k == 0), stop=(k == NCH - 1))
                    pj = PE.inc(pe.last)
                    PE.w(sc, pj)
                    ggr[0].wait(b, sc)
                    sg = SC.inc(sc.activation(gg[:, 0, :],
                                              pag[:, 1024 + pb:1024 + pb + 512],
                                              AF.Gelu,
                                              bias=b1c[:, NK2 + j:NK2 + j + 1],
                                              scale=1.0))
                    SC.w(ve, sg)
                    vg = VE.inc(ve.scalar_tensor_tensor(
                        glu[:, j, :], pag[:, pb:pb + 512], b1c[:, j:j + 1],
                        gg[:, 0, :], ALU.add, ALU.mult))
                    agB[bk_].used(sc=sg, ve=vg)
                    ggr[0].used(ve=vg)
                z2r.used(pe=PE.n)
                vglu = VE.n
                if i + 1 < NT:
                    emit_ln_apply(i + 1)
                VE.w(pe, vglu)
                v_xo = 0
                for m in range(NCH):
                    bk_ = m % 2
                    pb = bk_ * 512
                    w2B[bk_].wait(b, pe)
                    for k in range(NK2):
                        pe.matmul(pw2[:, pb:pb + 512], w2[:, k, m * 128:(m + 1) * 128],
                                  glu[:, k, :], start=(k == 0), stop=(k == NK2 - 1))
                    pm = PE.inc(pe.last)
                    PE.w(ve, pm)
                    v_xo = VE.inc(ve.scalar_tensor_tensor(
                        xft[:, sl, m, 1:TT + 1], pw2[:, pb:pb + 512],
                        b2c[:, m:m + 1],
                        xft[:, sl, m, 1:TT + 1], ALU.add, ALU.add))
                    w2B[bk_].used(ve=v_xo)
                glur.used(pe=PE.n)
                xftr[sl].used(ve=v_xo)
                if l == 0:
                    VE.w(sy, v_xo)
                    STS.dma(sy.dma_start(out=b.xffn3[:, :, a:a + TT],
                                         in_=xft[:, sl, :, 1:TT + 1]))
                else:
                    VE.w(g, v_xo)
                    STS.dma(g.dma_start(out=b.out3[:, :, a:a + TT],
                                        in_=xft[:, sl, :, 1:TT + 1],
                                        accum_op=ALU.add))
            b.phase_end()



def kernel(**inputs):
    T = 4096
    consts = host_consts(T)
    w = host_weights(inputs)
    nc = build_nc(T)
    x = np.asarray(inputs["L0_attn_x"], np.float32)
    in_maps = []
    for core in range(8):
        m = {"x": np.ascontiguousarray(x[core].T)}
        for k, v in w.items():
            m[k] = np.ascontiguousarray(np.asarray(v, np.float32))
        for k, v in consts.items():
            m[k] = np.ascontiguousarray(np.asarray(v, np.float32))
        in_maps.append(m)
    res = run_bass_kernel_spmd(nc, in_maps, core_ids=list(range(8)))
    out = np.stack([np.asarray(res.results[i]["out"]).T for i in range(8)])
    return np.ascontiguousarray(out.astype(np.float32))


# version 20
# speedup vs baseline: 1.0248x; 1.0035x over previous
import numpy as np
import ml_dtypes
from contextlib import ExitStack
import concourse.bass as bass
import concourse.mybir as mybir
from concourse.bass_utils import run_bass_kernel_spmd

F32 = mybir.dt.float32
BF16 = mybir.dt.bfloat16
AF = mybir.ActivationFunctionType
ALU = mybir.AluOpType
AX = mybir.AxisListType
BFNP = ml_dtypes.bfloat16

D = 1024
NCH = 8
HA, DHA, FTA = 16, 64, 256
HB, DHB, WINB = 8, 64, 512
HID = 2730
HIDP = 2816
W1N = 2 * HIDP
NK2 = HIDP // 128
QK_SCALE = 8.0
TT = 512



def host_consts(T):
    c = {}
    c["ident"] = np.eye(128, dtype=np.float32)
    c["ones_col"] = np.ones((128, 1), np.float32)
    c["ones_row"] = np.ones((1, 128), np.float32)
    sel = np.zeros((128, 4, 8), np.float32)
    for cc in range(4):
        sel[0:64, cc, 2 * cc] = 1.0
        sel[64:128, cc, 2 * cc + 1] = 1.0
    c["selq"] = sel
    sb = np.zeros((8, 4, 128), np.float32)
    for cc in range(4):
        sb[2 * cc, cc, 0:64] = 1.0
        sb[2 * cc + 1, cc, 64:128] = 1.0
    c["selbc"] = sb
    P = np.zeros((128, 128), np.float32)
    for m in range(128):
        blk = (m // 64) * 64
        ml = m % 64
        P[blk + (ml + 32) % 64, m] = -1.0 if ml < 32 else 1.0
    c["perm"] = P
    half = np.arange(0, DHB, 2, dtype=np.float64)
    inv_freq = 1.0 / 10000.0 ** (half / DHB)
    t = np.arange(T, dtype=np.float64)
    freqs = t[:, None] * inv_freq[None, :]
    freqs = np.concatenate([freqs, freqs], -1)
    base = (half + 0.4 * DHB) / (1.4 * DHB)
    power = (t - T // 2) / float(WINB // 2)
    scale = base[None, :] ** power[:, None]
    scale = np.concatenate([scale, scale], -1)
    cos, sin = np.cos(freqs), np.sin(freqs)

    def two(v):
        return np.ascontiguousarray(np.tile(v.T.astype(np.float32), (2, 1)))

    c["cq"] = two(cos * scale * QK_SCALE)
    c["sq"] = two(sin * scale * QK_SCALE)
    c["ck"] = two(cos / scale)
    c["sk"] = two(sin / scale)
    jj = np.arange(640)[None, :]
    il = np.arange(128)[:, None]
    gen = ((jj >= il) & (jj <= il + WINB)).astype(np.float32)
    masks = np.zeros((5, 128, 640), np.float32)
    masks[4] = gen
    for qc in range(4):
        masks[qc] = gen * (jj >= WINB - qc * 128)
    c["mask"] = np.ascontiguousarray(masks.transpose(1, 0, 2))
    return c


def col128(b):
    return np.ascontiguousarray(np.asarray(b, np.float32).reshape(-1, 128).T)


def host_weights(inp):
    w = {}
    w["Wq"] = inp["L0_attn_Wq"]
    w["Wk"] = inp["L0_attn_Wk"]
    w["Wv"] = inp["L0_attn_Wv"]
    w["Wo"] = inp["L0_attn_Wo"]
    w["bq"] = col128(inp["L0_attn_bq"])
    w["bk"] = col128(inp["L0_attn_bk"])
    w["bv"] = np.ascontiguousarray(np.asarray(inp["L0_attn_bv"], np.float32).reshape(1, D))
    w["bo"] = col128(inp["L0_attn_bo"])
    w["rf"] = inp["L0_attn_random_features"]
    for l in range(2):
        w[f"wqkv{l}"] = np.ascontiguousarray(inp["L0_ffn_wqkv"][l])
        w[f"wo{l}"] = np.ascontiguousarray(inp["L0_ffn_wo"][l])
        qs = np.asarray(inp["L0_ffn_q_scale"][l], np.float32)
        ks = np.asarray(inp["L0_ffn_k_scale"][l], np.float32)
        w[f"qs{l}"] = np.ascontiguousarray(np.tile(qs, 2)[:, None])
        w[f"qsr{l}"] = np.ascontiguousarray(np.tile(np.roll(qs, -32), 2)[:, None])
        w[f"ks{l}"] = np.ascontiguousarray(np.tile(ks, 2)[:, None])
        w[f"ksr{l}"] = np.ascontiguousarray(np.tile(np.roll(ks, -32), 2)[:, None])
        w[f"lg{l}"] = np.ascontiguousarray(np.asarray(inp["L0_ffn_ln_g"][l], np.float32).reshape(1, D))
        w[f"lb{l}"] = col128(inp["L0_ffn_ln_b"][l])
        w1 = np.asarray(inp["L0_ffn_w1"][l], np.float32)
        w1p = np.zeros((D, W1N), np.float32)
        w1p[:, :HID] = w1[:, :HID]
        w1p[:, HIDP:HIDP + HID] = w1[:, HID:]
        w[f"w1{l}"] = w1p
        b1 = np.asarray(inp["L0_ffn_b1"][l], np.float32)
        b1p = np.zeros((W1N,), np.float32)
        b1p[:HID] = b1[:HID]
        b1p[HIDP:HIDP + HID] = b1[HID:]
        w[f"b1{l}"] = col128(b1p)
        w2 = np.asarray(inp["L0_ffn_w2"][l], np.float32)
        w2p = np.zeros((HIDP, D), np.float32)
        w2p[:HID] = w2
        w[f"w2{l}"] = w2p
        w[f"b2{l}"] = col128(inp["L0_ffn_b2"][l])
    return w



class Ctr:
    __slots__ = ("h", "n")

    def __init__(self, h):
        self.h = h
        self.n = 0

    def inc(self, instr, k=1):
        instr.then_inc(self.h, k)
        self.n += k
        return self.n

    def dma(self, instr):
        return self.inc(instr, 16)

    def w(self, eng, n=None):
        v = self.n if n is None else n
        assert v <= self.n, "wait on not-yet-emitted count"
        if v > 0:
            eng.wait_ge(self.h, v)


class Res:
    __slots__ = ("ve", "sc", "pe", "gp")

    def __init__(self):
        self.ve = self.sc = self.pe = self.gp = 0

    def used(self, ve=None, sc=None, pe=None, gp=None):
        if ve is not None:
            self.ve = max(self.ve, ve)
        if sc is not None:
            self.sc = max(self.sc, sc)
        if pe is not None:
            self.pe = max(self.pe, pe)
        if gp is not None:
            self.gp = max(self.gp, gp)

    def wait(self, b, eng):
        if self.ve:
            b.VE.w(eng, self.ve)
        if self.sc:
            b.SC.w(eng, self.sc)
        if self.pe:
            b.PE.w(eng, self.pe)
        if self.gp:
            b.GP.w(eng, self.gp)


class B:
    pass


class PEProxy:

    def __init__(self, eng):
        self._e = eng
        self.last = None

    def matmul(self, *a, **k):
        self.last = self._e.matmul(*a, **k)
        return self.last

    def transpose(self, *a, **k):
        self.last = self._e.transpose(*a, **k)
        return self.last

    def wait_ge(self, *a, **k):
        return self._e.wait_ge(*a, **k)

    def nop(self, *a, **k):
        return self._e.nop(*a, **k)


def build_nc(T=4096, dbg=(), parts=("A0", "A1", "A4", "B0a", "B0f", "B1a", "B1f"),
             a1_sub="abcde"):
    b = B()
    b.T = T
    b.NT = T // TT
    b.NW = T // WINB
    nc = bass.Bass("TRN2", target_bir_lowering=False,
                   detect_race_conditions=False)
    b.nc = nc
    st = ExitStack()
    b.st = st

    def dram(name, shape, dt, kind="ExternalInput"):
        return nc.dram_tensor(name, list(shape), dt, kind=kind).ap()

    def scratch(name, shape, dt):
        kind = "ExternalOutput" if name in dbg else "Internal"
        return dram(name, shape, dt, kind=kind)

    b.dram = dram
    b.x_h = dram("x", (D, T), F32)
    b.Wq_h = dram("Wq", (D, D), F32)
    b.Wk_h = dram("Wk", (D, D), F32)
    b.Wv_h = dram("Wv", (D, D), F32)
    b.Wo_h = dram("Wo", (D, D), F32)
    b.bq_h = dram("bq", (128, 8), F32)
    b.bk_h = dram("bk", (128, 8), F32)
    b.bv_h = dram("bv", (1, D), F32)
    b.bo_h = dram("bo", (128, 8), F32)
    b.rf_h = dram("rf", (DHA, FTA), F32)
    b.wqkv_h = [dram(f"wqkv{l}", (D, 1536), F32) for l in range(2)]
    b.wol_h = [dram(f"wo{l}", (512, D), F32) for l in range(2)]
    b.qs_h = [dram(f"qs{l}", (128, 1), F32) for l in range(2)]
    b.qsr_h = [dram(f"qsr{l}", (128, 1), F32) for l in range(2)]
    b.ks_h = [dram(f"ks{l}", (128, 1), F32) for l in range(2)]
    b.ksr_h = [dram(f"ksr{l}", (128, 1), F32) for l in range(2)]
    b.lg_h = [dram(f"lg{l}", (1, D), F32) for l in range(2)]
    b.lb_h = [dram(f"lb{l}", (128, 8), F32) for l in range(2)]
    b.w1_h = [dram(f"w1{l}", (D, W1N), F32) for l in range(2)]
    b.b1_h = [dram(f"b1{l}", (128, 44), F32) for l in range(2)]
    b.w2_h = [dram(f"w2{l}", (HIDP, D), F32) for l in range(2)]
    b.b2_h = [dram(f"b2{l}", (128, 8), F32) for l in range(2)]
    b.ident_h = dram("ident", (128, 128), F32)
    b.onesc_h = dram("ones_col", (128, 1), F32)
    b.onesr_h = dram("ones_row", (1, 128), F32)
    b.selq_h = dram("selq", (128, 4, 8), F32)
    b.selbc_h = dram("selbc", (8, 4, 128), F32)
    b.perm_h = dram("perm", (128, 128), F32)
    b.cq_h = dram("cq", (128, T), F32)
    b.sq_h = dram("sq", (128, T), F32)
    b.ck_h = dram("ck", (128, T), F32)
    b.sk_h = dram("sk", (128, T), F32)
    b.mask_h = dram("mask", (128, 5, 640), F32)
    b.xln_d = scratch("xln", (D, T), BF16)
    b.xA_d = scratch("xA", (D, T), F32)
    b.xffn_d = scratch("xffn", (D, T), F32)
    b.kvt_dbg = scratch("kvt", (128, 32 * DHA), F32) if "kvt" in dbg else None
    b.out_h = dram("out", (D, T), F32, kind="ExternalOutput")
    b.x3 = b.x_h.rearrange("(c p) t -> p c t", p=128)
    b.xln3 = b.xln_d.rearrange("(c p) t -> p c t", p=128)
    b.xA3 = b.xA_d.rearrange("(c p) t -> p c t", p=128)
    b.xffn3 = b.xffn_d.rearrange("(c p) t -> p c t", p=128)
    b.out3 = b.out_h.rearrange("(c p) t -> p c t", p=128)

    b.LD = Ctr(st.enter_context(nc.semaphore("ld")))
    b.LS = Ctr(st.enter_context(nc.semaphore("ls")))
    b.PE = Ctr(st.enter_context(nc.semaphore("pe")))
    b.VE = Ctr(st.enter_context(nc.semaphore("ve")))
    b.SC = Ctr(st.enter_context(nc.semaphore("sc")))
    b.GP = Ctr(st.enter_context(nc.semaphore("gp")))
    b.CTRS = [b.LD, b.LS, b.PE, b.VE, b.SC, b.GP]

    def mksem(name):
        c = Ctr(st.enter_context(nc.semaphore(name)))
        b.CTRS.append(c)
        return c

    b.mksem = mksem

    b.g = nc.gpsimd
    b.sy = nc.sync
    b.pe = PEProxy(nc.tensor)
    b.ve = nc.vector
    b.sc = nc.scalar

    def phase_end():
        for c in b.CTRS:
            if c.n:
                nc.sync.wait_ge(c.h, c.n)
        nc.all_engine_barrier()

    b.phase_end = phase_end

    b.ident_b = st.enter_context(nc.sbuf_tensor("ident_b", [128, 128], BF16))
    b.onesc_b = st.enter_context(nc.sbuf_tensor("onesc_b", [128, 1], BF16))
    b.onesr_f = st.enter_context(nc.sbuf_tensor("onesr_f", [1, 128], F32))
    b.selq_b = st.enter_context(nc.sbuf_tensor("selq_b", [128, 4, 8], BF16))
    b.selbc_b = st.enter_context(nc.sbuf_tensor("selbc_b", [8, 4, 128], BF16))
    b.perm_b = st.enter_context(nc.sbuf_tensor("perm_b", [128, 128], BF16))
    b.r1_d = scratch("r1row", (1, T), F32)
    b.m1_d = scratch("m1row", (1, T), F32)
    b.r2_d = scratch("r2row", (1, T), F32)
    b.m2_d = scratch("m2row", (1, T), F32)
    b.rg_d = [scratch(f"rgrow{l}", (1, T), F32) for l in range(2)]
    b.nm_d = [scratch(f"nmrow{l}", (1, T), F32) for l in range(2)]
    b.rows4_d = [scratch(f"rows4_{l}", (1, 4, T), F32) for l in range(2)]


    LD, LS = b.LD, b.LS
    g, sy = b.g, b.sy
    LD.dma(g.dma_start(out=b.ident_b[:, :], in_=b.ident_h[:, :]))
    LD.dma(g.dma_start(out=b.onesc_b[:, :], in_=b.onesc_h[:, :]))
    LD.dma(g.dma_start(out=b.selq_b[:, :, :], in_=b.selq_h[:, :, :]))
    LD.dma(g.dma_start(out=b.perm_b[:, :], in_=b.perm_h[:, :]))
    LS.dma(sy.dma_start(out=b.onesr_f[:, :], in_=b.onesr_h[:, :]))
    LD.dma(g.dma_start(out=b.selbc_b[:, :, :], in_=b.selbc_h[:, :, :]))
    phase_end()

    b.a1_sub = a1_sub
    with ExitStack() as a_scope:
        b.kvt_b = a_scope.enter_context(
            nc.sbuf_tensor("kvt_b", [128, 32, DHA], BF16))
        b.wq_sb = a_scope.enter_context(nc.sbuf_tensor("a4_wq", [128, NCH, D], BF16))
        b.wo_sb = a_scope.enter_context(nc.sbuf_tensor("a4_wo", [128, NCH, D], BF16))
        if "A0" in parts:
            phase_a0(b)
        if "A1" in parts:
            phase_a1(b)
        if "A4" in parts:
            phase_a4(b)
    if "A4" in parts:
        phase_ln2s(b)
    for l in range(2):
        if f"B{l}a" in parts:
            phase_battn(b, l)
        if f"B{l}f" in parts:
            phase_bffn(b, l)
    st.close()
    return nc


def ln_soup(b, stp, s_sb, q2_sb, r_dram, mr_dram, tag, neg=False):
    nc, ve, sc, sy, VE, SC, LS = b.nc, b.ve, b.sc, b.sy, b.VE, b.SC, b.LS
    T = b.T
    t_ = stp.enter_context(nc.sbuf_tensor(f"t_{tag}", [1, T], F32))
    ve.tensor_scalar_mul(s_sb[:, :], s_sb[:, :], 1.0 / D)
    ve.tensor_mul(t_[:, :], s_sb[:, :], s_sb[:, :])
    ve.scalar_tensor_tensor(t_[:, :], q2_sb[:, :], 1.0 / D, t_[:, :],
                            ALU.mult, ALU.subtract)
    vv = VE.inc(ve.tensor_scalar_add(t_[:, :], t_[:, :], 1e-5))
    b.VE.w(sc, vv)
    sc.activation(t_[:, :], t_[:, :], AF.Ln, bias=0.0, scale=1.0)
    s2 = SC.inc(sc.activation(q2_sb[:, :], t_[:, :], AF.Exp, bias=0.0, scale=-0.5))
    b.SC.w(ve, s2)
    if neg:
        vr = VE.inc(ve.scalar_tensor_tensor(s_sb[:, :], s_sb[:, :], -1.0,
                                            q2_sb[:, :], ALU.mult, ALU.mult))
    else:
        vr = VE.inc(ve.tensor_mul(s_sb[:, :], s_sb[:, :], q2_sb[:, :]))
    VE.w(sy, vr)
    LS.dma(sy.dma_start(out=r_dram[:, :], in_=q2_sb[:, :]))
    LS.dma(sy.dma_start(out=mr_dram[:, :], in_=s_sb[:, :]))


def elu1(b, psrc, bcol, m, e_scr, rl_scr, dst, pe_done):
    sc, ve, SC, VE, PE = b.sc, b.ve, b.SC, b.VE, b.PE
    VE.w(sc, VE.n)
    PE.w(sc, pe_done)
    s1 = SC.inc(sc.activation(e_scr[:, :], psrc, AF.Exp,
                              bias=bcol[:, m:m + 1], scale=1.0))
    PE.w(ve, pe_done)
    SC.w(ve, s1)
    ve.tensor_scalar(rl_scr[:, :], psrc, bcol[:, m:m + 1], 0.0, ALU.add, ALU.max)
    v = VE.inc(ve.scalar_tensor_tensor(dst, e_scr[:, :], 1.0, rl_scr[:, :],
                                       ALU.min, ALU.add))
    return s1, v


def phase_a0(b):
    nc, g, sy, pe, ve = b.nc, b.g, b.sy, b.pe, b.ve
    LD, LS, PE, VE = b.LD, b.LS, b.PE, b.VE
    NT, T = b.NT, b.T
    with ExitStack() as stp:
        xb = stp.enter_context(nc.sbuf_tensor("a0_xb", [128, 2, NCH, TT], BF16))
        xsq = stp.enter_context(nc.sbuf_tensor("a0_xsq", [128, NCH, TT], BF16))
        s_sb = stp.enter_context(nc.sbuf_tensor("a0_s", [1, T], F32))
        q2_sb = stp.enter_context(nc.sbuf_tensor("a0_q2", [1, T], F32))
        pr = stp.enter_context(nc.psum_tensor("a0_pr", [1, 1024], F32))
        XSL = [b.mksem("a0x0"), b.mksem("a0x1")]
        pe_rec = [0] * NT
        ve_rec = [0] * NT
        for tt in range(NT + 1):
            if tt < NT:
                if tt >= 2:
                    b.PE.w(g, pe_rec[tt - 2])
                XSL[tt % 2].dma(g.dma_start(
                    out=xb[:, tt % 2, :, :], in_=b.x3[:, :, tt * TT:(tt + 1) * TT]))
            if tt == 0:
                continue
            i = tt - 1
            sl = i % 2
            XSL[sl].w(ve)
            if i >= 1:
                PE.w(ve, pe_rec[i - 1])
            vsq = VE.inc(ve.tensor_mul(xsq[:, :, :], xb[:, sl, :, :], xb[:, sl, :, :]))
            XSL[sl].w(pe)
            VE.w(pe, vsq)
            if i >= 1:
                VE.w(pe, ve_rec[i - 1])
            for c in range(NCH):
                pe.matmul(pr[:, 0:TT], b.onesc_b[:, :], xb[:, sl, c, :],
                          start=(c == 0), stop=(c == NCH - 1))
            for c in range(NCH):
                pe.matmul(pr[:, TT:2 * TT], b.onesc_b[:, :], xsq[:, c, :],
                          start=(c == 0), stop=(c == NCH - 1))
            pe_rec[i] = PE.inc(pe.last)
            PE.w(ve, pe_rec[i])
            ve.tensor_copy(s_sb[:, i * TT:(i + 1) * TT], pr[:, 0:TT])
            ve_rec[i] = VE.inc(ve.tensor_copy(q2_sb[:, i * TT:(i + 1) * TT],
                                              pr[:, TT:2 * TT]))
        ln_soup(b, stp, s_sb, q2_sb, b.r1_d, b.m1_d, "ln1")
        b.phase_end()


def phase_a1(b):
    nc, g, sy, pe, ve, sc = b.nc, b.g, b.sy, b.pe, b.ve, b.sc
    LD, LS, PE, VE, SC, GP = b.LD, b.LS, b.PE, b.VE, b.SC, b.GP
    NT, T = b.NT, b.T
    with ExitStack() as stp:
        wk_sb = stp.enter_context(nc.sbuf_tensor("a1_wk", [128, NCH, D], BF16))
        wv_sb = stp.enter_context(nc.sbuf_tensor("a1_wv", [128, NCH, D], BF16))
        rf_sb = stp.enter_context(nc.sbuf_tensor("a1_rf", [128, FTA], BF16))
        bkc = stp.enter_context(nc.sbuf_tensor("a1_bkc", [128, 8], F32))
        bvbc = stp.enter_context(nc.sbuf_tensor("a1_bvbc", [128, D], F32))
        xb = stp.enter_context(nc.sbuf_tensor("a1_xb", [128, 2, NCH, TT], BF16))
        xln = stp.enter_context(nc.sbuf_tensor("a1_xln", [128, 2, NCH, TT], BF16))
        kel = stp.enter_context(nc.sbuf_tensor("a1_kel", [128, NCH, TT], BF16))
        e_scr = stp.enter_context(nc.sbuf_tensor("a1_e", [128, TT], F32))
        rl_scr = stp.enter_context(nc.sbuf_tensor("a1_rl", [128, TT], F32))
        v_tile = stp.enter_context(nc.sbuf_tensor("a1_v", [128, 4, D], BF16))
        kp_sb = stp.enter_context(nc.sbuf_tensor("a1_kp", [128, 2, 2, 256], BF16))
        kvacc = stp.enter_context(nc.sbuf_tensor("a1_kva", [128, 4, TT], F32))
        rowb = stp.enter_context(nc.sbuf_tensor("a1_rowb", [1, 2, 2, TT], F32))
        pA = stp.enter_context(nc.psum_tensor("a1_pA", [128, 1024], F32))
        pkp = stp.enter_context(nc.psum_tensor("a1_pkp", [128, 1024], F32))
        pkv = stp.enter_context(nc.psum_tensor("a1_pkv", [128, 2048], F32))

        LD.dma(g.dma_start(out=wk_sb[:, :, :],
                           in_=b.Wk_h.rearrange("(c p) m -> p c m", p=128)))
        LD.dma(g.dma_start(out=wv_sb[:, :, :],
                           in_=b.Wv_h.rearrange("(c p) m -> p c m", p=128)))
        LD.dma(g.dma_start(out=rf_sb[0:64, :], in_=b.rf_h[:, :]))
        LD.dma(g.dma_start(out=rf_sb[64:128, :], in_=b.rf_h[:, :]))
        LS.dma(sy.dma_start(out=bkc[:, :], in_=b.bk_h[:, :]))
        LS.dma(sy.dma_start(out=bvbc[:, :],
                            in_=b.bv_h.partition_broadcast(128)[:, 0, :]))
        GP.inc(g.memset(kvacc[:, :, :], 0.0))
        LD.dma(g.dma_start(out=b.wq_sb[:, :, :],
                           in_=b.Wq_h.rearrange("(c p) m -> p c m", p=128)))
        LD.dma(g.dma_start(out=b.wo_sb[:, :, :],
                           in_=b.Wo_h.rearrange("(c p) m -> p c m", p=128)))
        wk_ld = LD.n
        c_ld = LS.n
        gz = GP.n

        XSL = [b.mksem("a1x0"), b.mksem("a1x1")]
        RSL = [b.mksem("a1r0"), b.mksem("a1r1")]
        SSL = [b.mksem("a1s0"), b.mksem("a1s1")]
        pa = [Res(), Res()]
        pkp_r = Res()
        kpr = [Res(), Res()]
        pkv_r = Res()
        kelr = Res()
        vtr = Res()
        xbr = [Res(), Res()]
        rowr = [Res(), Res()]
        xlnr = [Res(), Res()]
        for tt in range(NT + 1):
            if tt < NT:
                s2 = tt % 2
                xbr[s2].wait(b, g)
                XSL[s2].dma(g.dma_start(
                    out=xb[:, s2, :, :], in_=b.x3[:, :, tt * TT:(tt + 1) * TT]))
                rowr[s2].wait(b, sy)
                RSL[s2].dma(sy.dma_start(out=rowb[:, s2, 0, :],
                                         in_=b.r1_d[:, tt * TT:(tt + 1) * TT]))
                RSL[s2].dma(sy.dma_start(out=rowb[:, s2, 1, :],
                                         in_=b.m1_d[:, tt * TT:(tt + 1) * TT]))
            if tt == 0:
                continue
            i = tt - 1
            sl = i % 2
            pa[0].wait(b, pe)
            pa[1].wait(b, pe)
            RSL[sl].w(pe)
            LS.w(pe)
            pe.matmul(pA[:, 0:TT], b.onesr_f[:, :], rowb[:, sl, 0, :],
                      start=True, stop=True)
            mm_bc = PE.inc(pe.matmul(pA[:, TT:2 * TT], b.onesr_f[:, :],
                                     rowb[:, sl, 1, :], start=True, stop=True))
            rowr[sl].used(pe=mm_bc)
            PE.w(ve, mm_bc)
            XSL[sl].w(ve)
            SSL[sl].w(ve)
            xlnr[sl].wait(b, ve)
            ve.tensor_tensor(xln[:, sl, :, :], xb[:, sl, :, :],
                             pA[:, 0:TT].unsqueeze(1).broadcast_to([128, NCH, TT]),
                             ALU.mult)
            v_xln = VE.inc(ve.tensor_tensor(
                xln[:, sl, :, :], xln[:, sl, :, :],
                pA[:, TT:2 * TT].unsqueeze(1).broadcast_to([128, NCH, TT]),
                ALU.subtract))
            xbr[sl].used(ve=v_xln)
            pa[0].used(ve=v_xln)
            pa[1].used(ve=v_xln)
            VE.w(sy, v_xln)
            SSL[sl].dma(sy.dma_start(out=b.xln3[:, :, i * TT:(i + 1) * TT],
                                     in_=xln[:, sl, :, :]))
            LD.w(pe)
            PE.w(ve, mm_bc)
            kelr.wait(b, ve)
            kelr.wait(b, sc)
            ve_kel = [0] * NCH
            for m in range(NCH):
                bk_ = m % 2
                pb = bk_ * TT
                pa[bk_].wait(b, pe)
                VE.w(pe, v_xln)
                for k in range(NCH):
                    pe.matmul(pA[:, pb:pb + TT], wk_sb[:, k, m * 128:(m + 1) * 128],
                              xln[:, sl, k, :], start=(k == 0), stop=(k == NCH - 1))
                pd = PE.inc(pe.last)
                s1, vk = elu1(b, pA[:, pb:pb + TT], bkc, m, e_scr, rl_scr,
                              kel[:, m, :], pd)
                ve_kel[m] = vk
                pa[bk_].used(ve=vk, sc=s1)
            vtr.wait(b, ve)
            for tc in range(4):
                for mh in range(2):
                    j = tc * 2 + mh
                    bk_ = j % 2
                    pb = bk_ * 512
                    pa[bk_].wait(b, pe)
                    for k in range(NCH):
                        pe.matmul(pA[:, pb:pb + 512],
                                  xln[:, sl, k, tc * 128:(tc + 1) * 128],
                                  wv_sb[:, k, mh * 512:(mh + 1) * 512],
                                  start=(k == 0), stop=(k == NCH - 1))
                    pev = PE.inc(pe.last)
                    PE.w(ve, pev)
                    v_v = VE.inc(ve.tensor_tensor(
                        v_tile[:, tc, mh * 512:(mh + 1) * 512], pA[:, pb:pb + 512],
                        bvbc[:, mh * 512:(mh + 1) * 512], ALU.add))
                    pa[bk_].used(ve=v_v)
            xlnr[sl].used(pe=PE.n)
            for tc in range(4):
                pend = None
                for hp in range(8):
                    slot = hp % 2
                    if slot == 0:
                        tgt, rl = pkp, [pkp_r]
                    else:
                        tgt, rl = pA, [pa[0], pa[1]]
                    for r in rl:
                        r.wait(b, pe)
                    VE.w(pe, ve_kel[NCH - 1])
                    kpr[slot].wait(b, ve)
                    for par in range(2):
                        h = hp * 2 + par
                        r0 = par * 64
                        pe.matmul(tgt[:, par * 512:par * 512 + 256],
                                  kel[r0:r0 + 64, h // 2, tc * 128:(tc + 1) * 128],
                                  rf_sb[r0:r0 + 64, :], start=True, stop=True)
                    pekp = PE.inc(pe.last)
                    PE.w(ve, pekp)
                    v_kp = VE.inc(ve.tensor_copy(
                        kp_sb[:, slot, :, :],
                        tgt.rearrange("p (a n) -> p a n", n=512)[:, :, 0:256]))
                    for r in rl:
                        r.used(ve=v_kp)
                    if pend is not None:
                        php, pvkp = pend
                        VE.w(pe, pvkp)
                        pkv_r.wait(b, pe)
                        for par in range(2):
                            h = php * 2 + par
                            for fc in range(2):
                                s = h * 2 + fc
                                pe.matmul(pkv[:, s * 64:(s + 1) * 64],
                                          kp_sb[:, php % 2, par,
                                                fc * 128:(fc + 1) * 128],
                                          v_tile[:, tc, h * 64:(h + 1) * 64],
                                          start=True, stop=True)
                        kpr[php % 2].used(pe=PE.inc(pe.last))
                    pend = (hp, v_kp)
                php, pvkp = pend
                VE.w(pe, pvkp)
                pkv_r.wait(b, pe)
                for par in range(2):
                    h = php * 2 + par
                    for fc in range(2):
                        s = h * 2 + fc
                        pe.matmul(pkv[:, s * 64:(s + 1) * 64],
                                  kp_sb[:, php % 2, par, fc * 128:(fc + 1) * 128],
                                  v_tile[:, tc, h * 64:(h + 1) * 64],
                                  start=True, stop=True)
                pkvd = PE.inc(pe.last)
                kpr[php % 2].used(pe=pkvd)
                vtr.used(pe=pkvd)
                kelr.used(pe=pkvd)
                PE.w(ve, pkvd)
                GP.w(ve, gz)
                v_kva = VE.inc(ve.tensor_tensor(
                    kvacc[:, :, :].rearrange("p a n -> p (a n)"), pkv[:, :],
                    kvacc[:, :, :].rearrange("p a n -> p (a n)"), ALU.add))
                pkv_r.used(ve=v_kva)
        vkf = VE.inc(ve.tensor_copy(b.kvt_b[:, :, :].rearrange("p a n -> p (a n)"),
                                    kvacc[:, :, :].rearrange("p a n -> p (a n)")))
        if b.kvt_dbg is not None:
            VE.w(sy, vkf)
            LS.dma(sy.dma_start(out=b.kvt_dbg[:, :],
                                in_=kvacc[:, :, :].rearrange("p a n -> p (a n)")))
        b.phase_end()


def phase_a4(b):
    nc, g, sy, pe, ve, sc = b.nc, b.g, b.sy, b.pe, b.ve, b.sc
    LD, LS, PE, VE, SC = b.LD, b.LS, b.PE, b.VE, b.SC
    NT, T = b.NT, b.T
    with ExitStack() as stp:
        wq_sb = b.wq_sb
        wo_sb = b.wo_sb
        rf_sb = stp.enter_context(nc.sbuf_tensor("a4_rf", [128, FTA], BF16))
        bqc = stp.enter_context(nc.sbuf_tensor("a4_bqc", [128, 8], F32))
        boc = stp.enter_context(nc.sbuf_tensor("a4_boc", [128, 8], F32))
        xlt = stp.enter_context(nc.sbuf_tensor("a4_xlt", [128, 2, NCH, TT], BF16))
        xt = stp.enter_context(nc.sbuf_tensor("a4_xt", [128, NCH, TT], F32))
        qel = stp.enter_context(nc.sbuf_tensor("a4_qel", [128, NCH, TT], BF16))
        e_scr = stp.enter_context(nc.sbuf_tensor("a4_e", [128, TT], F32))
        rl_scr = stp.enter_context(nc.sbuf_tensor("a4_rl", [128, TT], F32))
        qp_sb = stp.enter_context(nc.sbuf_tensor("a4_qp", [128, 2, 2, TT], BF16))
        afm = stp.enter_context(nc.sbuf_tensor("a4_afm", [128, NCH, TT], BF16))
        xA_t = stp.enter_context(nc.sbuf_tensor("a4_xA", [128, NCH, TT], F32))
        xAb = stp.enter_context(nc.sbuf_tensor("a4_xAb", [128, NCH, TT], BF16))
        xAsq = stp.enter_context(nc.sbuf_tensor("a4_xAsq", [128, NCH, TT], BF16))
        s_sb = stp.enter_context(nc.sbuf_tensor("a4_s", [1, T], F32))
        q2_sb = stp.enter_context(nc.sbuf_tensor("a4_q2", [1, T], F32))
        pg = stp.enter_context(nc.psum_tensor("a4_pg", [128, 1024], F32))
        pqp = stp.enter_context(nc.psum_tensor("a4_pqp", [128, 1024], F32))
        pout = stp.enter_context(nc.psum_tensor("a4_pout", [128, 1024], F32))
        prow = stp.enter_context(nc.psum_tensor("a4_prow", [1, 1024], F32))

        LD.dma(g.dma_start(out=rf_sb[0:64, :], in_=b.rf_h[:, :]))
        LD.dma(g.dma_start(out=rf_sb[64:128, :], in_=b.rf_h[:, :]))
        LS.dma(sy.dma_start(out=bqc[:, :], in_=b.bq_h[:, :]))
        LS.dma(sy.dma_start(out=boc[:, :], in_=b.bo_h[:, :]))
        w_ld, c_ld = LD.n, LS.n

        XSL = [b.mksem("a4x0"), b.mksem("a4x1")]
        XTS = b.mksem("a4xt")
        SAS = b.mksem("a4sa")
        b.v_afm_last = 0
        pg_b = [Res(), Res()]
        pqp_r = Res()
        pout_b = [Res(), Res()]
        prow_r = Res()
        qpr = [Res(), Res()]
        qelr = Res()
        afmr = Res()
        xltr = [Res(), Res()]
        xtr = Res()
        for tt in range(NT + 1):
            if tt < NT:
                s2 = tt % 2
                xltr[s2].wait(b, sy)
                XSL[s2].dma(sy.dma_start(
                    out=xlt[:, s2, :, :],
                    in_=b.xln3[:, :, tt * TT:(tt + 1) * TT]))
            if tt == 0:
                continue
            i = tt - 1
            sl = i % 2
            xtr.wait(b, sy)
            SAS.w(sy)
            XTS.dma(sy.dma_start(out=xt[:, :, :],
                                 in_=b.x3[:, :, i * TT:(i + 1) * TT]))
            LD.w(pe)
            XSL[sl].w(pe)
            qelr.wait(b, ve)
            qelr.wait(b, sc)
            ve_qel = [0] * NCH
            for m in range(NCH):
                bk_ = m % 2
                pb = bk_ * TT
                pg_b[bk_].wait(b, pe)
                for k in range(NCH):
                    pe.matmul(pg[:, pb:pb + TT], wq_sb[:, k, m * 128:(m + 1) * 128],
                              xlt[:, sl, k, :], start=(k == 0), stop=(k == NCH - 1))
                pd = PE.inc(pe.last)
                s1, vq = elu1(b, pg[:, pb:pb + TT], bqc, m, e_scr, rl_scr,
                              qel[:, m, :], pd)
                ve_qel[m] = vq
                pg_b[bk_].used(ve=vq, sc=s1)
            xltr[sl].used(pe=PE.n)
            def emit_out(pd_, last=False):
                h_, v_qp_ = pd_
                hp_, par_ = h_ // 2, h_ % 2
                r0_ = par_ * 64
                ob_ = (hp_ % 2) * TT
                VE.w(pe, v_qp_)
                if par_ == 0:
                    pout_b[hp_ % 2].wait(b, pe)
                for fc_ in range(2):
                    s_ = h_ * 2 + fc_
                    pe.matmul(pout[r0_:r0_ + 64, ob_:ob_ + TT], b.kvt_b[:, s_, :],
                              qp_sb[:, h_ % 2, fc_, :], start=(fc_ == 0),
                              stop=(fc_ == 1), tile_position=(0, r0_))
                cnt_ = PE.inc(pe.last)
                qpr[h_ % 2].used(pe=cnt_)
                if par_ == 1:
                    PE.w(ve, cnt_)
                    v_ = VE.inc(ve.tensor_copy(afm[:, hp_, :], pout[:, ob_:ob_ + TT]))
                    pout_b[hp_ % 2].used(ve=v_)
                    b.v_afm_last = v_

            afmr.wait(b, ve)
            pend = None
            for h in range(HA):
                hp = h // 2
                par = h % 2
                r0 = par * 64
                slot = h % 2
                if slot == 0:
                    tgt, rl = pqp, [pqp_r]
                else:
                    tgt, rl = pg, [pg_b[0], pg_b[1]]
                for r in rl:
                    r.wait(b, pe)
                VE.w(pe, ve_qel[NCH - 1])
                qpr[slot].wait(b, ve)
                for fc in range(2):
                    pe.matmul(tgt[:, fc * TT:(fc + 1) * TT],
                              rf_sb[r0:r0 + 64, fc * 128:(fc + 1) * 128],
                              qel[r0:r0 + 64, h // 2, :], start=True, stop=True)
                pq = PE.inc(pe.last)
                PE.w(ve, pq)
                v_qp = VE.inc(ve.tensor_copy(
                    qp_sb[:, slot, :, :],
                    tgt.rearrange("p (a n) -> p a n", n=TT)))
                for r in rl:
                    r.used(ve=v_qp)
                if pend is not None:
                    emit_out(pend)
                pend = (h, v_qp)
            emit_out(pend, last=True)
            v_xa = 0
            for m in range(NCH):
                bk_ = m % 2
                pb = bk_ * TT
                pg_b[bk_].wait(b, pe)
                VE.w(pe, b.v_afm_last)
                for k in range(NCH):
                    pe.matmul(pg[:, pb:pb + TT], wo_sb[:, k, m * 128:(m + 1) * 128],
                              afm[:, k, :], start=(k == 0), stop=(k == NCH - 1))
                pwo = PE.inc(pe.last)
                PE.w(ve, pwo)
                XTS.w(ve)
                LS.w(ve)
                v_xa = VE.inc(ve.scalar_tensor_tensor(
                    xA_t[:, m, :], pg[:, pb:pb + TT], boc[:, m:m + 1],
                    xt[:, m, :], ALU.add, ALU.add))
                pg_b[bk_].used(ve=v_xa)
            afmr.used(pe=PE.n)
            xtr.used(ve=v_xa)
            vc = VE.inc(ve.tensor_copy(xAb[:, :, :], xA_t[:, :, :]))
            vsq = VE.inc(ve.tensor_mul(xAsq[:, :, :], xAb[:, :, :], xAb[:, :, :]))
            VE.w(sy, v_xa)
            SAS.dma(sy.dma_start(out=b.xA3[:, :, i * TT:(i + 1) * TT],
                                 in_=xA_t[:, :, :]))
            SAS.dma(sy.dma_start(out=b.out3[:, :, i * TT:(i + 1) * TT],
                                 in_=xA_t[:, :, :]))
            VE.w(pe, vsq)
            prow_r.wait(b, pe)
            for c in range(NCH):
                pe.matmul(prow[:, 0:TT], b.onesc_b[:, :], xAb[:, c, :],
                          start=(c == 0), stop=(c == NCH - 1))
            for c in range(NCH):
                pe.matmul(prow[:, TT:2 * TT], b.onesc_b[:, :], xAsq[:, c, :],
                          start=(c == 0), stop=(c == NCH - 1))
            pr = PE.inc(pe.last)
            PE.w(ve, pr)
            ve.tensor_copy(s_sb[:, i * TT:(i + 1) * TT], prow[:, 0:TT])
            vrow = VE.inc(ve.tensor_copy(q2_sb[:, i * TT:(i + 1) * TT],
                                         prow[:, TT:2 * TT]))
            prow_r.used(ve=vrow)
        ln_soup(b, stp, s_sb, q2_sb, b.r2_d, b.m2_d, "ln2")
        b.phase_end()


def phase_ln2s(b):
    nc, g, sy, pe, ve, sc = b.nc, b.g, b.sy, b.pe, b.ve, b.sc
    LD, LS, PE, VE, SC, GP = b.LD, b.LS, b.PE, b.VE, b.SC, b.GP
    NT, T = b.NT, b.T
    with ExitStack() as stp:
        xt = stp.enter_context(nc.sbuf_tensor("l2_xt", [128, 2, NCH, TT], F32))
        xo = stp.enter_context(nc.sbuf_tensor("l2_xo", [128, 2, NCH, TT], F32))
        rowb = stp.enter_context(nc.sbuf_tensor("l2_row", [1, 2, 2, TT], F32))
        pA = stp.enter_context(nc.psum_tensor("l2_pA", [128, 1024], F32))
        XSL = [b.mksem("l2x0"), b.mksem("l2x1")]
        RSL = [b.mksem("l2r0"), b.mksem("l2r1")]
        SSL = [b.mksem("l2s0"), b.mksem("l2s1")]
        xtr = [Res(), Res()]
        xor_ = [Res(), Res()]
        rowr = [Res(), Res()]
        pa = [Res(), Res()]
        for tt in range(NT + 1):
            if tt < NT:
                s2 = tt % 2
                xtr[s2].wait(b, sy)
                XSL[s2].dma(sy.dma_start(
                    out=xt[:, s2, :, :], in_=b.xA3[:, :, tt * TT:(tt + 1) * TT]))
                rowr[s2].wait(b, sy)
                RSL[s2].dma(sy.dma_start(out=rowb[:, s2, 0, :],
                                         in_=b.r2_d[:, tt * TT:(tt + 1) * TT]))
                RSL[s2].dma(sy.dma_start(out=rowb[:, s2, 1, :],
                                         in_=b.m2_d[:, tt * TT:(tt + 1) * TT]))
            if tt == 0:
                continue
            i = tt - 1
            sl = i % 2
            pa[0].wait(b, pe)
            pa[1].wait(b, pe)
            RSL[sl].w(pe)
            pe.matmul(pA[:, 0:TT], b.onesr_f[:, :], rowb[:, sl, 0, :],
                      start=True, stop=True)
            mm_bc = PE.inc(pe.matmul(pA[:, TT:2 * TT], b.onesr_f[:, :],
                                     rowb[:, sl, 1, :], start=True, stop=True))
            rowr[sl].used(pe=mm_bc)
            PE.w(ve, mm_bc)
            XSL[sl].w(ve)
            SSL[sl].w(ve)
            xor_[sl].wait(b, ve)
            ve.tensor_tensor(xo[:, sl, :, :], xt[:, sl, :, :],
                             pA[:, 0:TT].unsqueeze(1).broadcast_to([128, NCH, TT]),
                             ALU.mult)
            v_x = VE.inc(ve.tensor_tensor(
                xo[:, sl, :, :], xo[:, sl, :, :],
                pA[:, TT:2 * TT].unsqueeze(1).broadcast_to([128, NCH, TT]),
                ALU.subtract))
            xtr[sl].used(ve=v_x)
            pa[0].used(ve=v_x)
            pa[1].used(ve=v_x)
            VE.w(sy, v_x)
            SSL[sl].dma(sy.dma_start(out=b.xffn3[:, :, i * TT:(i + 1) * TT],
                                     in_=xo[:, sl, :, :]))
        b.phase_end()


def phase_battn(b, l):
    nc, g, sy, pe, ve, sc = b.nc, b.g, b.sy, b.pe, b.ve, b.sc
    LD, LS, PE, VE, SC, GP = b.LD, b.LS, b.PE, b.VE, b.SC, b.GP
    NW, T = b.NW, b.T
    with ExitStack() as stp:
        wqkv = stp.enter_context(nc.sbuf_tensor(f"b{l}_wqkv", [128, NCH, 1536], BF16))
        wo = stp.enter_context(nc.sbuf_tensor(f"b{l}_wo", [128, 4, D], BF16))
        cq = stp.enter_context(nc.sbuf_tensor(f"b{l}_cq", [128, T], BF16))
        sq = stp.enter_context(nc.sbuf_tensor(f"b{l}_sq", [128, T], BF16))
        ck = stp.enter_context(nc.sbuf_tensor(f"b{l}_ck", [128, T], BF16))
        sk = stp.enter_context(nc.sbuf_tensor(f"b{l}_sk", [128, T], BF16))
        qsc = stp.enter_context(nc.sbuf_tensor(f"b{l}_qsc", [128, 1], F32))
        qsrc = stp.enter_context(nc.sbuf_tensor(f"b{l}_qsrc", [128, 1], F32))
        ksc = stp.enter_context(nc.sbuf_tensor(f"b{l}_ksc", [128, 1], F32))
        ksrc = stp.enter_context(nc.sbuf_tensor(f"b{l}_ksrc", [128, 1], F32))
        xft = stp.enter_context(nc.sbuf_tensor(f"b{l}_xft",
                                               [128, 2, NCH, WINB + 1], F32))
        z = stp.enter_context(nc.sbuf_tensor(f"b{l}_z", [128, 2, NCH, WINB], BF16))
        qraw = stp.enter_context(nc.sbuf_tensor(f"b{l}_qraw", [128, 4, WINB], BF16))
        kraw = stp.enter_context(nc.sbuf_tensor(f"b{l}_kraw", [128, 4, WINB], BF16))
        qrot = stp.enter_context(nc.sbuf_tensor(f"b{l}_qrot", [128, 4, WINB], BF16))
        sqs = qrot
        krr = stp.enter_context(nc.sbuf_tensor(f"b{l}_krr", [128, 2, 4, WINB], BF16))
        vrr = stp.enter_context(nc.sbuf_tensor(f"b{l}_vrr", [128, 3, 4, WINB], BF16))
        u1 = stp.enter_context(nc.sbuf_tensor(f"b{l}_u1", [128, 2, WINB], BF16))
        rsq = stp.enter_context(nc.sbuf_tensor(f"b{l}_rsq", [8, WINB], F32))
        rsk = stp.enter_context(nc.sbuf_tensor(f"b{l}_rsk", [8, WINB], F32))
        rsqb = stp.enter_context(nc.sbuf_tensor(f"b{l}_rsqb", [8, WINB], BF16))
        rskb = stp.enter_context(nc.sbuf_tensor(f"b{l}_rskb", [8, WINB], BF16))
        row4 = stp.enter_context(nc.sbuf_tensor(f"b{l}_row4", [1, 4, WINB], F32))
        mask_sb = stp.enter_context(nc.sbuf_tensor(f"b{l}_mask", [128, 1, 640], BF16))
        ae = stp.enter_context(nc.sbuf_tensor(f"b{l}_ae", [128, 4, 640], BF16))
        am = stp.enter_context(nc.sbuf_tensor(f"b{l}_am", [128, 4, 640], BF16))
        am2 = stp.enter_context(nc.sbuf_tensor(f"b{l}_am2", [128, 4, 640], BF16))
        nmx = stp.enter_context(nc.sbuf_tensor(f"b{l}_nmx", [128, 4], F32))
        zsum = stp.enter_context(nc.sbuf_tensor(f"b{l}_zs", [128, 4], F32))
        rz = stp.enter_context(nc.sbuf_tensor(f"b{l}_rz", [128, 4], F32))
        atT = stp.enter_context(nc.sbuf_tensor(f"b{l}_atT", [128, NCH, WINB], BF16))
        ofm = stp.enter_context(nc.sbuf_tensor(f"b{l}_ofm", [128, 4, WINB], BF16))
        xout = stp.enter_context(nc.sbuf_tensor(f"b{l}_xout", [128, NCH, WINB], F32))
        PA = stp.enter_context(nc.psum_tensor(f"b{l}_PA", [128, 1024], F32))
        PB = stp.enter_context(nc.psum_tensor(f"b{l}_PB", [128, 1024], F32))
        PC = stp.enter_context(nc.psum_tensor(f"b{l}_PC", [128, 1024], F32))
        PT = stp.enter_context(nc.psum_tensor(f"b{l}_PT", [128, 1024], BF16))
        PT2 = stp.enter_context(nc.psum_tensor(f"b{l}_PT2", [128, 1024], BF16))

        LD.dma(g.dma_start(out=wqkv[:, :, :],
                           in_=b.wqkv_h[l].rearrange("(c p) m -> p c m", p=128)))
        LD.dma(g.dma_start(out=wo[:, :, :],
                           in_=b.wol_h[l].rearrange("(c p) m -> p c m", p=128)))
        LD.dma(g.dma_start(out=mask_sb[:, 0, :], in_=b.mask_h[:, 4, :]))
        LD.dma(g.dma_start(out=cq[:, :], in_=b.cq_h[:, :]))
        LD.dma(g.dma_start(out=sq[:, :], in_=b.sq_h[:, :]))
        LD.dma(g.dma_start(out=ck[:, :], in_=b.ck_h[:, :]))
        LD.dma(g.dma_start(out=sk[:, :], in_=b.sk_h[:, :]))
        LS.dma(sy.dma_start(out=qsc[:, :], in_=b.qs_h[l][:, :]))
        LS.dma(sy.dma_start(out=qsrc[:, :], in_=b.qsr_h[l][:, :]))
        LS.dma(sy.dma_start(out=ksc[:, :], in_=b.ks_h[l][:, :]))
        LS.dma(sy.dma_start(out=ksrc[:, :], in_=b.ksr_h[l][:, :]))
        GP.inc(g.memset(krr[:, 1, :, :], 0.0))
        GP.inc(g.memset(vrr[:, 2, :, :], 0.0))
        GP.inc(g.memset(xft[:, :, :, 0:1], 0.0))
        GP.inc(g.memset(z[:, 0, 4:8, 0:1], 0.0))
        w_ld, c_ld, g_ms = LD.n, LS.n, GP.n
        LD.w(pe)
        LS.w(pe)
        GP.w(pe, g_ms)
        GP.w(ve, g_ms)
        GP.w(sy, g_ms)
        paB = [Res(), Res()]
        pbB = [Res(), Res()]
        pcB = [Res(), Res()]
        ptR = [Res(), Res()]
        atTr = Res()
        zR = [Res(), Res()]
        vrrR = [Res(), Res(), Res()]
        qkr = Res()
        xftR = [Res(), Res()]
        aer = [Res(), Res(), Res(), Res()]
        XS2 = [b.mksem(f"b{l}x0"), b.mksem(f"b{l}x1")]
        ZS2 = [b.mksem(f"b{l}z0"), b.mksem(f"b{l}z1")]
        wst = {}

        def head_tasks(w2):
            if w2 >= NW:
                return []
            t02 = w2 * WINB
            sl2 = w2 % 2
            st2 = wst.setdefault(w2, {})

            def t_load():
                xftR[sl2].wait(b, sy)
                if w2 == 0:
                    XS2[sl2].dma(sy.dma_start(out=xft[:, sl2, :, 1:WINB + 1],
                                              in_=b.xffn3[:, :, 0:WINB]))
                else:
                    XS2[sl2].dma(sy.dma_start(
                        out=xft[:, sl2, :, :],
                        in_=b.xffn3[:, :, t02 - 1:t02 + WINB]))

            def t_z():
                zR[sl2].wait(b, g)
                if w2 == 0:
                    ZS2[sl2].dma(g.dma_start(out=z[:, sl2, 0:4, :],
                                             in_=b.xffn3[:, 0:4, 0:WINB]))
                    ZS2[sl2].dma(g.dma_start(out=z[:, sl2, 4:8, 1:WINB],
                                             in_=b.xffn3[:, 4:8, 0:WINB - 1]))
                else:
                    ZS2[sl2].dma(g.dma_start(out=z[:, sl2, 0:4, :],
                                             in_=b.xffn3[:, 0:4, t02:t02 + WINB]))
                    ZS2[sl2].dma(g.dma_start(
                        out=z[:, sl2, 4:8, :],
                        in_=b.xffn3[:, 4:8, t02 - 1:t02 - 1 + WINB]))

            def mk_qkv(m):
                def t_qkv():
                    bk_ = m % 2
                    pb = bk_ * 512
                    pbB[bk_].wait(b, pe)
                    ZS2[sl2].w(pe)
                    for k in range(NCH):
                        pe.matmul(PB[:, pb:pb + 512],
                                  wqkv[:, k, m * 128:(m + 1) * 128],
                                  z[:, sl2, k, :], start=(k == 0),
                                  stop=(k == NCH - 1))
                    pq = PE.inc(pe.last)
                    dst = qraw[:, m, :] if m < 4 else kraw[:, m - 4, :]
                    if m % 2 == 0:
                        PE.w(sc, pq)
                        qkr.wait(b, sc)
                        pbB[bk_].used(sc=SC.inc(sc.copy(dst, PB[:, pb:pb + 512])))
                    else:
                        PE.w(ve, pq)
                        qkr.wait(b, ve)
                        pbB[bk_].used(ve=VE.inc(ve.tensor_copy(
                            dst, PB[:, pb:pb + 512])))
                return t_qkv

            def mk_v(tc):
                def t_v():
                    bk_ = tc % 2
                    pb = bk_ * 512
                    pbB[bk_].wait(b, pe)
                    vrrR[w2 % 3].wait(b, pe)
                    ZS2[sl2].w(pe)
                    for k in range(NCH):
                        pe.matmul(PB[:, pb:pb + 512],
                                  z[:, sl2, k, tc * 128:(tc + 1) * 128],
                                  wqkv[:, k, 1024:1536], start=(k == 0),
                                  stop=(k == NCH - 1))
                    pv = PE.inc(pe.last)
                    if tc % 2 == 0:
                        PE.w(sc, pv)
                        pbB[bk_].used(sc=SC.inc(sc.copy(
                            vrr[:, w2 % 3, tc, :], PB[:, pb:pb + 512])))
                    else:
                        PE.w(ve, pv)
                        pbB[bk_].used(ve=VE.inc(ve.tensor_copy(
                            vrr[:, w2 % 3, tc, :], PB[:, pb:pb + 512])))
                return t_v

            def t_done():
                st2["evd_sc"] = SC.n
                st2["evd_ve"] = VE.n
                zR[sl2].used(pe=PE.n)

            return ([t_load, t_z] + [mk_qkv(m) for m in range(8)] +
                    [mk_v(tc) for tc in range(4)] + [t_done])

        for t in head_tasks(0):
            t()

        for w in range(NW):
            t0 = w * WINB
            wsl = w % 2
            psl = 1 - wsl
            vsl = w % 3
            pvsl = (w - 1) % 3 if w > 0 else 2
            st = wst[w]
            res_t = xft
            SC.w(ve, st["evd_sc"])
            for raw, rr, rrb in ((qraw, rsq, rsqb), (kraw, rsk, rskb)):
                vsq = VE.inc(ve.tensor_mul(sqs[:, :, :], raw[:, :, :], raw[:, :, :]))
                VE.w(pe, vsq)
                pcB[0].wait(b, pe)
                for c in range(4):
                    pe.matmul(PC[0:8, 0:512], b.selq_b[:, c, :], sqs[:, c, :],
                              start=(c == 0), stop=(c == 3))
                pn = PE.inc(pe.last)
                PE.w(ve, pn)
                veps = VE.inc(ve.tensor_scalar_add(rr[:, :], PC[0:8, 0:512], 1e-30))
                pcB[0].used(ve=veps)
                VE.w(sc, veps)
                sc.activation(rr[:, :], rr[:, :], AF.Ln, bias=0.0, scale=1.0)
                s2 = SC.inc(sc.activation(rr[:, :], rr[:, :], AF.Exp, bias=0.0,
                                          scale=-0.5))
                SC.w(ve, s2)
                VE.inc(ve.tensor_scalar_min(rrb[:, :], rr[:, :], 1e12))
            qk_pe = 0
            for ti, (raw, rot_q, C_t, S_t, scol, srcol, rrb) in enumerate((
                    (qraw, True, cq, sq, qsc, qsrc, rsqb),
                    (kraw, False, ck, sk, ksc, ksrc, rskb))):
                for c in range(4):
                    bk_ = (ti * 4 + c) % 2
                    pbB[bk_].wait(b, pe)
                    pcB[bk_].wait(b, pe)
                    pe.matmul(PB[:, bk_ * 512:bk_ * 512 + 512], b.perm_b[:, :],
                              raw[:, c, :], start=True, stop=True)
                    pe.matmul(PC[:, bk_ * 512:bk_ * 512 + 512], b.selbc_b[:, c, :],
                              rrb[:, :], start=True, stop=True)
                    pr2 = PE.inc(pe.last)
                    qk_pe = pr2
                    PE.w(ve, pr2)
                    ve.scalar_tensor_tensor(u1[:, 0, :], raw[:, c, :], scol[:, :],
                                            C_t[:, t0:t0 + WINB], ALU.mult, ALU.mult)
                    ve.scalar_tensor_tensor(u1[:, 1, :],
                                            PB[:, bk_ * 512:bk_ * 512 + 512],
                                            srcol[:, :],
                                            S_t[:, t0:t0 + WINB], ALU.mult, ALU.mult)
                    ve.tensor_add(u1[:, 0, :], u1[:, 0, :], u1[:, 1, :])
                    dst = qrot[:, c, :] if rot_q else krr[:, wsl, c, :]
                    v4 = VE.inc(ve.tensor_tensor(dst, u1[:, 0, :],
                                                 PC[:, bk_ * 512:bk_ * 512 + 512],
                                                 ALU.mult))
                    pbB[bk_].used(ve=v4)
                    pcB[bk_].used(ve=v4)
            qkr.used(pe=qk_pe, ve=VE.n)
            vrot = VE.n
            s_n_rec = {}
            v_ev_rec = {}
            s_sim_rec = {}
            s_vm_rec = {}
            tasks = head_tasks(w + 1)

            def sim_env(j):
                return (PC, [pcB[0], pcB[1]]) if j % 2 == 0 else \
                       (PA, [paB[0], paB[1]])

            def sim_mms(j, h, qc):
                r0 = (h % 2) * 64
                hc = h // 2
                simT, _ = sim_env(j)
                lo = qc * 128
                hi = qc * 128 + 640
                a = lo
                out = []
                while a < hi:
                    e = min((a // 512 + 1) * 512, hi)
                    if a < lo + 512 < e:
                        e = lo + 512
                    slot, cs = (psl, a) if a < 512 else (wsl, a - 512)
                    out.append((simT[:, a - lo:e - lo],
                                qrot[r0:r0 + 64, hc, lo:lo + 128],
                                krr[r0:r0 + 64, slot, hc, cs:cs + (e - a)]))
                    a = e
                return out

            def emit_sim_pair(js):
                for j in js:
                    for r in sim_env(j)[1]:
                        r.wait(b, pe)
                VE.w(pe, vrot)
                mms = [sim_mms(j, *items[j]) for j in js]
                for parts in zip(*mms):
                    for o_, s_, m_ in parts:
                        pe.matmul(o_, s_, m_, start=True, stop=True)
                cnt = PE.inc(pe.last)
                for j in js:
                    s_sim_rec[j] = cnt

            def emit_max(j, h, qc):
                s2i = j % 4
                simT, simR = sim_env(j)
                PE.w(ve, s_sim_rec[j])
                aer[s2i].wait(b, ve)
                vm = VE.inc(ve.reduce_max(nmx[:, s2i:s2i + 1], simT[:, 0:640],
                                          AX.X, negate=True))
                s_vm_rec[j] = vm
                for r in simR:
                    r.used(ve=vm)

            def emit_post(j, h, qc):
                s2i = j % 4
                simT, simR = sim_env(j)
                VE.w(sc, s_vm_rec[j])
                se = SC.inc(sc.activation(ae[:, s2i, :], simT[:, 0:640], AF.Exp,
                                          bias=nmx[:, s2i:s2i + 1], scale=1.0))
                for r in simR:
                    r.used(sc=se)
                SC.w(ve, se)
                if w == 0 and qc < 4:
                    z0 = (4 - qc) * 128
                    if z0 > 0:
                        ve.memset(am[:, s2i, 0:z0], 0.0)
                    ve.scalar_tensor_tensor(am[:, s2i, z0:640], ae[:, s2i, z0:640],
                                            1.0, mask_sb[:, 0, z0:640],
                                            ALU.mult, ALU.mult,
                                            accum_out=zsum[:, s2i:s2i + 1])
                else:
                    ve.scalar_tensor_tensor(am[:, s2i, :], ae[:, s2i, :], 1.0,
                                            mask_sb[:, 0, :], ALU.mult, ALU.mult,
                                            accum_out=zsum[:, s2i:s2i + 1])
                ve.tensor_scalar_max(zsum[:, s2i:s2i + 1], zsum[:, s2i:s2i + 1],
                                     1e-30)
                v_z = VE.inc(ve.reciprocal(rz[:, s2i:s2i + 1],
                                           zsum[:, s2i:s2i + 1]))
                if j % 2 == 0:
                    s_n_rec[j] = ("ve", VE.inc(ve.tensor_tensor(
                        am2[:, s2i, :], am[:, s2i, :],
                        rz[:, s2i:s2i + 1].broadcast_to([128, 640]), ALU.mult)))
                else:
                    VE.w(sc, v_z)
                    s_n_rec[j] = ("sc", SC.inc(sc.activation(
                        am2[:, s2i, :], am[:, s2i, :], AF.Identity, bias=0.0,
                        scale=rz[:, s2i:s2i + 1])))

            def emit_transp_evac(j, h, qc):
                s2i = j % 4
                pt_ = PT if j % 2 == 0 else PT2
                ptRj = ptR[j % 2]
                ptRj.wait(b, pe)
                atTr.wait(b, pe)
                kind_n, cnt_n = s_n_rec[j]
                if kind_n == "ve":
                    VE.w(pe, cnt_n)
                else:
                    SC.w(pe, cnt_n)
                for jj in range(5):
                    pe.transpose(pt_[:, jj * 128:(jj + 1) * 128],
                                 am2[:, s2i, jj * 128:(jj + 1) * 128],
                                 b.ident_b[:, :])
                ptd = PE.inc(pe.last)
                aer[s2i].used(pe=ptd)
                PE.w(sc, ptd)
                s_ev = SC.inc(sc.copy(
                    atT[:, qc:qc + 5, qc * 128:(qc + 1) * 128],
                    pt_.rearrange("p (j n) -> p j n", n=128)[:, 0:5, :]))
                ptRj.used(sc=s_ev)
                v_ev_rec[h] = ("sc", s_ev)

            def emit_av(h):
                nonlocal v_ofm
                r0 = (h % 2) * 64
                hc = h // 2
                ob = (hc % 2) * 512
                for hh in (h, h - 1 if h % 2 else h):
                    kind, cnt = v_ev_rec[hh]
                    if kind == "ve":
                        VE.w(pe, cnt)
                    else:
                        SC.w(pe, cnt)
                pbB[hc % 2].wait(b, pe)
                first = True
                for kcl in (4, 0, 1, 2, 3, 5, 6, 7):
                    qlo = max(kcl - 4, 0) * 128
                    qhi = min(kcl + 1, 4) * 128
                    if qhi <= qlo:
                        continue
                    slot = pvsl if kcl < 4 else vsl
                    tc = kcl % 4
                    pe.matmul(PB[r0:r0 + 64, ob + qlo:ob + qhi],
                              vrr[:, slot, tc, h * 64:(h + 1) * 64],
                              atT[:, kcl, qlo:qhi],
                              start=first, stop=(kcl == 7), tile_position=(0, r0))
                    first = False
                pav = PE.inc(pe.last)
                atTr.used(pe=pav)
                vrrR[vsl].used(pe=pav)
                vrrR[pvsl].used(pe=pav)
                if h % 2 == 1:
                    PE.w(ve, pav)
                    v_ofm = VE.inc(ve.tensor_copy(ofm[:, hc, :], PB[:, ob:ob + 512]))
                    pbB[hc % 2].used(ve=v_ofm)

            items = []
            for hp_ in range(HB // 2):
                for qc_ in range(4):
                    items.append((2 * hp_, qc_))
                    items.append((2 * hp_ + 1, qc_))
            LAG = 2
            v_ofm = 0
            ti_ = 0
            emit_sim_pair([0, 1])
            emit_max(0, *items[0])
            emit_max(1, *items[1])
            for j in range(len(items) + LAG):
                if j < len(items):
                    emit_post(j, *items[j])
                if j % 2 == 1 and j + 1 < len(items):
                    js = [j + 1] + ([j + 2] if j + 2 < len(items) else [])
                    emit_sim_pair(js)
                    for j2 in js:
                        emit_max(j2, *items[j2])
                while ti_ < len(tasks) and ti_ <= (j * len(tasks)) // max(
                        len(items) - 4, 1):
                    tasks[ti_]()
                    ti_ += 1
                if j >= LAG:
                    h2, qc2 = items[j - LAG]
                    emit_transp_evac(j - LAG, h2, qc2)
                    if qc2 == 3:
                        emit_av(h2)
            while ti_ < len(tasks):
                tasks[ti_]()
                ti_ += 1
            v_xo = 0
            for m in range(NCH):
                bk_ = m % 2
                pb = bk_ * 512
                paB[bk_].wait(b, pe)
                VE.w(pe, v_ofm)
                for k in range(4):
                    pe.matmul(PA[:, pb:pb + 512], wo[:, k, m * 128:(m + 1) * 128],
                              ofm[:, k, :], start=(k == 0), stop=(k == 3))
                pwo = PE.inc(pe.last)
                PE.w(ve, pwo)
                XS2[wsl].w(ve)
                v_xo = VE.inc(ve.tensor_tensor(xout[:, m, :], PA[:, pb:pb + 512],
                                               res_t[:, wsl, m, 1:WINB + 1],
                                               ALU.add))
                paB[bk_].used(ve=v_xo)
            xftR[wsl].used(ve=v_xo)
            VE.w(sy, v_xo)
            ZS2[psl].w(sy)
            LS.dma(sy.dma_start(out=b.xffn3[:, :, t0:t0 + WINB], in_=xout[:, :, :]))
            xob = z
            zqt = atT
            VE.w(sc, v_xo)
            zR[wsl].wait(b, sc)
            vxb = SC.inc(sc.copy(xob[:, wsl, :, :], xout[:, :, :]))
            SC.w(ve, vxb)
            vq2 = VE.inc(ve.tensor_mul(zqt[:, :, :], xob[:, wsl, :, :],
                                       xob[:, wsl, :, :]))
            VE.w(pe, vq2)
            SC.w(pe, vxb)
            for r in (paB[0], paB[1], pcB[0], pcB[1]):
                r.wait(b, pe)
            for c in range(4):
                pe.matmul(PA[0:1, 0:512], b.onesc_b[:, :], xob[:, wsl, c, :],
                          start=(c == 0), stop=(c == 3))
            for c in range(4, 8):
                pe.matmul(PA[0:1, 512:1024], b.onesc_b[:, :], xob[:, wsl, c, :],
                          start=(c == 4), stop=(c == 7))
            for c in range(4):
                pe.matmul(PC[0:1, 0:512], b.onesc_b[:, :], zqt[:, c, :],
                          start=(c == 0), stop=(c == 3))
            for c in range(4, 8):
                pe.matmul(PC[0:1, 512:1024], b.onesc_b[:, :], zqt[:, c, :],
                          start=(c == 4), stop=(c == 7))
            prw = PE.inc(pe.last)
            atTr.used(pe=prw)
            zR[wsl].used(pe=prw)
            PE.w(sc, prw)
            sc.copy(row4[:, 0:2, :].rearrange("p a n -> p (a n)"),
                    PA[0:1, 0:1024])
            s_r4 = SC.inc(sc.copy(
                row4[:, 2:4, :].rearrange("p a n -> p (a n)"), PC[0:1, 0:1024]))
            for r in (paB[0], paB[1], pcB[0], pcB[1]):
                r.used(sc=s_r4)
            SC.w(sy, s_r4)
            LS.dma(sy.dma_start(out=b.rows4_d[l][:, :, t0:t0 + WINB],
                                in_=row4[:, :, :]))
        b.phase_end()


def phase_bffn(b, l):
    nc, g, sy, pe, ve, sc = b.nc, b.g, b.sy, b.pe, b.ve, b.sc
    LD, LS, PE, VE, SC, GP = b.LD, b.LS, b.PE, b.VE, b.SC, b.GP
    NT, T = b.NT, b.T
    with ExitStack() as stp:
        lgc = stp.enter_context(nc.sbuf_tensor(f"f{l}_lgc", [128, 8], F32))
        lbc = stp.enter_context(nc.sbuf_tensor(f"f{l}_lbc", [128, 8], F32))
        b1c = stp.enter_context(nc.sbuf_tensor(f"f{l}_b1c", [128, 44], F32))
        b2c = stp.enter_context(nc.sbuf_tensor(f"f{l}_b2c", [128, 8], F32))
        LS.dma(sy.dma_start(out=lgc[:, :], in_=b.lg_h[l][:, :]))
        LS.dma(sy.dma_start(out=lbc[:, :], in_=b.lb_h[l][:, :]))
        LS.dma(sy.dma_start(out=b1c[:, :], in_=b.b1_h[l][:, :]))
        LS.dma(sy.dma_start(out=b2c[:, :], in_=b.b2_h[l][:, :]))

        w1 = stp.enter_context(nc.sbuf_tensor(f"f{l}_w1", [128, NCH, W1N], BF16))
        LD.dma(g.dma_start(out=w1[:, :, :],
                           in_=b.w1_h[l].rearrange("(c p) m -> p c m", p=128)))
        w2 = stp.enter_context(nc.sbuf_tensor(f"f{l}_w2", [128, NK2, D], BF16))
        LD.dma(g.dma_start(out=w2[:, :, :],
                           in_=b.w2_h[l].rearrange("(c p) m -> p c m", p=128)))

        with ExitStack() as s1:
            t1 = s1.enter_context(nc.sbuf_tensor(f"f{l}_t1", [1, T], F32))
            t2 = s1.enter_context(nc.sbuf_tensor(f"f{l}_t2", [1, T], F32))
            t3 = s1.enter_context(nc.sbuf_tensor(f"f{l}_t3", [1, T], F32))
            r4 = b.rows4_d[l]
            LS.dma(sy.dma_start(out=t1[:, :], in_=r4[:, 0, :]))
            LS.dma(sy.dma_start(out=t2[:, :], in_=r4[:, 1, :]))
            LS.dma(sy.dma_start(out=t3[:, :], in_=r4[:, 2, :]))
            LS.w(ve)
            VE.inc(ve.tensor_add(t1[:, 1:T], t1[:, 1:T], t2[:, 0:T - 1]))
            VE.w(sy, VE.n)
            LS.dma(sy.dma_start(out=t2[:, :], in_=r4[:, 3, :]))
            LS.w(ve)
            VE.inc(ve.tensor_add(t3[:, 1:T], t3[:, 1:T], t2[:, 0:T - 1]))
            ve.tensor_copy(t2[:, :], t3[:, :])
            ln_soup(b, s1, t1, t2, b.rg_d[l], b.nm_d[l], f"f{l}", neg=True)

        with ExitStack() as s2p:
            xft = s2p.enter_context(nc.sbuf_tensor(f"f{l}_xft2",
                                                   [128, 2, NCH, TT + 1], F32))
            xln2 = s2p.enter_context(nc.sbuf_tensor(f"f{l}_xl2", [128, NCH, TT], BF16))
            uscr = s2p.enter_context(nc.sbuf_tensor(f"f{l}_us", [128, 1, TT], F32))
            glu = s2p.enter_context(nc.sbuf_tensor(f"f{l}_glu", [128, NK2, TT], BF16))
            gg = s2p.enter_context(nc.sbuf_tensor(f"f{l}_gg", [128, 1, TT], BF16))
            rowb = s2p.enter_context(nc.sbuf_tensor(f"f{l}_rowb", [1, 2, 2, TT], F32))
            pbc = s2p.enter_context(nc.psum_tensor(f"f{l}_pbc", [128, 1024], F32))
            pag = s2p.enter_context(nc.psum_tensor(f"f{l}_pag", [128, 2048], F32))
            pw2 = s2p.enter_context(nc.psum_tensor(f"f{l}_pw2", [128, 1024], F32))
            LS.w(g)
            b.SC.w(g, b.SC.n)
            b.VE.w(g, b.VE.n)
            GP.inc(g.memset(xft[:, :, :, 0:1], 0.0))
            XFS = b.mksem(f"f{l}s2x")
            RS2 = b.mksem(f"f{l}s2r")
            STS = b.mksem(f"f{l}s2s")
            xftr = [Res(), Res()]
            rowr = [Res(), Res()]
            pbcB = [Res(), Res()]
            agB = [Res(), Res()]
            w2B = [Res(), Res()]
            z2r = Res()
            glur = Res()
            ggr = [Res(), Res()]
            usr = [Res(), Res()]
            LD.w(pe)
            GP.w(ve, GP.n)
            GP.w(sy, GP.n)
            XF2 = [XFS, b.mksem(f"f{l}s2y")]
            RSL2 = [RS2, b.mksem(f"f{l}s2r2")]
            vxl_rec = [0] * (NT + 1)

            def emit_ln_apply(i2):
                sl2 = i2 % 2
                pbcB[0].wait(b, pe)
                pbcB[1].wait(b, pe)
                RSL2[sl2].w(pe)
                pe.matmul(pbc[:, 0:TT], b.onesr_f[:, :], rowb[:, sl2, 0, :],
                          start=True, stop=True)
                pb2 = PE.inc(pe.matmul(pbc[:, TT:2 * TT], b.onesr_f[:, :],
                                       rowb[:, sl2, 1, :], start=True, stop=True))
                rowr[sl2].used(pe=pb2)
                for c in range(NCH):
                    us = 0
                    off = 1 if c < 4 else 0
                    PE.w(ve, pb2)
                    XF2[sl2].w(ve)
                    usr[us].wait(b, ve)
                    z2r.wait(b, ve)
                    ve.tensor_tensor(uscr[:, us, :], xft[:, sl2, c, off:off + TT],
                                     pbc[:, 0:TT], ALU.mult)
                    vu = VE.inc(ve.tensor_tensor(uscr[:, us, :], uscr[:, us, :],
                                                 pbc[:, TT:2 * TT], ALU.add))
                    xftr[sl2].used(ve=vu)
                    pbcB[0].used(ve=vu)
                    pbcB[1].used(ve=vu)
                    VE.w(sc, vu)
                    LS.w(sc)
                    s_x = SC.inc(sc.activation(xln2[:, c, :], uscr[:, us, :],
                                               AF.Identity, bias=lbc[:, c:c + 1],
                                               scale=lgc[:, c:c + 1]))
                    usr[us].used(sc=s_x)
                    z2r.used(sc=s_x)
                vxl_rec[i2] = SC.n

            for i in range(NT):
                a = i * TT
                sl = i % 2
                if i == 0:
                    for pf in (0, 1):
                        if pf >= NT:
                            break
                        ap_ = pf * TT
                        if pf == 0:
                            XF2[0].dma(sy.dma_start(out=xft[:, 0, :, 1:TT + 1],
                                                    in_=b.xffn3[:, :, 0:TT]))
                        else:
                            XF2[1].dma(sy.dma_start(
                                out=xft[:, 1, :, :],
                                in_=b.xffn3[:, :, ap_ - 1:ap_ + TT]))
                        RSL2[pf].dma(sy.dma_start(out=rowb[:, pf, 0, :],
                                                  in_=b.rg_d[l][:, ap_:ap_ + TT]))
                        RSL2[pf].dma(sy.dma_start(out=rowb[:, pf, 1, :],
                                                  in_=b.nm_d[l][:, ap_:ap_ + TT]))
                elif i + 1 < NT:
                    ap_ = (i + 1) * TT
                    s3 = (i + 1) % 2
                    xftr[s3].wait(b, sy)
                    STS.w(sy)
                    XF2[s3].dma(sy.dma_start(out=xft[:, s3, :, :],
                                             in_=b.xffn3[:, :, ap_ - 1:ap_ + TT]))
                    rowr[s3].wait(b, sy)
                    RSL2[s3].dma(sy.dma_start(out=rowb[:, s3, 0, :],
                                              in_=b.rg_d[l][:, ap_:ap_ + TT]))
                    RSL2[s3].dma(sy.dma_start(out=rowb[:, s3, 1, :],
                                              in_=b.nm_d[l][:, ap_:ap_ + TT]))
                if i == 0:
                    emit_ln_apply(0)
                SC.w(pe, vxl_rec[i])
                for j in range(NK2):
                    bk_ = j % 2
                    pb = bk_ * 512
                    agB[bk_].wait(b, pe)
                    for k in range(NCH):
                        pe.matmul(pag[:, pb:pb + 512],
                                  w1[:, k, j * 128:(j + 1) * 128],
                                  xln2[:, k, :], start=(k == 0), stop=(k == NCH - 1))
                    for k in range(NCH):
                        pe.matmul(pag[:, 1024 + pb:1024 + pb + 512],
                                  w1[:, k, (NK2 + j) * 128:(NK2 + j + 1) * 128],
                                  xln2[:, k, :], start=(k == 0), stop=(k == NCH - 1))
                    pj = PE.inc(pe.last)
                    PE.w(sc, pj)
                    ggr[0].wait(b, sc)
                    sg = SC.inc(sc.activation(gg[:, 0, :],
                                              pag[:, 1024 + pb:1024 + pb + 512],
                                              AF.Gelu,
                                              bias=b1c[:, NK2 + j:NK2 + j + 1],
                                              scale=1.0))
                    SC.w(ve, sg)
                    vg = VE.inc(ve.scalar_tensor_tensor(
                        glu[:, j, :], pag[:, pb:pb + 512], b1c[:, j:j + 1],
                        gg[:, 0, :], ALU.add, ALU.mult))
                    agB[bk_].used(sc=sg, ve=vg)
                    ggr[0].used(ve=vg)
                z2r.used(pe=PE.n)
                vglu = VE.n
                if i + 1 < NT:
                    emit_ln_apply(i + 1)
                VE.w(pe, vglu)
                v_xo = 0
                for m in range(NCH):
                    bk_ = m % 2
                    pb = bk_ * 512
                    w2B[bk_].wait(b, pe)
                    for k in range(NK2):
                        pe.matmul(pw2[:, pb:pb + 512], w2[:, k, m * 128:(m + 1) * 128],
                                  glu[:, k, :], start=(k == 0), stop=(k == NK2 - 1))
                    pm = PE.inc(pe.last)
                    PE.w(ve, pm)
                    v_xo = VE.inc(ve.scalar_tensor_tensor(
                        xft[:, sl, m, 1:TT + 1], pw2[:, pb:pb + 512],
                        b2c[:, m:m + 1],
                        xft[:, sl, m, 1:TT + 1], ALU.add, ALU.add))
                    w2B[bk_].used(ve=v_xo)
                glur.used(pe=PE.n)
                xftr[sl].used(ve=v_xo)
                if l == 0:
                    VE.w(sy, v_xo)
                    STS.dma(sy.dma_start(out=b.xffn3[:, :, a:a + TT],
                                         in_=xft[:, sl, :, 1:TT + 1]))
                else:
                    VE.w(g, v_xo)
                    STS.dma(g.dma_start(out=b.out3[:, :, a:a + TT],
                                        in_=xft[:, sl, :, 1:TT + 1],
                                        accum_op=ALU.add))
            b.phase_end()



def kernel(**inputs):
    T = 4096
    consts = host_consts(T)
    w = host_weights(inputs)
    nc = build_nc(T)
    x = np.asarray(inputs["L0_attn_x"], np.float32)
    in_maps = []
    for core in range(8):
        m = {"x": np.ascontiguousarray(x[core].T)}
        for k, v in w.items():
            m[k] = np.ascontiguousarray(np.asarray(v, np.float32))
        for k, v in consts.items():
            m[k] = np.ascontiguousarray(np.asarray(v, np.float32))
        in_maps.append(m)
    res = run_bass_kernel_spmd(nc, in_maps, core_ids=list(range(8)))
    out = np.stack([np.asarray(res.results[i]["out"]).T for i in range(8)])
    return np.ascontiguousarray(out.astype(np.float32))
